# version 18
# baseline (speedup 1.0000x reference)
import contextlib
import numpy as np
import ml_dtypes
import concourse.bass as bass
import concourse.mybir as mybir
from concourse.bass_utils import run_bass_kernel_spmd

F32 = mybir.dt.float32
BF16 = mybir.dt.bfloat16
I32 = mybir.dt.int32
ALU = mybir.AluOpType
AF = mybir.ActivationFunctionType
AX = mybir.AxisListType

TOK = 2048
FULL = 4096
NDT = 8
S = 16
TS = 128
FT = S * TS
NJ = FULL // TS
EPS = 1e-5
GROUPS = [[0, 1], [2, 3], [4, 5], [6, 7]]
ENGS = ("sync", "vector", "scalar", "tensor", "gpsimd")


class Plan:
    def __init__(self):
        self.ops = {e: [] for e in ENGS}
        self.cnt = {n: 0 for n in ("dy", "dy0", "dy1", "dg", "dg2", "sel1", "sel2", "rep", "bar", "v", "s", "t", "g", "c")}
        self.sems = {}

    def op(self, eng, fn, waits=(), inc=None, amt=1):
        tok = None
        if inc is not None:
            self.cnt[inc] += amt
            tok = (inc, self.cnt[inc])
        self.ops[eng].append((fn, tuple(w for w in waits if w is not None), inc, amt))
        return tok

    def emit(self, eng_name, eng):
        last = {}
        for fn, waits, inc, amt in self.ops[eng_name]:
            for sn, vv in waits:
                if last.get(sn, 0) >= vv:
                    continue
                eng.wait_ge(self.sems[sn], vv)
                last[sn] = vv
            if fn is None:
                continue
            ins = fn(eng)
            if inc is not None:
                ins.then_inc(self.sems[inc], amt)


class Kern:
    def __init__(self):
        nc = bass.Bass("TRN2", target_bir_lowering=False, debug=False)
        nc.detect_race_conditions = False
        self.nc = nc
        self.st = contextlib.ExitStack()
        self.p = Plan()
        self.regs = {}

    def din(self, n, sh, dt=F32):
        return self.nc.dram_tensor(n, sh, dt, kind="ExternalInput").ap()

    def dout(self, n, sh, dt=F32):
        return self.nc.dram_tensor(n, sh, dt, kind="ExternalOutput").ap()

    def dtmp(self, n, sh, dt=F32):
        return self.nc.dram_tensor(n, sh, dt, kind="Internal").ap()

    def sbuf(self, n, sh, dt=F32, scope=None):
        return (scope or self.st).enter_context(self.nc.sbuf_tensor(n, sh, dt))

    def psum(self, n, sh, dt=F32):
        return self.st.enter_context(self.nc.psum_tensor(n, sh, dt))

    def semaphore(self, n):
        h = self.st.enter_context(self.nc.semaphore(n))
        self.p.sems[n] = h
        return h

    def ld(self, out, in_, waits=(), sem="dy"):
        return self.p.op("sync", lambda e, o=out, i=in_: e.dma_start(o, i), waits, sem, 16)

    def stg(self, out, in_, waits=(), sem="dg"):
        return self.p.op("gpsimd", lambda e, o=out, i=in_: e.dma_start(o, i), waits, sem, 16)

    def cstg(self, out, in_, flag, waits=()):
        return self.p.op("gpsimd",
                         lambda e, o=out, i=in_, f=flag: e.dma_start(o, i, cond=self.regs[f]),
                         waits, "dg", 16)

    def v(self, fn, waits=()):
        return self.p.op("vector", fn, waits, "v", 1)

    def s(self, fn, waits=()):
        return self.p.op("scalar", fn, waits, "s", 1)

    def t(self, fn, waits=()):
        return self.p.op("tensor", fn, waits, "t", 1)

    def g(self, fn, waits=()):
        return self.p.op("gpsimd", fn, waits, "g", 1)

    def cc(self, kind, op, ins, outs, waits=()):
        return self.p.op(
            "gpsimd",
            lambda e, k=kind, o=op, i=ins, u=outs: e.collective_compute(
                k, o, replica_groups=GROUPS, ins=i, outs=u),
            waits, "c", 1)


def sm(ap, s_=S):
    return ap.rearrange("p (s t) -> p s t", s=s_)


def build_program():
    K = Kern()
    for sn in ("dy", "dy0", "dy1", "dg", "dg2", "sel1", "sel2", "rep", "bar", "v", "s", "t", "g", "c"):
        K.semaphore(sn)

    d_xT = K.din("xT", [1024, TOK])
    d_flags = K.din("flags", [2], I32)
    d_hmask = K.din("hmask", [128, 2])
    d_WdbcT = K.din("WdbcT", [1024, 128])
    d_WdtT = K.din("WdtT", [64, 1024])
    d_bdt_a = K.din("bdt_a", [128, 8])
    d_Alog_a = K.din("Alog_a", [512, S])
    d_Da = K.din("Da", [128, 8])
    d_WinT = K.din("WinT", [1024, 4096])
    d_WoutT = K.din("WoutT", [2048, 1024])
    d_cw = {d: K.din(f"cw{d}", [1024, 4]) for d in "fb"}
    d_cb = {d: K.din(f"cb{d}", [128, 8]) for d in "fb"}
    d_xpT = {d: K.din(f"xp{d}T", [1024, 128]) for d in "fb"}
    d_dtwT = {d: K.din(f"dtw{d}T", [64, 1024]) for d in "fb"}
    d_dtb = {d: K.din(f"dtb{d}", [128, 8]) for d in "fb"}
    d_Alog = {d: K.din(f"Alog{d}", [1024, S]) for d in "fb"}
    d_Dp = {d: K.din(f"Dp{d}", [128, 8]) for d in "fb"}
    d_WexpT = K.din("WexpT", [4096, 1024])
    d_Wcon = K.din("Wcon", [4096, 1024])
    d_bexp = K.din("bexp", [128, 32])
    d_upbias = K.din("upbias", [128, 32])
    d_bcon = K.din("bcon", [128, 8])
    d_downbias = K.din("downbias", [128, 8])
    d_eye = K.din("eye", [128, 128])
    d_bfA = {c: K.din(f"bfA{c}", [7, 4, 2048]) for c in "ud"}
    d_bfB = {c: K.din(f"bfB{c}", [128, 5, 4, 16]) for c in "ud"}
    o_x = K.dout("out_xT", [TOK, 1024])

    t_scr_ad = K.dtmp("scr_ad", [512, TOK], BF16)
    t_scr_aw = K.dtmp("scr_aw", [512, TOK], BF16)
    t_cc1_in = K.dtmp("cc1_in", [1056, TOK], BF16)
    t_cc1_out = K.dtmp("cc1_out", [2, 1056, TOK], BF16)
    t_af = K.dtmp("scr_af", [512, FULL], BF16)
    t_wf = K.dtmp("scr_wf", [512, FULL], BF16)
    t_bcf = K.dtmp("scr_bcf", [32, FULL], BF16)
    t_scr_ay = K.dtmp("scr_ay", [512, FULL], BF16)
    t_cc2_in = K.dtmp("cc2_in", [512, TOK], BF16)
    t_cc2_out = K.dtmp("cc2_out", [2, 512, TOK], BF16)
    t_scr_x = K.dtmp("scr_x", [1024, TOK], F32)
    t_scr_M = K.dtmp("scr_M", [1024, TOK], BF16)
    t_wpe = K.dtmp("scr_wpe", [1024, 4096], BF16)
    t_wpc = K.dtmp("scr_wpc", [4096, 1024], BF16)
    t_scr_bxin = K.dtmp("scr_bxin", [1024, TOK], BF16)
    t_scr_bz = K.dtmp("scr_bz", [1024, TOK], BF16)
    t_dt = K.dtmp("scr_dt", [1024, FULL], BF16)
    t_dlr = K.dtmp("scr_dlr", [2, 64, FULL], BF16)
    t_bcb = K.dtmp("scr_bcb", [2, 32, FULL], BF16)
    t_yf = K.dtmp("scr_yf", [1024, FULL], BF16)
    t_ccb1_in = K.dtmp("ccb1_in", [2048, TOK], BF16)
    t_ccb1_out = K.dtmp("ccb1_out", [2, 2048, TOK], BF16)
    t_sz = K.dtmp("scr_sz", [1024, FULL], BF16)
    t_xc = {d: K.dtmp(f"scr_xc{d}", [1024, FULL], BF16) for d in "fb"}
    t_ccb2_in = K.dtmp("ccb2_in", [192, FULL], F32)
    t_ccb2_out = K.dtmp("ccb2_out", [192, FULL], F32)
    t_scr_bys = K.dtmp("scr_bys", [1024, FULL], BF16)
    t_ccb3_in = K.dtmp("ccb3_in", [1024, TOK], BF16)
    t_ccb3_out = K.dtmp("ccb3_out", [2, 1024, TOK], BF16)

    sx = [K.sbuf(f"x{i}", [128, TOK]) for i in range(NDT)]
    s_ones = K.sbuf("ones", [128, 1])
    s_eye = K.sbuf("eyeS", [128, 128])
    s_eyeb = K.sbuf("eyeSb", [128, 128], BF16)
    s_flags = K.sbuf("flagsS", [1, 2], I32)
    s_WdbcT = K.sbuf("WdbcT_s", [128, NDT, 128], BF16)
    s_WdtT = K.sbuf("WdtT_s", [64, 1024], BF16)
    s_bdt_a = K.sbuf("bdt_a_s", [128, NDT])
    s_Aa = K.sbuf("Aa_s", [128, 4, S], BF16)
    s_Da = K.sbuf("Da_s", [128, NDT])
    s_cw = {d: K.sbuf(f"cw{d}_s", [128, NDT, 4]) for d in "fb"}
    s_cb = {d: K.sbuf(f"cb{d}_s", [128, NDT]) for d in "fb"}
    s_xpT = {d: K.sbuf(f"xp{d}T_s", [128, NDT, 128], BF16) for d in "fb"}
    s_dtwT = {d: K.sbuf(f"dtw{d}T_s", [64, 1024], BF16) for d in "fb"}
    s_dtb = {d: K.sbuf(f"dtb{d}_s", [128, NDT]) for d in "fb"}
    s_Ab = {d: K.sbuf(f"Ab{d}_s", [128, NDT, S], BF16) for d in "fb"}
    s_Db = {d: K.sbuf(f"Db{d}_s", [128, NDT]) for d in "fb"}
    s_bpu = K.sbuf("bpu_s", [128, 32])
    s_bpc = K.sbuf("bpc_s", [128, NDT])
    s_bfB = {c: K.sbuf(f"bfB{c}_s", [128, 5, 4, 16]) for c in "ud"}

    ps = [K.psum(f"ps{i}", [128, 512]) for i in range(8)]

    def _selview(base, like_ap):
        sh = list(like_ap.shape)
        v = base
        if len(sh) == 2:
            return v[0:sh[0], 0:sh[1]]
        if len(sh) == 3:
            return v[:].rearrange("p (g f) -> p g f", g=sh[1])[0:sh[0], :, 0:sh[2]]
        raise ValueError(sh)

    def _selfree(ap):
        sh = list(ap.shape)
        n = 1
        for x in sh[1:]:
            n *= x
        return sh[0], n

    def stage_sel(c0_ap, c1_ap, pre):
        rows, free = _selfree(c0_ap)
        t1 = K.ld(_selview(s_bnc, c0_ap), c0_ap, pre, sem="sel1")
        t2 = K.stg(_selview(s_bnc2, c1_ap), c1_ap, pre, sem="sel2")
        return [t1, t2, (rows, free)]

    def sel_read(swap, st_):
        t1, t2, (rows, free) = st_
        a_, b_ = (0, 1) if not swap else (1, 0)
        t3 = K.v(lambda e, r=rows, f=free, a=a_: e.tensor_scalar(
            s_bnc[0:r, 0:f], s_bnc[0:r, 0:f], s_hm[0:r, a:a + 1], None, ALU.mult),
            [t1, t2, tok_regs])
        return K.v(lambda e, r=rows, f=free, b=b_: e.scalar_tensor_tensor(
            s_bnc[0:r, 0:f], s_bnc2[0:r, 0:f], s_hm[0:r, b:b + 1], s_bnc[0:r, 0:f],
            ALU.mult, ALU.add), [t3])

    def select1(dst_fn, c0_ap, c1_ap, pre):
        st_ = stage_sel(c0_ap, c1_ap, pre)
        t4 = sel_read(False, st_)
        o, i = dst_fn(s_bnc)
        return K.stg(o, i, [t4], sem="dg2")

    tok_x = [K.ld(sx[i][:], d_xT.rearrange("(a p) t -> a p t", p=128)[i]) for i in range(NDT)]
    tok_fl = K.ld(s_flags[:], d_flags.unsqueeze(0))
    tok_eye = K.ld(s_eye[:], d_eye)
    tok_eyeb = K.s(lambda e: e.copy(s_eyeb[:], s_eye[:]), [tok_eye])
    tok_ones = K.v(lambda e: e.memset(s_ones[:], 1.0))
    s_eps = K.sbuf("epsS", [1, 1])
    s_bnc = K.sbuf("bounceS", [128, 8192], BF16)
    s_bnc2 = K.sbuf("bounceS2", [128, 8192], BF16)
    s_hm = K.sbuf("hmaskS", [128, 2])
    tok_eps = K.v(lambda e: e.memset(s_eps[:], EPS))

    t_dbg = K.dtmp("t_dbg", [1, 2], I32)
    tok_regs = K.ld(s_hm[:], d_hmask)

    def drain_dg(pre=()):
        cnt = K.p.cnt["dg"]
        K.p.op("gpsimd", None, tuple(pre) + (("dg", cnt),))
        return K.p.op("gpsimd", lambda e: e.dma_start(t_dbg, s_flags[:]), (), "bar", 16)

    boot = contextlib.ExitStack()
    s_bs = K.sbuf("bootstage", [128, 4096], scope=boot)
    prev_cast = [None, None]

    def load_cast(dst_ap, src_ap, rows, cols):
        tk = K.ld(s_bs[0:rows, 0:cols], src_ap, [prev_cast[0]])
        prev_cast[0] = K.s(lambda e, d=dst_ap, r=rows, c=cols: e.copy(d, s_bs[0:r, 0:c]), [tk])
        return prev_cast[0]

    load_cast(s_WdbcT[:], d_WdbcT.rearrange("(a p) c -> p a c", p=128), 128, NDT * 128)
    load_cast(s_WdtT[:], d_WdtT, 64, 1024)
    for d in "fb":
        load_cast(s_xpT[d][:], d_xpT[d].rearrange("(a p) c -> p a c", p=128), 128, NDT * 128)
        load_cast(s_dtwT[d][:], d_dtwT[d], 64, 1024)
    tk = K.ld(s_bs[0:128, 0:4 * S], d_Alog_a.rearrange("(a p) s -> p a s", p=128),
              [prev_cast[0], prev_cast[1]])
    tk = K.s(lambda e: e.activation(s_bs[0:128, 128:128 + 4 * S], s_bs[0:128, 0:4 * S], AF.Exp), [tk])
    prev_cast[0] = K.v(lambda e: e.tensor_scalar_mul(
        s_Aa[:].rearrange("p a s -> p (a s)"), s_bs[0:128, 128:128 + 4 * S], -1.0), [tk])
    for d in "fb":
        tk = K.ld(s_bs[0:128, 0:NDT * S], d_Alog[d].rearrange("(a p) s -> p a s", p=128),
                  [prev_cast[0], prev_cast[1]])
        tk = K.s(lambda e: e.activation(s_bs[0:128, 256:256 + NDT * S], s_bs[0:128, 0:NDT * S], AF.Exp), [tk])
        prev_cast[0] = K.v(lambda e, d_=d: e.tensor_scalar_mul(
            s_Ab[d_][:].rearrange("p a s -> p (a s)"), s_bs[0:128, 256:256 + NDT * S], -1.0), [tk])

    K.ld(s_bdt_a[:], d_bdt_a)
    K.ld(s_Da[:], d_Da)
    for d in "fb":
        K.ld(s_cw[d][:], d_cw[d].rearrange("(a p) k -> p a k", p=128))
        K.ld(s_cb[d][:], d_cb[d])
        K.ld(s_dtb[d][:], d_dtb[d])
        K.ld(s_Db[d][:], d_Dp[d])
    tok_bc1 = K.ld(s_bpc[:], d_bcon)
    tok_db = K.ld(s_bs[0:128, 512:512 + NDT], d_downbias,
                  [tok_bc1, prev_cast[0], prev_cast[1]])
    tok_bpc = K.v(lambda e: e.tensor_add(s_bpc[:], s_bpc[:], s_bs[0:128, 512:512 + NDT]),
                  [tok_db, tok_bc1])
    for c in "ud":
        K.ld(s_bfB[c][:], d_bfB[c])

    boot.close()

    comp = contextlib.ExitStack()
    sZbU = K.sbuf("ZcbU", [128, 4096], BF16, scope=comp)
    sZbD = K.sbuf("ZcbD", [128, 4096], BF16, scope=comp)
    sbvec = K.sbuf("bvecc", [128, 32], scope=comp)
    sbvec2 = K.sbuf("bvecc2", [128, 32], scope=comp)
    zbsc = contextlib.ExitStack()
    sZ = K.sbuf("Zc", [128, 4096], scope=zbsc)
    sZ2 = K.sbuf("Zc2", [128, 4096], scope=zbsc)
    sT1 = K.sbuf("Tc1", [128, 2048], scope=zbsc)
    sWrep = K.sbuf("Wrep", [128, 2, 2048], scope=zbsc)
    sAvec = K.sbuf("Avec", [28, 2048], scope=zbsc)

    def build_Z(chain, pre):
        tka = K.ld(sAvec[:], d_bfA[chain].rearrange("a b c -> (a b) c"), pre)
        tz = K.stg(sZ[:].rearrange("p (q i) -> p q i", q=32),
                   s_eye[:].unsqueeze(1).broadcast_to([128, 32, 128]), pre + [tok_eye], sem="rep")
        cur, nxt = sZ, sZ2
        tok = None
        for st_ in range(7):
            sig = 1 << st_
            nh = 64 // sig
            def pos(buf, two, nh=nh, sig=sig):
                return buf[:].rearrange("p (q h two l) -> p q h two l",
                                        q=32, h=nh, two=2)[:, :, :, two, :]

            def wv(quad, nh=nh, sig=sig):
                return sWrep[:, quad, :].rearrange("p (q h l) -> p q h l", q=32, h=nh)

            c_, n_ = cur, nxt
            trep = None
            for qd in range(2):
                trep = K.stg(
                    sWrep[:, qd, :],
                    sAvec[st_ * 4 + qd:st_ * 4 + qd + 1, :].unsqueeze(1)
                    .broadcast_to([1, 128, 2048]),
                    ([tka, tz] if tok is None else [tok]) if qd == 0 else [trep], sem="rep")
            K.v(lambda e, c=c_, n=n_, pos=pos, wv=wv: e.tensor_tensor(
                pos(n, 0), pos(c, 0), wv(0), ALU.mult),
                ([trep, tz, tka] if tok is None else [trep, tok]))
            K.v(lambda e, c=c_, nh=nh, pos=pos, wv=wv: e.tensor_tensor(
                sT1[:].rearrange("p (q h l) -> p q h l", q=32, h=nh), pos(c, 1), wv(1), ALU.mult))
            tok0 = K.v(lambda e, n=n_, nh=nh, pos=pos: e.tensor_tensor(
                pos(n, 0), pos(n, 0),
                sT1[:].rearrange("p (q h l) -> p q h l", q=32, h=nh), ALU.add))
            trep = None
            for qd in range(2):
                trep = K.stg(
                    sWrep[:, qd, :],
                    sAvec[st_ * 4 + 2 + qd:st_ * 4 + 3 + qd, :].unsqueeze(1)
                    .broadcast_to([1, 128, 2048]),
                    [tok0] if qd == 0 else [trep], sem="rep")
            K.v(lambda e, c=c_, n=n_, pos=pos, wv=wv: e.tensor_tensor(
                pos(n, 1), pos(c, 0), wv(0), ALU.mult), [trep])
            K.v(lambda e, c=c_, nh=nh, pos=pos, wv=wv: e.tensor_tensor(
                sT1[:].rearrange("p (q h l) -> p q h l", q=32, h=nh), pos(c, 1), wv(1), ALU.mult))
            tok = K.v(lambda e, n=n_, nh=nh, pos=pos: e.tensor_tensor(
                pos(n, 1), pos(n, 1),
                sT1[:].rearrange("p (q h l) -> p q h l", q=32, h=nh), ALU.add))
            cur, nxt = nxt, cur
        return cur, tok

    def bstep_pairs(chain, cur, get_a, width, pre):
        tok = None
        for st_ in range(5):
            sigq = 1 << st_
            for hi in range(16 // sigq):
                for lo in range(sigq):
                    pr = hi * sigq + lo
                    qa = hi * 2 * sigq + lo
                    qb = qa + sigq

                    def w(quad, p_=pr, s__=st_, c_=chain):
                        return s_bfB[c_][:, s__, quad, p_:p_ + 1]
                    deps = (pre if tok is None else [tok])
                    K.v(lambda e, a=get_a(cur, qa), w=w: e.tensor_scalar(
                        sTb[0:128, 0:width], a, w(0), None, ALU.mult), deps)
                    t2 = K.v(lambda e, b=get_a(cur, qb), w=w: e.scalar_tensor_tensor(
                        sTb[0:128, 0:width], b, w(1), sTb[0:128, 0:width], ALU.mult, ALU.add))
                    K.v(lambda e, a=get_a(cur, qa), w=w: e.tensor_scalar(
                        sTb2[0:128, 0:width], a, w(2), None, ALU.mult))
                    t4 = K.v(lambda e, b=get_a(cur, qb), w=w: e.scalar_tensor_tensor(
                        sTb2[0:128, 0:width], b, w(3), sTb2[0:128, 0:width], ALU.mult, ALU.add))
                    t5 = K.v(lambda e, a=get_a(cur, qa): e.tensor_copy(a, sTb[0:128, 0:width]), [t2])
                    tok = K.v(lambda e, b=get_a(cur, qb): e.tensor_copy(b, sTb2[0:128, 0:width]), [t4, t5])
        return cur, tok

    Zu, tokZu = build_Z("u", [tok_ones, tok_bpc])
    tokZub = K.v(lambda e, c=Zu: e.tensor_copy(sZbU[:], c[:]), [tokZu])
    tkb = K.ld(sbvec[:], d_bexp)
    bt = None
    for q in range(32):
        tmm = K.t(lambda e, q_=q: e.matmul(
            ps[3][0:128, 0:1], Zu[:, q_ * 128:(q_ + 1) * 128], sbvec[:, q_:q_ + 1],
            start=True, stop=True), [tkb, tokZu] + ([bt] if bt else []))
        bt = K.s(lambda e, q_=q: e.copy(sbvec2[:, q_:q_ + 1], ps[3][0:128, 0:1]), [tmm])
    Zd, tokZd = build_Z("d", [tokZub, bt])
    tokZdb = K.v(lambda e, c=Zd: e.tensor_copy(sZbD[:], c[:]), [tokZd])
    zbsc.close()

    sWq = [K.sbuf(f"Wq{i}", [128, 1024], BF16, scope=comp) for i in range(32)]
    sTb = K.sbuf("Tcb", [128, 1024], BF16, scope=comp)
    sTb2 = K.sbuf("Tcb2", [128, 1024], BF16, scope=comp)
    sWl = K.sbuf("Wload", [128, 1024], scope=comp)
    sWlb = K.sbuf("Wloadb", [128, 1024], BF16, scope=comp)
    sEv = K.sbuf("Evac", [128, 512], BF16, scope=comp)

    tprev = tokZdb
    for q in range(32):
        tkl = K.ld(sWl[:], d_WexpT[q * 128:(q + 1) * 128, :], [tprev, tokZdb])
        tkl = K.s(lambda e: e.copy(sWlb[:], sWl[:]), [tkl])
        evs = []
        for nch in range(2):
            tmm = K.t(lambda e, q_=q, n_=nch: e.matmul(
                ps[n_][:], sZbU[:, q_ * 128:(q_ + 1) * 128],
                sWlb[:, n_ * 512:(n_ + 1) * 512],
                start=True, stop=True), [tkl, tokZub])
            evs.append(K.s(lambda e, q_=q, n_=nch: e.copy(
                sWq[q_][:, n_ * 512:(n_ + 1) * 512], ps[n_][:]), [tmm]))
        tprev = evs[-1]
    Wu, tok_bu = bstep_pairs("u", sWq, lambda L, q: L[q][:], 1024, [tprev])
    tok_wpe = []
    for q in range(32):
        for mt in range(NDT):
            tt = K.t(lambda e, q_=q, m_=mt: e.transpose(
                ps[2][0:128, 0:64].bitcast(BF16), Wu[q_][:, m_ * 128:(m_ + 1) * 128], s_eyeb[:]),
                [tok_bu, tok_eyeb] + ([tok_wpe[-1]] if tok_wpe else []))
            te = K.s(lambda e: e.copy(sEv[0:128, 0:128], ps[2][0:128, 0:64].bitcast(BF16)), [tt])
            tok_wpe.append(K.stg(t_wpe[mt * 128:(mt + 1) * 128, q * 128:(q + 1) * 128],
                                 sEv[0:128, 0:128], [te]))
    bias_buf, btok = bstep_pairs("u", sbvec2, lambda L, q: L[:, q:q + 1], 1, [bt, tok_bu])
    tk_upb = K.ld(sWl[0:128, 0:32], d_upbias)
    tok_bpu = K.v(lambda e, bb=bias_buf: e.tensor_add(s_bpu[:], bb[:], sWl[0:128, 0:32]),
                  [btok, tk_upb])

    tprev = None
    _dpre = [tok_bpu, tok_wpe[-1]]
    for q in range(32):
        tkl = K.ld(sWl[:], d_Wcon[q * 128:(q + 1) * 128, :], ([tprev] if tprev else _dpre))
        tprev = K.s(lambda e, q_=q: e.copy(sWq[q_][:], sWl[:]), [tkl] + ([] if tprev else _dpre))
    Wd1, tok_bd = bstep_pairs("d", sWq, lambda L, q: L[q][:], 1024, [tprev])
    tok_wpc = []
    for q in range(32):
        for nch in range(2):
            tmm = K.t(lambda e, q_=q, n_=nch: e.matmul(
                ps[4][:], sZbD[:, q_ * 128:(q_ + 1) * 128], Wd1[q_][:, n_ * 512:(n_ + 1) * 512],
                start=True, stop=True), [tok_bd, tokZdb] + ([tok_wpc[-1]] if tok_wpc else []))
            te = K.s(lambda e: e.copy(sEv[:], ps[4][:]), [tmm])
            tok_wpc.append(K.stg(t_wpc[q * 128:(q + 1) * 128, nch * 512:(nch + 1) * 512],
                                 sEv[:], [te]))
    comp.close()
    tok_compose = tok_wpc[-1]

    ln_idx = [0]

    def layernorm_make(pre, dst_fn):
        ln_idx[0] += 1
        nm = ln_idx[0]
        sc = contextlib.ExitStack()
        stg_ = K.sbuf(f"lnstage{nm}", [128, TOK], scope=sc)
        arep = K.sbuf(f"lnarep{nm}", [128, TOK], scope=sc)
        brep = K.sbuf(f"lnbrep{nm}", [128, TOK], scope=sc)
        s_stat = K.sbuf(f"lnstat{nm}", [1, 2, TOK], scope=sc)
        ev = []
        for ch in range(4):
            tmm = None
            for i in range(NDT):
                tmm = K.t(lambda e, i_=i, c_=ch: e.matmul(
                    ps[4 + c_][0:1, :], s_ones[:], sx[i_][:, c_ * 512:(c_ + 1) * 512],
                    start=(i_ == 0), stop=(i_ == NDT - 1)), pre + [tok_ones])
            ev.append(K.s(lambda e, c_=ch: e.copy(s_stat[0:1, 0, c_ * 512:(c_ + 1) * 512],
                                                  ps[4 + c_][0:1, :]), [tmm]))
        tsq = None
        last_mm = {}
        for i in range(NDT):
            tsq = K.s(lambda e, i_=i: e.activation(stg_[:], sx[i_][:], AF.Square),
                      pre + ([last_mm.get(3)] if i > 0 else [ev[-1]]))
            for ch in range(4):
                last_mm[ch] = K.t(lambda e, i_=i, c_=ch: e.matmul(
                    ps[c_][0:1, :], s_ones[:], stg_[:, c_ * 512:(c_ + 1) * 512],
                    start=(i_ == 0), stop=(i_ == NDT - 1)), [tsq])
        for ch in range(4):
            ev.append(K.s(lambda e, c_=ch: e.copy(s_stat[0:1, 1, c_ * 512:(c_ + 1) * 512],
                                                  ps[c_][0:1, :]), [last_mm[ch]]))
        t1 = K.v(lambda e: e.tensor_scalar_mul(s_stat[0:1, 0, :], s_stat[0:1, 0, :], 1.0 / 1024), [ev[-1], ev[3]])
        t1 = K.v(lambda e: e.tensor_scalar_mul(s_stat[0:1, 1, :], s_stat[0:1, 1, :], 1.0 / 1024), [t1])
        t1 = K.v(lambda e: e.scalar_tensor_tensor(
            arep[0:1, :], s_stat[0:1, 0, :], -1.0, s_stat[0:1, 0, :], ALU.mult, ALU.mult), [t1])
        t1 = K.v(lambda e: e.tensor_add(s_stat[0:1, 1, :], s_stat[0:1, 1, :], arep[0:1, :]), [t1])
        t1 = K.s(lambda e: e.activation(s_stat[0:1, 1, :], s_stat[0:1, 1, :], AF.Sqrt, bias=s_eps[0:1, 0:1]), [t1, tok_eps])
        t1 = K.v(lambda e: e.reciprocal(s_stat[0:1, 1, :], s_stat[0:1, 1, :]), [t1])
        t1 = K.v(lambda e: e.tensor_tensor(s_stat[0:1, 0, :], s_stat[0:1, 0, :], s_stat[0:1, 1, :], ALU.mult), [t1])
        t1 = K.v(lambda e: e.tensor_scalar_mul(s_stat[0:1, 0, :], s_stat[0:1, 0, :], -1.0), [t1])
        tr1 = K.stg(arep[:], s_stat[0:1, 1, :].unsqueeze(1).broadcast_to([1, 128, TOK]), [t1], sem="rep")
        tr2 = K.stg(brep[:], s_stat[0:1, 0, :].unsqueeze(1).broadcast_to([1, 128, TOK]), [t1], sem="rep")
        out_toks = []
        for i in range(NDT):
            tn1 = K.v(lambda e, i_=i: e.tensor_tensor(stg_[:], sx[i_][:], arep[:], ALU.mult),
                      [tr1, tr2] + ([out_toks[-1]] if out_toks else []))
            out_toks.append(dst_fn(i, stg_, brep, tn1))
        sc.close()
        return out_toks[-1]

    a1 = contextlib.ExitStack()
    sM = [K.sbuf(f"Ma{i}", [128, TOK], BF16, scope=a1) for i in range(NDT)]
    s_dlr = K.sbuf("dlr_a", [64, TOK], BF16, scope=a1)
    s_bB = K.sbuf("bB_a", [16, TOK], BF16, scope=a1)
    s_bC = K.sbuf("bC_a", [16, TOK], BF16, scope=a1)

    def dst_a(i, stg_, brep, tn1):
        t2 = K.v(lambda e, i_=i: e.tensor_tensor(sM[i_][:], stg_[:], brep[:], ALU.add), [tn1])
        return K.v(lambda e, i_=i: e.scalar_tensor_tensor(
            sx[i_][:], sM[i_][:], s_Da[:, i_:i_ + 1], sx[i_][:], ALU.mult, ALU.add), [t2])

    tok_ln1 = layernorm_make(tok_x + [tok_compose, prev_cast[0]], dst_a)

    dbc_ev = []
    for ch in range(4):
        tmm = None
        for i in range(NDT):
            tmm = K.t(lambda e, i_=i, c_=ch: e.matmul(
                ps[c_][:], s_WdbcT[:, i_, :], sM[i_][:, c_ * 512:(c_ + 1) * 512],
                start=(i_ == 0), stop=(i_ == NDT - 1)), [tok_ln1])
        dbc_ev.append(K.s(lambda e, c_=ch: e.copy(s_dlr[:, c_ * 512:(c_ + 1) * 512], ps[c_][0:64, :]), [tmm]))
        dbc_ev.append(K.s(lambda e, c_=ch: e.copy(s_bB[:, c_ * 512:(c_ + 1) * 512], ps[c_][64:80, :]), [tmm]))
        dbc_ev.append(K.s(lambda e, c_=ch: e.copy(s_bC[:, c_ * 512:(c_ + 1) * 512], ps[c_][96:112, :]), [tmm]))

    a1b = contextlib.ExitStack()
    s_dloc = [K.sbuf(f"dloc{i}", [128, TOK], BF16, scope=a1b) for i in range(2)]
    s_wloc = [K.sbuf(f"wloc{i}", [128, TOK], BF16, scope=a1b) for i in range(2)]
    s_dstg = K.sbuf("dstg_a", [128, 512], scope=a1b)
    dw_last = None
    snd = []
    st_hist = []
    for i in range(NDT):
        par2 = i % 2
        for ch in range(4):
            par = 4 + (ch % 2)
            tmm = K.t(lambda e, i_=i, c_=ch, p_=par: e.matmul(
                ps[p_][:], s_WdtT[0:64, i_ * 128:(i_ + 1) * 128], s_dlr[:, c_ * 512:(c_ + 1) * 512],
                start=True, stop=True), [dbc_ev[-1]] + ([dw_last] if dw_last else []))
            t2 = K.s(lambda e, i_=i, p_=par: e.activation(
                s_dstg[:], ps[p_][:], AF.Exp, bias=s_bdt_a[:, i_:i_ + 1]), [tmm])
            dw_last = K.s(lambda e, i_=i, c_=ch, p2=par2: e.activation(
                s_dloc[p2][:, c_ * 512:(c_ + 1) * 512], s_dstg[:],
                AF.Ln, bias=1.0), [t2] + (st_hist[-2:-1] if len(st_hist) >= 2 else []))
        dw_last = K.v(lambda e, i_=i, p2=par2: e.tensor_tensor(
            s_wloc[p2][:], s_dloc[p2][:], sM[i_][:], ALU.mult), [dw_last])
        if i < 4:
            dd = t_scr_ad[i * 128:(i + 1) * 128, :]
            dw = t_scr_aw[i * 128:(i + 1) * 128, :]
        else:
            dd = t_cc1_in[(i - 4) * 128:(i - 3) * 128, :]
            dw = t_cc1_in[512 + (i - 4) * 128:512 + (i - 3) * 128, :]
        s1 = K.stg(dd, s_dloc[par2][:], [dw_last])
        s2 = K.stg(dw, s_wloc[par2][:], [s1])
        snd.append(s2)
        st_hist.append(s2)
    snd.append(K.stg(t_cc1_in[1024:1040, :], s_bB[:], [dbc_ev[-1]]))
    snd.append(K.stg(t_cc1_in[1040:1056, :], s_bC[:], [dbc_ev[-1]]))
    tok_ag1 = K.cc("AllGather", ALU.bypass, [t_cc1_in], [t_cc1_out], snd + [tok_regs])

    asm = []
    g4 = lambda t: t.rearrange("(g p) c -> p g c", p=128)
    b4 = lambda b: b[:].rearrange("p (g f) -> p g f", g=4)

    def asml(dst, local_ap, slot0_ap, slot1_ap, pre, bview=None, gview=None):
        bv = bview or b4
        gv = gview or (lambda t: t)
        t1 = select1(lambda b: (gv(dst[0]), bv(b)), local_ap, slot0_ap, pre)
        t2 = select1(lambda b: (gv(dst[1]), bv(b)), slot1_ap, local_ap, pre + [t1])
        return t2

    asm.append(asml((g4(t_af[:, 0:TOK]), g4(t_af[:, TOK:FULL])), g4(t_scr_ad),
                    g4(t_cc1_out[0, 0:512, :]), g4(t_cc1_out[1, 0:512, :]), [tok_ag1]))
    asm.append(asml((g4(t_wf[:, 0:TOK]), g4(t_wf[:, TOK:FULL])), g4(t_scr_aw),
                    g4(t_cc1_out[0, 512:1024, :]), g4(t_cc1_out[1, 512:1024, :]),
                    [tok_ag1, asm[-1]]))
    asm.append(asml((t_bcf[:, 0:TOK], t_bcf[:, TOK:FULL]), t_cc1_in[1024:1056, :],
                    t_cc1_out[0, 1024:1056, :], t_cc1_out[1, 1024:1056, :],
                    [tok_ag1, asm[-1]],
                    bview=lambda b: b[0:32, 0:TOK], gview=lambda t: t))
    tok_asm = drain_dg([asm[-1]])
    a1b.close()
    a1.close()

    def make_scanbufs(scope, tag):
        sets = []
        for par in range(2):
            B = {}
            B["arg"] = K.sbuf(f"arg_{tag}{par}", [128, FT], scope=scope)
            B["dA"] = K.sbuf(f"dA_{tag}{par}", [128, FT], BF16, scope=scope)
            B["BX"] = K.sbuf(f"BX_{tag}{par}", [128, FT], BF16, scope=scope)
            B["h"] = K.sbuf(f"h_{tag}{par}", [128, FT], BF16, scope=scope)
            B["tmp"] = K.sbuf(f"tmp_{tag}{par}", [128, FT], BF16, scope=scope)
            B["inj"] = K.sbuf(f"inj_{tag}{par}", [128, S], scope=scope)
            B["yr"] = K.sbuf(f"yr_{tag}{par}", [128, TS], scope=scope)
            B["yrb"] = K.sbuf(f"yrb_{tag}{par}", [128, TS], BF16, scope=scope)
            sets.append(B)
        R = []
        for par in range(2):
            D = {}
            D["fl1"] = K.sbuf(f"fl1_{tag}{par}", [1, FT], BF16, scope=scope)
            D["fl2"] = K.sbuf(f"fl2_{tag}{par}", [1, FT], BF16, scope=scope)
            D["Brep"] = K.sbuf(f"Brep_{tag}{par}", [128, FT], BF16, scope=scope)
            D["Crep"] = K.sbuf(f"Crep_{tag}{par}", [128, FT], BF16, scope=scope)
            R.append(D)
        return sets, R

    def rep_build(D, bc_B, bc_C, j, pre):
        tf1 = K.stg(D["fl1"][:].rearrange("o (s t) -> o s t", s=S),
                    bc_B[:, j * TS:(j + 1) * TS].unsqueeze(1), pre, sem="rep")
        tf2 = K.stg(D["fl2"][:].rearrange("o (s t) -> o s t", s=S),
                    bc_C[:, j * TS:(j + 1) * TS].unsqueeze(1), pre, sem="rep")
        tr1 = K.stg(D["Brep"][:], D["fl1"][0:1, :].unsqueeze(1).broadcast_to([1, 128, FT]), [tf1], sem="rep")
        tr2 = K.stg(D["Crep"][:], D["fl2"][0:1, :].unsqueeze(1).broadcast_to([1, 128, FT]), [tf2], sem="rep")
        return [tr1, tr2]

    def scan_itemA(B, D, Arow, d_ap, w_ap, pre_v, rev):
        tg1 = K.v(lambda e: e.tensor_tensor(
            sm(B["arg"][:]), Arow.unsqueeze(2).broadcast_to([128, S, TS]),
            d_ap.unsqueeze(1).broadcast_to([128, S, TS]), ALU.mult), pre_v)
        ts1 = K.s(lambda e: e.activation(B["dA"][:], B["arg"][:], AF.Exp), [tg1])
        brep_ap = sm(D["Brep"][:])[:, :, ::-1] if rev else sm(D["Brep"][:])
        tv1 = K.v(lambda e, ba=brep_ap: e.tensor_tensor(
            sm(B["BX"][:]), w_ap.unsqueeze(1).broadcast_to([128, S, TS]), ba, ALU.mult), pre_v)
        return dict(B=B, D=D, ts1=ts1, tv1=tv1, rev=rev)

    def scan_itemB(ctx, hp_ap, hp_pre):
        B, D, rev = ctx["B"], ctx["D"], ctx["rev"]
        crep_ap = sm(D["Crep"][:])[:, :, ::-1] if rev else sm(D["Crep"][:])
        dA3, BX3 = sm(B["dA"][:]), sm(B["BX"][:])
        tv2 = K.v(lambda e: e.tensor_tensor(B["inj"][:], dA3[:, :, 0], hp_ap, ALU.mult),
                  [ctx["ts1"], ctx["tv1"]] + hp_pre)
        tv3 = K.v(lambda e: e.tensor_tensor(BX3[:, :, 0], BX3[:, :, 0], B["inj"][:], ALU.add), [tv2])
        tv4 = K.v(lambda e: e.memset(dA3[:, :, 0], 0.0), [tv3])
        tv5 = K.v(lambda e: e.tensor_tensor_scan(B["h"][:], B["dA"][:], B["BX"][:], 0.0, ALU.mult, ALU.add), [tv4])
        tv6 = K.v(lambda e: e.tensor_copy(hp_ap, sm(B["h"][:])[:, :, TS - 1]), [tv5])
        tv7 = K.v(lambda e, ca=crep_ap: e.tensor_tensor(sm(B["tmp"][:]), sm(B["h"][:]), ca, ALU.mult), [tv5])

        def _red(e, B=B):
            with K.nc.allow_low_precision(reason="DVE reduce accumulates fp32 internally"):
                return e.tensor_reduce(
                    B["yr"][:], B["tmp"][:].rearrange("p (s t) -> p t s", s=S), AX.X, ALU.add)
        tg2 = K.v(_red, [tv7])
        return tv6, tg2

    a2 = contextlib.ExitStack()
    SBs, SRs = make_scanbufs(a2, "a")

    s_dsl = K.sbuf("dsl_a", [128, 2, 4, TS], BF16, scope=a2)
    s_wsl = K.sbuf("wsl_a", [128, 2, 4, TS], BF16, scope=a2)
    s_hp4 = K.sbuf("hp_a", [128, 4, S], scope=a2)

    tok_bcS = tok_asm
    tok_hp0 = K.v(lambda e: e.memset(s_hp4[:].rearrange("p a s -> p (a s)"), 0.0), [tok_asm])
    item_hist = []
    rep_hist = []
    tsl_hist = []
    pend = [None]
    nitem = 0

    def flushA(ctx):
        if ctx is None:
            return
        hpw, red = scan_itemB(ctx, ctx["hp_ap"], [tok_hp0])
        tcb = K.v(lambda e, B_=ctx["B"]: e.tensor_copy(B_["yrb"][:], B_["yr"][:]), [red])
        tyc = K.stg(t_scr_ay[ctx["i"] * 128:(ctx["i"] + 1) * 128,
                             ctx["j"] * TS:(ctx["j"] + 1) * TS], ctx["B"]["yrb"][:], [tcb], sem="dg2")
        item_hist.append(tyc)

    for j in range(NJ):
        jp = j % 2
        rguard = rep_hist[-1] if len(rep_hist) >= 2 else None
        reps = rep_build(SRs[jp], t_bcf[0:16, :], t_bcf[16:32, :],
                         j, [tok_bcS] + ([rguard] if rguard else []))
        for i in range(4):
            par = nitem % 2
            SB = SBs[par]
            bsel = par
            guard = tsl_hist[-2][1] if len(tsl_hist) >= 2 else None
            bufg = tsl_hist[-2][0] if len(tsl_hist) >= 2 else None
            tl1 = K.ld(s_dsl[:, bsel, i, :].unsqueeze(1),
                       t_af[i * 128:(i + 1) * 128, j * TS:(j + 1) * TS].unsqueeze(1),
                       [tok_asm, guard], sem=f"dy{par}")
            tl2 = K.ld(s_wsl[:, bsel, i, :].unsqueeze(1),
                       t_wf[i * 128:(i + 1) * 128, j * TS:(j + 1) * TS].unsqueeze(1),
                       [tl1], sem=f"dy{par}")
            pv = [tl2] + reps + ([bufg] if bufg else [])
            ctx = scan_itemA(SBs[par], SRs[jp], s_Aa[:, i, :], s_dsl[:, bsel, i, :],
                             s_wsl[:, bsel, i, :], pv, rev=False)
            ctx.update(i=i, j=j, hp_ap=s_hp4[:, i, :])
            tsl_hist.append((ctx["ts1"], ctx["tv1"]))
            flushA(pend[0])
            pend[0] = ctx
            nitem += 1
        rep_hist.append(item_hist[-1] if item_hist else None)
    flushA(pend[0])
    pend[0] = None
    prev_item = item_hist[-1]
    snd2 = []
    tok_aydone = drain_dg([prev_item])
    snd2.append(select1(lambda b: (g4(t_cc2_in), b4(b)),
                        t_scr_ay[:, TOK:FULL].rearrange("(g p) c -> p g c", p=128),
                        t_scr_ay[:, 0:TOK].rearrange("(g p) c -> p g c", p=128), [tok_aydone]))
    tok_ag2 = K.cc("AllGather", ALU.bypass, [t_cc2_in], [t_cc2_out], snd2)
    a2.close()

    a3 = contextlib.ExitStack()
    s_yfx = [K.sbuf(f"yfx{i}", [128, TOK], BF16, scope=a3) for i in range(NDT)]
    ytoks = []
    t1 = select1(lambda b: (s_yfx[0][:], b[:].rearrange("p (g f) -> p g f", g=4)[:, 0, :]),
                 t_scr_ay[:, 0:TOK].rearrange("(g p) c -> p g c", p=128),
                 t_scr_ay[:, TOK:FULL].rearrange("(g p) c -> p g c", p=128),
                 [tok_ag2, tok_aydone])
    ytoks.append(t1)
    for i in range(1, 4):
        ytoks.append(K.stg(s_yfx[i][:], s_bnc[:].rearrange("p (g f) -> p g f", g=4)[:, i, :], [t1], sem="dg2"))
    t2 = select1(lambda b: (s_yfx[4][:], b[:].rearrange("p (g f) -> p g f", g=4)[:, 0, :]),
                 g4(t_cc2_out[1]), g4(t_cc2_out[0]), ytoks)
    ytoks.append(t2)
    for i in range(1, 4):
        ytoks.append(K.stg(s_yfx[4 + i][:], s_bnc[:].rearrange("p (g f) -> p g f", g=4)[:, i, :], [t2], sem="dg2"))
    tok_attn_a = None
    for i in range(NDT):
        tok_attn_a = K.v(lambda e, i_=i: e.tensor_add(sx[i_][:], sx[i_][:], s_yfx[i_][:]),
                         [ytoks[-1]])
    a3.close()

    def ffn(pre):
        fsc0 = contextlib.ExitStack()
        sMt = K.sbuf(f"Mf{ln_idx[0]}", [128, TOK], BF16, scope=fsc0)
        mspill = []

        def dst_f(i, stg_, brep, tn1):
            t2 = K.v(lambda e: e.tensor_tensor(sMt[:], stg_[:], brep[:], ALU.add), [tn1])
            tk = K.stg(t_scr_M[i * 128:(i + 1) * 128, :], sMt[:], [t2])
            mspill.append(tk)
            return tk

        tok_ln = layernorm_make(pre, dst_f)
        fsc0.close()

        fsc = contextlib.ExitStack()
        nm = ln_idx[0]
        sWe = [K.sbuf(f"We{nm}_{k}", [128, 2048], BF16, scope=fsc) for k in range(NDT)]
        sWc = [K.sbuf(f"Wc{nm}_{q}", [128, 1024], BF16, scope=fsc) for q in range(16)]
        sH = [K.sbuf(f"H{nm}_{q}", [128, 256], BF16, scope=fsc) for q in range(16)]
        sMc = K.sbuf(f"Mc{nm}", [128, NDT, 256], BF16, scope=fsc)
        last_x = tok_ln
        for grp in range(2):
            hbase = grp * 16
            wl = [K.ld(sWe[k][:], t_wpe[k * 128:(k + 1) * 128, hbase * 128:(hbase + 16) * 128],
                       [tok_ln, last_x]) for k in range(NDT)]
            wl += [K.ld(sWc[q][:], t_wpc[(hbase + q) * 128:(hbase + q + 1) * 128, :],
                        [tok_ln, last_x]) for q in range(16)]
            for ch in range(8):
                tmc = K.ld(sMc[:],
                           t_scr_M.rearrange("(a p) t -> p a t", p=128)[:, :, ch * 256:(ch + 1) * 256],
                           [tok_ln, last_x])
                ev_up = {}
                for ht in range(16):
                    par = ht % 2
                    tmm = None
                    for k in range(NDT):
                        tmm = K.t(lambda e, h_=ht, k_=k, p_=par: e.matmul(
                            ps[p_][:, 0:256], sWe[k_][:, h_ * 128:(h_ + 1) * 128], sMc[:, k_, :],
                            start=(k_ == 0), stop=(k_ == NDT - 1)),
                            wl + [tmc] + ([ev_up.get(ht - 2)] if ev_up.get(ht - 2) else []))
                    ev_up[ht] = K.s(lambda e, h_=ht, p_=par, hb=hbase: e.activation(
                        sH[h_][:], ps[p_][:, 0:256], AF.Gelu,
                        bias=s_bpu[:, hb + h_:hb + h_ + 1]), [tmm])
                ev_dn = {}
                for mt in range(NDT):
                    par = 2 + (mt % 2)
                    tmm = None
                    for q in range(16):
                        tmm = K.t(lambda e, m_=mt, q_=q, p_=par: e.matmul(
                            ps[p_][:, 0:256], sWc[q_][:, m_ * 128:(m_ + 1) * 128], sH[q_][:],
                            start=(q_ == 0), stop=(q_ == 15)),
                            [ev_up[15]] + ([ev_dn.get(mt - 2)] if ev_dn.get(mt - 2) else []))
                    if grp == 0:
                        ev_dn[mt] = K.v(lambda e, m_=mt, p_=par, c_=ch: e.scalar_tensor_tensor(
                            sx[m_][:, c_ * 256:(c_ + 1) * 256], ps[p_][:, 0:256],
                            s_bpc[:, m_:m_ + 1], sx[m_][:, c_ * 256:(c_ + 1) * 256],
                            ALU.add, ALU.add), [tmm, mspill[mt]])
                    else:
                        ev_dn[mt] = K.v(lambda e, m_=mt, p_=par, c_=ch: e.tensor_add(
                            sx[m_][:, c_ * 256:(c_ + 1) * 256],
                            sx[m_][:, c_ * 256:(c_ + 1) * 256], ps[p_][:, 0:256]), [tmm])
                last_x = ev_dn[NDT - 1]
        fsc.close()
        return last_x

    tok_ffn1 = ffn([tok_attn_a])

    b0 = contextlib.ExitStack()
    sMB = [K.sbuf(f"Mb{i}", [128, TOK], BF16, scope=b0) for i in range(NDT)]

    def dst_b(i, stg_, brep, tn1):
        return K.v(lambda e, i_=i: e.tensor_tensor(sMB[i_][:], stg_[:], brep[:], ALU.add), [tn1])

    tok_ln3 = layernorm_make([tok_ffn1], dst_b)

    sWinL = K.sbuf("WinL", [128, 2, NDT, 128], scope=b0)
    sWinB = K.sbuf("WinB", [128, 2, NDT, 128], BF16, scope=b0)
    sOutT = K.sbuf("OutT_b", [128, 2, TOK], BF16, scope=b0)
    ip_hist = []
    tcast_prev = None
    last_evs = None
    for tix in range(32):
        par = tix % 2
        guard = ip_hist[-2] if len(ip_hist) >= 2 else None
        tkl = K.ld(sWinL[:, par, :, :],
                   d_WinT.rearrange("(a p) c -> p a c", p=128)[:, :, tix * 128:(tix + 1) * 128],
                   [tok_ln3, guard], sem=f"dy{par}")
        tcast = K.s(lambda e, p_=par: e.copy(sWinB[:, p_, :, :], sWinL[:, p_, :, :]),
                    [tkl] + ([tcast_prev] if tcast_prev else []))
        tcast_prev = tcast
        evs = []
        for ch in range(4):
            pp = 4 + (ch % 2)
            tmm = None
            for k in range(NDT):
                tmm = K.t(lambda e, p_=par, k_=k, c_=ch, q_=pp: e.matmul(
                    ps[q_][:], sWinB[:, p_, k_, :], sMB[k_][:, c_ * 512:(c_ + 1) * 512],
                    start=(k_ == 0), stop=(k_ == NDT - 1)),
                    [tcast] + ([evs[-2]] if len(evs) >= 2 else ([last_evs] if last_evs else [])))
            if tix < 16:
                evs.append(K.s(lambda e, p_=par, c_=ch, q_=pp: e.copy(
                    sOutT[:, p_, c_ * 512:(c_ + 1) * 512], ps[q_][:]), [tmm]))
            else:
                evs.append(K.s(lambda e, p_=par, c_=ch, q_=pp: e.activation(
                    sOutT[:, p_, c_ * 512:(c_ + 1) * 512], ps[q_][:], AF.Silu), [tmm]))
        if tix < 8:
            dst = t_scr_bxin[tix * 128:(tix + 1) * 128, :]
        elif tix < 16:
            dst = t_ccb1_in[(tix - 8) * 128:(tix - 7) * 128, :]
        elif tix < 24:
            dst = t_scr_bz[(tix - 16) * 128:(tix - 15) * 128, :]
        else:
            dst = t_ccb1_in[1024 + (tix - 24) * 128:1024 + (tix - 23) * 128, :]
        ip_hist.append(K.stg(dst, sOutT[:, par, :], [evs[-1]]))
        last_evs = evs[-1]
    tok_agb1 = K.cc("AllGather", ALU.bypass, [t_ccb1_in], [t_ccb1_out], [ip_hist[-1], ip_hist[-2]])
    b0.close()

    b1 = contextlib.ExitStack()
    s_xinF = [K.sbuf(f"xinF{i}", [128, 6 + FULL], BF16, scope=b1) for i in range(NDT)]
    s_cacc = K.sbuf("cacc_b", [128, FULL], scope=b1)
    s_pev1 = K.sbuf("pev1_b", [64, 512], scope=b1)
    s_pev2 = K.sbuf("pev2_b", [16, 512], scope=b1)
    s_pev3 = K.sbuf("pev3_b", [16, 512], scope=b1)
    s_xct = K.sbuf("xct_b", [128, FULL], BF16, scope=b1)

    asmb = [K.v(lambda e, i_=i: e.memset(s_xinF[i_][:, 0:3], 0.0), [tok_agb1]) for i in range(NDT)]
    asmb += [K.v(lambda e, i_=i: e.memset(s_xinF[i_][:, 3 + FULL:6 + FULL], 0.0), [tok_agb1]) for i in range(NDT)]
    g4b = lambda t, g0: t[g0 * 512:(g0 + 1) * 512, :].rearrange("(g p) c -> p g c", p=128)
    for g0 in range(2):
        stg_ = stage_sel(g4b(t_scr_bxin, g0), g4b(t_ccb1_out[0, 0:1024, :], g0),
                         [tok_agb1, asmb[-1]])
        tX = sel_read(False, stg_)
        for ii in range(4):
            i = g0 * 4 + ii
            asmb.append(K.stg(s_xinF[i][:, 3:3 + TOK],
                              s_bnc[:].rearrange("p (g f) -> p g f", g=4)[:, ii, :], [tX], sem="dg2"))
        stg_ = stage_sel(g4b(t_ccb1_out[1, 0:1024, :], g0), g4b(t_scr_bxin, g0), asmb[-4:])
        tY = sel_read(False, stg_)
        for ii in range(4):
            i = g0 * 4 + ii
            asmb.append(K.stg(s_xinF[i][:, 3 + TOK:3 + FULL],
                              s_bnc[:].rearrange("p (g f) -> p g f", g=4)[:, ii, :], [tY], sem="dg2"))
        stg_ = stage_sel(g4b(t_scr_bz, g0), g4b(t_ccb1_out[0, 1024:2048, :], g0), asmb[-4:])
        tX = sel_read(False, stg_)
        asmb.append(K.stg(g4b(t_sz[:, 0:TOK], g0), s_bnc[:].rearrange("p (g f) -> p g f", g=4), [tX], sem="dg2"))
        stg_ = stage_sel(g4b(t_ccb1_out[1, 1024:2048, :], g0), g4b(t_scr_bz, g0), [asmb[-1]])
        tY = sel_read(False, stg_)
        asmb.append(K.stg(g4b(t_sz[:, TOK:FULL], g0), s_bnc[:].rearrange("p (g f) -> p g f", g=4), [tY], sem="dg2"))
    tok_asmb = drain_dg([asmb[-1]])

    proj_ev = {}
    xc_done = {}
    xcst = {}
    for dj, d in enumerate("fb"):
        pe = []
        for i in range(NDT):
            offs = [k for k in range(4)] if d == "f" else [6 - k for k in range(4)]
            tv = K.v(lambda e, i_=i, o_=offs[0], d_=d: e.tensor_scalar(
                s_cacc[:], s_xinF[i_][:, o_:o_ + FULL], s_cw[d_][:, i_, 0:1], None, ALU.mult),
                [tok_asmb] + ([xc_done.get((dj, i - 1))] if xc_done.get((dj, i - 1)) else
                              ([xc_done.get((0, NDT - 1))] if dj == 1 and i == 0 else [])))
            for k in range(1, 4):
                tv = K.v(lambda e, i_=i, o_=offs[k], k_=k, d_=d: e.scalar_tensor_tensor(
                    s_cacc[:], s_xinF[i_][:, o_:o_ + FULL], s_cw[d_][:, i_, k_:k_ + 1],
                    s_cacc[:], ALU.mult, ALU.add), [tv])
            tsil = K.s(lambda e, i_=i, d_=d: e.activation(
                s_xct[:], s_cacc[:], AF.Silu, bias=s_cb[d_][:, i_:i_ + 1]), [tv])
            xcst[(dj, i)] = K.stg(t_xc[d][i * 128:(i + 1) * 128, :], s_xct[:], [tsil])
            mm = []
            for ch in range(8):
                mm.append(K.t(lambda e, i_=i, c_=ch, d_=d: e.matmul(
                    ps[c_][:], s_xpT[d_][:, i_, :], s_xct[:, c_ * 512:(c_ + 1) * 512],
                    start=(i_ == 0), stop=(i_ == NDT - 1)),
                    [tsil] + ([proj_ev.get(dj - 1)] if i == 0 and proj_ev.get(dj - 1) else [])))
            xc_done[(dj, i)] = mm[-1]
            pe = mm
        t1_ = None
        for ch in range(8):
            cl = slice(ch * 512, (ch + 1) * 512)
            tc = K.s(lambda e, p_=ch: e.copy(s_pev1[:], ps[p_][0:64, :]),
                     [pe[ch], xc_done[(dj, NDT - 1)]] + ([t1_] if t1_ else []))
            tc = K.s(lambda e, p_=ch: e.copy(s_pev2[:], ps[p_][64:80, :]), [tc])
            tc = K.s(lambda e, p_=ch: e.copy(s_pev3[:], ps[p_][96:112, :]), [tc])
            t1_ = K.stg(t_ccb2_in[dj * 96:dj * 96 + 64, cl], s_pev1[:], [tc])
            t1_ = K.stg(t_ccb2_in[dj * 96 + 64:dj * 96 + 80, cl], s_pev2[:], [t1_])
            t1_ = K.stg(t_ccb2_in[dj * 96 + 80:dj * 96 + 96, cl], s_pev3[:], [t1_])
        proj_ev[dj] = t1_
    tok_arb2 = K.cc("AllReduce", ALU.add, [t_ccb2_in], [t_ccb2_out], [proj_ev[0], proj_ev[1]])
    b1.close()

    b2 = contextlib.ExitStack()
    SCs, SRb = make_scanbufs(b2, "b")
    s_dlrs = K.sbuf("dlrs_b", [64, 2, 512], BF16, scope=b2)

    s_dsl2 = K.sbuf("dsl_b", [128, 2, TS], BF16, scope=b2)
    s_xcsl = K.sbuf("xcsl_b", [128, 2, TS], BF16, scope=b2)
    s_yfsl = K.sbuf("yfsl_b", [128, 2, TS], BF16, scope=b2)
    s_szsl = K.sbuf("szsl_b", [128, 2, TS], BF16, scope=b2)
    s_yout = K.sbuf("yout_b", [128, TS], BF16, scope=b2)
    s_hpB = K.sbuf("hpB", [128, NDT, S], scope=b2)
    s_dstgB = s_bnc[:].bitcast(F32)[:, 1024:1536]

    b2p = contextlib.ExitStack()
    s_pl1 = s_bnc2[:].bitcast(F32)[0:64, 0:1024]
    s_pl1b = s_bnc2[0:64, 2048:3072]
    s_pl2b = s_bnc2[0:16, 3072:4096]
    s_pl2 = s_bnc2[:].bitcast(F32)[0:16, 2048:3072]
    s_pl3 = s_bnc2[:].bitcast(F32)[0:16, 3072:4096]
    pcast = []
    for dj, d in enumerate("fb"):
        for hf in range(4):
            cl = slice(hf * 1024, (hf + 1) * 1024)
            tk1 = K.ld(s_pl1, t_ccb2_out[dj * 96:dj * 96 + 64, cl], [tok_arb2] + pcast)
            tk2 = K.ld(s_pl2, t_ccb2_out[dj * 96 + 64:dj * 96 + 80, cl], [tk1])
            tk3 = K.ld(s_pl3, t_ccb2_out[dj * 96 + 80:dj * 96 + 96, cl], [tk2])
            tcp = K.s(lambda e: e.copy(s_pl1b, s_pl1), [tk1])
            pcast.append(K.stg(t_dlr[dj, :, cl], s_pl1b, [tcp]))
            tcp = K.s(lambda e: e.copy(s_pl2b, s_pl2), [tk2])
            pcast.append(K.stg(t_bcb[dj, 0:16, cl], s_pl2b[0:16, :], [tcp]))
            tcp = K.s(lambda e: e.copy(s_pl2b, s_pl3), [tk3, pcast[-1]])
            pcast.append(K.stg(t_bcb[dj, 16:32, cl], s_pl2b[0:16, :], [tcp]))
    tok_pcast = pcast[-1]
    b2p.close()

    tok_ysum_last = None
    for dj, d in enumerate("fb"):
        dt_st = {}
        last = None
        for i in range(NDT):
            for ch in range(8):
                par = ch % 2
                tld = K.ld(s_dlrs[:, ch % 2, :].unsqueeze(1),
                           t_dlr[dj, :, ch * 512:(ch + 1) * 512].unsqueeze(1),
                           [tok_pcast, tok_asmb] + ([last] if last else []), sem=f"dy{ch % 2}")
                tmm = K.t(lambda e, i_=i, p_=par, d_=d, c2=ch % 2: e.matmul(
                    ps[p_][:], s_dtwT[d_][0:64, i_ * 128:(i_ + 1) * 128],
                    s_dlrs[:, c2, :],
                    start=True, stop=True),
                    [tld] + ([last] if last else []) +
                    ([tok_ysum_last] if tok_ysum_last else []))
                t2 = K.s(lambda e, p_=par, i_=i, d_=d: e.activation(
                    s_dstgB, ps[p_][:], AF.Exp, bias=s_dtb[d_][:, i_:i_ + 1]), [tmm])
                last = K.s(lambda e: e.activation(s_bnc[:, 4096:4608], s_dstgB, AF.Ln, bias=1.0), [t2])
                last = K.stg(t_dt[i * 128:(i + 1) * 128, ch * 512:(ch + 1) * 512], s_bnc[:, 4096:4608], [last])
            dt_st[i] = last
        tok_dtb = drain_dg([dt_st[NDT - 1]])
        rev = (d == "b")
        tok_hp0b = K.v(lambda e: e.memset(s_hpB[:].rearrange("p a s -> p (a s)"), 0.0),
                       [dt_st[NDT - 1]] + ([tok_ysum_last] if tok_ysum_last else []))
        jorder = list(range(NJ)) if not rev else list(range(NJ - 1, -1, -1))
        prev_item = None
        hist = []
        rep_hist = []
        tslh = []
        pend = [None]
        nitem = 0

        def flushB(ctx):
            nonlocal prev_item
            if ctx is None:
                return
            hpw, red = scan_itemB(ctx, ctx["hp_ap"], [tok_hp0b])
            i_, j_, bsel_, SC_ = ctx["i"], ctx["j"], ctx["bsel"], ctx["B"]
            if d == "f":
                tyf = K.v(lambda e, i2=i_, b2=bsel_, B2=SC_: e.scalar_tensor_tensor(
                    s_yout[:], s_xcsl[:, b2, :], s_Db["f"][:, i2:i2 + 1], B2["yr"][:],
                    ALU.mult, ALU.add), [red] + ([prev_item] if prev_item else []))
                prev_item = K.stg(t_yf[i_ * 128:(i_ + 1) * 128, j_ * TS:(j_ + 1) * TS],
                                  s_yout[:], [tyf], sem="dg2")
            else:
                t1 = K.v(lambda e, b2=bsel_, B2=SC_: e.tensor_tensor(
                    s_bnc[:].bitcast(F32)[:, 1536:1536 + TS], s_yfsl[:, b2, :], B2["yr"][:, ::-1], ALU.add),
                    [red, ctx["tl2"], ctx["tsz"]] + ([prev_item] if prev_item else []))
                t2 = K.v(lambda e, i2=i_, b2=bsel_: e.scalar_tensor_tensor(
                    s_bnc[:].bitcast(F32)[:, 1536:1536 + TS], s_xcsl[:, b2, ::-1], s_Db["b"][:, i2:i2 + 1], s_bnc[:].bitcast(F32)[:, 1536:1536 + TS],
                    ALU.mult, ALU.add), [t1])
                t3 = K.v(lambda e, b2=bsel_: e.tensor_tensor(
                    s_yout[:], s_bnc[:].bitcast(F32)[:, 1536:1536 + TS], s_szsl[:, b2, :], ALU.mult), [t2])
                prev_item = K.stg(t_scr_bys[i_ * 128:(i_ + 1) * 128, j_ * TS:(j_ + 1) * TS],
                                  s_yout[:], [t3], sem="dg2")
            hist.append(prev_item)

        for j in jorder:
            jp = j % 2
            rguard = rep_hist[-1] if len(rep_hist) >= 2 else None
            reps = rep_build(SRb[jp], t_bcb[dj, 0:16, :], t_bcb[dj, 16:32, :], j,
                             [tok_pcast] + ([rguard] if rguard else []) +
                             ([tok_ysum_last] if tok_ysum_last and not hist else []))
            for i in range(NDT):
                par = nitem % 2
                SC = SCs[par]
                bsel = par
                guard = tslh[-2][1] if len(tslh) >= 2 else None
                bufg = tslh[-2][0] if len(tslh) >= 2 else None
                fing = hist[-1] if len(hist) >= 1 else None
                tl0 = K.ld(s_dsl2[:, bsel, :].unsqueeze(1),
                           t_dt[i * 128:(i + 1) * 128, j * TS:(j + 1) * TS].unsqueeze(1),
                           [tok_dtb, guard, fing], sem=f"dy{par}")
                tl1 = K.ld(s_xcsl[:, bsel, :].unsqueeze(1),
                           t_xc[d][i * 128:(i + 1) * 128, j * TS:(j + 1) * TS].unsqueeze(1),
                           [tok_dtb, guard, fing], sem=f"dy{par}")
                if rev:
                    d_ap = s_dsl2[:, bsel, ::-1]
                    w_src = s_xcsl[:, bsel, ::-1]
                else:
                    d_ap = s_dsl2[:, bsel, :]
                    w_src = s_xcsl[:, bsel, :]
                pv = [tl0, tl1] + reps + ([bufg] if bufg else [])
                ctx = scan_itemA(SC, SRb[jp], s_Ab[d][:, i, :], d_ap, w_src, pv, rev=rev)
                ctx.update(i=i, j=j, bsel=bsel, hp_ap=s_hpB[:, i, :], tl2=None, tsz=None)
                if d == "b":
                    ctx["tl2"] = K.ld(s_yfsl[:, bsel, :].unsqueeze(1),
                                      t_yf[i * 128:(i + 1) * 128, j * TS:(j + 1) * TS].unsqueeze(1),
                                      [yf_done, fing], sem=f"dy{par}")
                    ctx["tsz"] = K.ld(s_szsl[:, bsel, :].unsqueeze(1),
                                      t_sz[i * 128:(i + 1) * 128, j * TS:(j + 1) * TS].unsqueeze(1),
                                      [tok_asmb, fing], sem=f"dy{par}")
                tslh.append((ctx["ts1"], ctx["tv1"]))
                flushB(pend[0])
                pend[0] = ctx
                nitem += 1
            rep_hist.append(hist[-1] if hist else None)
        flushB(pend[0])
        pend[0] = None
        if d == "f":
            cnt2 = K.p.cnt["dg2"]
            K.p.op("gpsimd", None, ((("dg2", cnt2),) + (("dg2", prev_item[1]),)))
            yf_done = K.p.op("gpsimd", lambda e: e.dma_start(t_dbg, s_flags[:]), (), "bar", 16)
        tok_ysum_last = prev_item
    snd3 = []
    cnt2 = K.p.cnt["dg2"]
    K.p.op("gpsimd", None, ((("dg2", cnt2),),) if False else (("dg2", cnt2),))
    tok_bysdone = K.p.op("gpsimd", lambda e: e.dma_start(t_dbg, s_flags[:]), (), "bar", 16)
    for g0 in range(2):
        stg_ = stage_sel(
            t_scr_bys[g0 * 512:(g0 + 1) * 512, TOK:FULL].rearrange("(g p) c -> p g c", p=128),
            t_scr_bys[g0 * 512:(g0 + 1) * 512, 0:TOK].rearrange("(g p) c -> p g c", p=128),
            [tok_bysdone] + snd3)
        tX = sel_read(False, stg_)
        snd3.append(K.stg(t_ccb3_in[g0 * 512:(g0 + 1) * 512, :].rearrange("(g p) c -> p g c", p=128),
                          s_bnc[:].rearrange("p (g f) -> p g f", g=4), [tX]))
    tok_agb3 = K.cc("AllGather", ALU.bypass, [t_ccb3_in], [t_ccb3_out], snd3)
    b2.close()

    b3 = contextlib.ExitStack()
    s_yasm = [K.sbuf(f"yasm{i}", [128, TOK], BF16, scope=b3) for i in range(16)]
    s_Wo = [K.sbuf(f"Wo{k}", [128, 1024], BF16, scope=b3) for k in range(14)]
    s_Wo.append(s_bnc2[:, 0:1024])
    s_Wo.append(s_bnc2[:, 1024:2048])
    s_WoL = s_bnc[:].bitcast(F32)[:, 0:1024]
    ya = []
    for g0 in range(2):
        stg_ = stage_sel(
            t_scr_bys[g0 * 512:(g0 + 1) * 512, 0:TOK].rearrange("(g p) c -> p g c", p=128),
            t_scr_bys[g0 * 512:(g0 + 1) * 512, TOK:FULL].rearrange("(g p) c -> p g c", p=128),
            [tok_agb3, tok_bysdone] + ya)
        tX = sel_read(False, stg_)
        for ii in range(4):
            ya.append(K.stg(s_yasm[g0 * 4 + ii][:],
                            s_bnc[:].rearrange("p (g f) -> p g f", g=4)[:, ii, :], [tX], sem="dg2"))
        stg_ = stage_sel(
            t_ccb3_out[1, g0 * 512:(g0 + 1) * 512, :].rearrange("(g p) c -> p g c", p=128),
            t_ccb3_out[0, g0 * 512:(g0 + 1) * 512, :].rearrange("(g p) c -> p g c", p=128),
            ya[-4:])
        tX = sel_read(False, stg_)
        for ii in range(4):
            ya.append(K.stg(s_yasm[8 + g0 * 4 + ii][:],
                            s_bnc[:].rearrange("p (g f) -> p g f", g=4)[:, ii, :], [tX], sem="dg2"))
    wo_prev = None
    wo_toks = []
    for k in range(16):
        tkl = K.ld(s_WoL, d_WoutT[k * 128:(k + 1) * 128, :],
                   [wo_prev] if wo_prev else [ya[-1]])
        wo_prev = K.s(lambda e, k_=k: e.copy(s_Wo[k_][:], s_WoL), [tkl])
        wo_toks.append(wo_prev)
    tok_attn_b = None
    for ch in range(4):
        mm = {}
        for k in range(16):
            for mt in range(NDT):
                mm[mt] = K.t(lambda e, k_=k, m_=mt, c_=ch: e.matmul(
                    ps[m_][:], s_Wo[k_][:, m_ * 128:(m_ + 1) * 128],
                    s_yasm[k_][:, c_ * 512:(c_ + 1) * 512],
                    start=(k_ == 0), stop=(k_ == 15)),
                    [wo_toks[-1], ya[-1]] + ([tok_attn_b] if tok_attn_b else []))
        for mt in range(NDT):
            tok_attn_b = K.v(lambda e, m_=mt, c_=ch: e.tensor_add(
                sx[m_][:, c_ * 512:(c_ + 1) * 512], sx[m_][:, c_ * 512:(c_ + 1) * 512],
                ps[m_][:]), [mm[mt]])
    b3.close()

    tok_ffn2 = ffn([tok_attn_b])

    out = contextlib.ExitStack()
    s_out = K.sbuf("outstage", [128, NDT, 128], scope=out)
    tok_out = []
    for ch in range(16):
        evs = []
        for i in range(NDT):
            par = i % 2
            tt = K.t(lambda e, i_=i, c_=ch, p_=par: e.transpose(
                ps[p_][0:128, 0:128], sx[i_][:, c_ * 128:(c_ + 1) * 128], s_eye[:]),
                [tok_ffn2, tok_eye] + ([evs[-2]] if len(evs) >= 2 else
                                       ([tok_out[-1]] if tok_out else [])))
            evs.append(K.s(lambda e, i_=i, p_=par: e.copy(s_out[:, i_, :], ps[p_][0:128, 0:128]), [tt]))
        tok_out.append(K.stg(o_x[ch * 128:(ch + 1) * 128, :],
                             s_out[:].rearrange("p a c -> p (a c)"), [evs[-1]]))
    out.close()
    tfin = K.stg(t_dbg, s_flags[:], [tok_out[-1]])
    K.p.ops["gpsimd"].append((None, ((("dg", tfin[1]),)), None, 1))

    blk = K.st.enter_context(K.nc.Block())

    @blk.sync
    def _(eng):
        K.p.emit("sync", eng)

    @blk.vector
    def _(eng):
        K.p.emit("vector", eng)

    @blk.scalar
    def _(eng):
        K.p.emit("scalar", eng)

    @blk.tensor
    def _(eng):
        K.p.emit("tensor", eng)

    @blk.gpsimd
    def _(eng):
        K.p.emit("gpsimd", eng)

    K.st.close()
    return K.nc



def _perm_d(a, axis, hh):
    if hh == 0:
        return a
    n = a.shape[axis]
    idx = np.concatenate([np.arange(n // 2, n), np.arange(0, n // 2)])
    return np.take(a, idx, axis=axis)


def make_in_maps(I):
    f32 = lambda a: np.ascontiguousarray(np.asarray(a), dtype=np.float32)
    x = f32(I["L0_attn_a_x"])
    upw = f32(I["L0_ffn_up_weights"])
    dnw = f32(I["L0_ffn_down_weights"])

    bfAu = np.empty((7, 4, 2048), np.float32)
    for s_ in range(7):
        w = upw[s_]
        bfAu[s_] = np.stack([w[:, 0, 0], w[:, 0, 1], w[:, 1, 0], w[:, 1, 1]])
    bfAd = np.empty((7, 4, 2048), np.float32)
    for j, s_ in enumerate(range(6, -1, -1)):
        w = dnw[s_]
        bfAd[j] = np.stack([w[:, 0, 0], w[:, 1, 0], w[:, 0, 1], w[:, 1, 1]])
    bfBu = np.empty((5, 4, 16, 128), np.float32)
    for j, s_ in enumerate(range(7, 12)):
        w = upw[s_].reshape(16, 128, 2, 2)
        bfBu[j] = np.stack([w[:, :, 0, 0], w[:, :, 0, 1], w[:, :, 1, 0], w[:, :, 1, 1]])
    bfBd = np.empty((5, 4, 16, 128), np.float32)
    for j, s_ in enumerate(range(11, 6, -1)):
        w = dnw[s_].reshape(16, 128, 2, 2)
        bfBd[j] = np.stack([w[:, :, 0, 0], w[:, :, 1, 0], w[:, :, 0, 1], w[:, :, 1, 1]])

    Wdbc = f32(I["L0_attn_a_W_deltaBC"])
    Wdt = f32(I["L0_attn_a_W_dt"])
    bdt_a = f32(I["L0_attn_a_b_dt"])
    Alog_a = f32(I["L0_attn_a_A_log"])
    Da = f32(I["L0_attn_a_D_param"])
    Win = f32(I["L0_attn_b_in_proj_w"])
    Wout = f32(I["L0_attn_b_out_proj_w"])
    Wexp = f32(I["L0_ffn_W_exp"])
    Wcon = f32(I["L0_ffn_W_con"])

    in_maps = []
    for c in range(8):
        b, hh = c // 2, c % 2
        m = {}
        m["flags"] = np.array([1 - hh, hh], np.int32)
        hm = np.zeros((128, 2), np.float32)
        hm[:, 0] = 1.0 - hh
        hm[:, 1] = float(hh)
        m["hmask"] = hm
        m["xT"] = np.ascontiguousarray(_perm_d(x[b, hh * TOK:(hh + 1) * TOK, :].T, 0, hh))
        WdbcTp = np.zeros((1024, 128), np.float32)
        wt_ = _perm_d(Wdbc.T, 0, hh)
        WdbcTp[:, 0:64] = wt_[:, 0:64]
        WdbcTp[:, 64:80] = wt_[:, 64:80]
        WdbcTp[:, 96:112] = wt_[:, 80:96]
        m["WdbcT"] = WdbcTp
        m["WdtT"] = np.ascontiguousarray(_perm_d(Wdt.T, 1, hh))
        m["bdt_a"] = np.ascontiguousarray(_perm_d(bdt_a, 0, hh).reshape(8, 128).T)
        half_a = slice(512 * hh, 512 * hh + 512)
        m["Alog_a"] = Alog_a[half_a].copy()
        m["Da"] = np.ascontiguousarray(_perm_d(Da, 0, hh).reshape(8, 128).T)
        di_idx = np.concatenate([np.arange(1024 * hh, 1024 * hh + 1024),
                                 np.arange(1024 * (1 - hh), 1024 * (1 - hh) + 1024)])
        WinT = _perm_d(Win.T, 0, hh)
        xin_cols = WinT[:, 0:2048][:, di_idx]
        z_cols = WinT[:, 2048:4096][:, di_idx]
        m["WinT"] = np.ascontiguousarray(np.concatenate([xin_cols, z_cols], axis=1))
        WoutT = _perm_d(Wout.T, 1, hh)
        m["WoutT"] = np.ascontiguousarray(WoutT[di_idx, :])
        half_b = slice(1024 * hh, 1024 * hh + 1024)
        for d, tag in (("fwd", "f"), ("bwd", "b")):
            m[f"cw{tag}"] = f32(I[f"L0_attn_b_{d}_conv_w"])[half_b, 0, :].copy()
            m[f"cb{tag}"] = np.ascontiguousarray(f32(I[f"L0_attn_b_{d}_conv_b"])[half_b].reshape(8, 128).T)
            xpt_ = f32(I[f"L0_attn_b_{d}_x_proj_w"]).T[half_b, :]
            xptp = np.zeros((1024, 128), np.float32)
            xptp[:, 0:64] = xpt_[:, 0:64]
            xptp[:, 64:80] = xpt_[:, 64:80]
            xptp[:, 96:112] = xpt_[:, 80:96]
            m[f"xp{tag}T"] = xptp
            m[f"dtw{tag}T"] = np.ascontiguousarray(f32(I[f"L0_attn_b_{d}_dt_w"]).T[:, half_b])
            m[f"dtb{tag}"] = np.ascontiguousarray(f32(I[f"L0_attn_b_{d}_dt_b"])[half_b].reshape(8, 128).T)
            m[f"Alog{tag}"] = f32(I[f"L0_attn_b_{d}_A_log"])[half_b].copy()
            m[f"Dp{tag}"] = np.ascontiguousarray(f32(I[f"L0_attn_b_{d}_D"])[half_b].reshape(8, 128).T)
        m["WexpT"] = np.ascontiguousarray(_perm_d(Wexp.T, 1, hh))
        m["Wcon"] = np.ascontiguousarray(_perm_d(Wcon, 1, hh))
        m["bexp"] = np.ascontiguousarray(f32(I["L0_ffn_b_exp"]).reshape(32, 128).T)
        m["upbias"] = np.ascontiguousarray(f32(I["L0_ffn_up_bias"]).reshape(32, 128).T)
        m["bcon"] = np.ascontiguousarray(_perm_d(f32(I["L0_ffn_b_con"]), 0, hh).reshape(8, 128).T)
        m["downbias"] = np.ascontiguousarray(_perm_d(f32(I["L0_ffn_down_bias"]), 0, hh).reshape(8, 128).T)
        m["eye"] = np.eye(128, dtype=np.float32)
        m["bfAu"] = bfAu
        m["bfAd"] = bfAd
        m["bfBu"] = np.ascontiguousarray(bfBu.transpose(3, 0, 1, 2))
        m["bfBd"] = np.ascontiguousarray(bfBd.transpose(3, 0, 1, 2))
        in_maps.append(m)
    return in_maps


_CACHE = {}


def kernel(**inputs):
    in_maps = make_in_maps(inputs)
    if "nc" not in _CACHE:
        _CACHE["nc"] = build_program()
    nc = _CACHE["nc"]
    res = run_bass_kernel_spmd(nc, in_maps, core_ids=list(range(8)))
    out = np.empty((4, FULL, 1024), np.float32)
    for c in range(8):
        b, hh = c // 2, c % 2
        r = np.asarray(res.results[c]["out_xT"], dtype=np.float32)
        if hh == 1:
            r = np.concatenate([r[:, 512:], r[:, :512]], axis=1)
        out[b, hh * TOK:(hh + 1) * TOK, :] = r
    return out


# version 19
# speedup vs baseline: 1.0560x; 1.0560x over previous
import contextlib
import numpy as np
import ml_dtypes
import concourse.bass as bass
import concourse.mybir as mybir
from concourse.bass_utils import run_bass_kernel_spmd

F32 = mybir.dt.float32
BF16 = mybir.dt.bfloat16
I32 = mybir.dt.int32
ALU = mybir.AluOpType
AF = mybir.ActivationFunctionType
AX = mybir.AxisListType

TOK = 2048
FULL = 4096
NDT = 8
S = 16
TS = 128
FT = S * TS
NJ = FULL // TS
EPS = 1e-5
GROUPS = [[0, 1], [2, 3], [4, 5], [6, 7]]
ENGS = ("sync", "vector", "scalar", "tensor", "gpsimd")


class Plan:
    def __init__(self):
        self.ops = {e: [] for e in ENGS}
        self.cnt = {n: 0 for n in ("dy", "dy0", "dy1", "dg", "dg2", "sel1", "sel2", "rep", "bar", "v", "s", "t", "g", "c")}
        self.sems = {}

    def op(self, eng, fn, waits=(), inc=None, amt=1):
        tok = None
        if inc is not None:
            self.cnt[inc] += amt
            tok = (inc, self.cnt[inc])
        self.ops[eng].append((fn, tuple(w for w in waits if w is not None), inc, amt))
        return tok

    def emit(self, eng_name, eng):
        last = {}
        for fn, waits, inc, amt in self.ops[eng_name]:
            for sn, vv in waits:
                if last.get(sn, 0) >= vv:
                    continue
                eng.wait_ge(self.sems[sn], vv)
                last[sn] = vv
            if fn is None:
                continue
            ins = fn(eng)
            if inc is not None:
                ins.then_inc(self.sems[inc], amt)


class Kern:
    def __init__(self):
        nc = bass.Bass("TRN2", target_bir_lowering=False, debug=False)
        nc.detect_race_conditions = False
        self.nc = nc
        self.st = contextlib.ExitStack()
        self.p = Plan()
        self.regs = {}

    def din(self, n, sh, dt=F32):
        return self.nc.dram_tensor(n, sh, dt, kind="ExternalInput").ap()

    def dout(self, n, sh, dt=F32):
        return self.nc.dram_tensor(n, sh, dt, kind="ExternalOutput").ap()

    def dtmp(self, n, sh, dt=F32):
        return self.nc.dram_tensor(n, sh, dt, kind="Internal").ap()

    def sbuf(self, n, sh, dt=F32, scope=None):
        return (scope or self.st).enter_context(self.nc.sbuf_tensor(n, sh, dt))

    def psum(self, n, sh, dt=F32):
        return self.st.enter_context(self.nc.psum_tensor(n, sh, dt))

    def semaphore(self, n):
        h = self.st.enter_context(self.nc.semaphore(n))
        self.p.sems[n] = h
        return h

    def ld(self, out, in_, waits=(), sem="dy"):
        return self.p.op("sync", lambda e, o=out, i=in_: e.dma_start(o, i), waits, sem, 16)

    def stg(self, out, in_, waits=(), sem="dg"):
        return self.p.op("gpsimd", lambda e, o=out, i=in_: e.dma_start(o, i), waits, sem, 16)

    def cstg(self, out, in_, flag, waits=()):
        return self.p.op("gpsimd",
                         lambda e, o=out, i=in_, f=flag: e.dma_start(o, i, cond=self.regs[f]),
                         waits, "dg", 16)

    def v(self, fn, waits=()):
        return self.p.op("vector", fn, waits, "v", 1)

    def s(self, fn, waits=()):
        return self.p.op("scalar", fn, waits, "s", 1)

    def t(self, fn, waits=()):
        return self.p.op("tensor", fn, waits, "t", 1)

    def g(self, fn, waits=()):
        return self.p.op("gpsimd", fn, waits, "g", 1)

    def cc(self, kind, op, ins, outs, waits=()):
        return self.p.op(
            "gpsimd",
            lambda e, k=kind, o=op, i=ins, u=outs: e.collective_compute(
                k, o, replica_groups=GROUPS, ins=i, outs=u),
            waits, "c", 1)


def sm(ap, s_=S):
    return ap.rearrange("p (s t) -> p s t", s=s_)


def build_program():
    K = Kern()
    for sn in ("dy", "dy0", "dy1", "dg", "dg2", "sel1", "sel2", "rep", "bar", "v", "s", "t", "g", "c"):
        K.semaphore(sn)

    d_xT = K.din("xT", [1024, TOK])
    d_flags = K.din("flags", [2], I32)
    d_hmask = K.din("hmask", [128, 2])
    d_WdbcT = K.din("WdbcT", [1024, 128])
    d_WdtT = K.din("WdtT", [64, 1024])
    d_bdt_a = K.din("bdt_a", [128, 8])
    d_Alog_a = K.din("Alog_a", [512, S])
    d_Da = K.din("Da", [128, 8])
    d_WinT = K.din("WinT", [1024, 4096])
    d_WoutT = K.din("WoutT", [2048, 1024])
    d_cw = {d: K.din(f"cw{d}", [1024, 4]) for d in "fb"}
    d_cb = {d: K.din(f"cb{d}", [128, 8]) for d in "fb"}
    d_xpT = {d: K.din(f"xp{d}T", [1024, 128]) for d in "fb"}
    d_dtwT = {d: K.din(f"dtw{d}T", [64, 1024]) for d in "fb"}
    d_dtb = {d: K.din(f"dtb{d}", [128, 8]) for d in "fb"}
    d_Alog = {d: K.din(f"Alog{d}", [1024, S]) for d in "fb"}
    d_Dp = {d: K.din(f"Dp{d}", [128, 8]) for d in "fb"}
    d_WexpT = K.din("WexpT", [4096, 1024])
    d_Wcon = K.din("Wcon", [4096, 1024])
    d_bexp = K.din("bexp", [128, 32])
    d_upbias = K.din("upbias", [128, 32])
    d_bcon = K.din("bcon", [128, 8])
    d_downbias = K.din("downbias", [128, 8])
    d_eye = K.din("eye", [128, 128])
    d_bfA = {c: K.din(f"bfA{c}", [7, 4, 2048]) for c in "ud"}
    d_bfB = {c: K.din(f"bfB{c}", [128, 5, 4, 16]) for c in "ud"}
    o_x = K.dout("out_xT", [TOK, 1024])

    t_scr_ad = K.dtmp("scr_ad", [512, TOK], BF16)
    t_scr_aw = K.dtmp("scr_aw", [512, TOK], BF16)
    t_cc1_in = K.dtmp("cc1_in", [1056, TOK], BF16)
    t_cc1_out = K.dtmp("cc1_out", [2, 1056, TOK], BF16)
    t_af = K.dtmp("scr_af", [512, FULL], BF16)
    t_wf = K.dtmp("scr_wf", [512, FULL], BF16)
    t_bcf = K.dtmp("scr_bcf", [32, FULL], BF16)
    t_scr_ay = K.dtmp("scr_ay", [512, FULL], BF16)
    t_cc2_in = K.dtmp("cc2_in", [512, TOK], BF16)
    t_cc2_out = K.dtmp("cc2_out", [2, 512, TOK], BF16)
    t_scr_x = K.dtmp("scr_x", [1024, TOK], F32)
    t_scr_M = K.dtmp("scr_M", [1024, TOK], BF16)
    t_wpe = K.dtmp("scr_wpe", [1024, 4096], BF16)
    t_wpc = K.dtmp("scr_wpc", [4096, 1024], BF16)
    t_scr_bxin = K.dtmp("scr_bxin", [1024, TOK], BF16)
    t_scr_bz = K.dtmp("scr_bz", [1024, TOK], BF16)
    t_dt = K.dtmp("scr_dt", [1024, FULL], BF16)
    t_dlr = K.dtmp("scr_dlr", [2, 64, FULL], BF16)
    t_bcb = K.dtmp("scr_bcb", [2, 32, FULL], BF16)
    t_yf = K.dtmp("scr_yf", [1024, FULL], BF16)
    t_ccb1_in = K.dtmp("ccb1_in", [2048, TOK], BF16)
    t_ccb1_out = K.dtmp("ccb1_out", [2, 2048, TOK], BF16)
    t_sz = K.dtmp("scr_sz", [1024, FULL], BF16)
    t_xc = {d: K.dtmp(f"scr_xc{d}", [1024, FULL], BF16) for d in "fb"}
    t_ccb2_in = K.dtmp("ccb2_in", [192, FULL], F32)
    t_ccb2_out = K.dtmp("ccb2_out", [192, FULL], F32)
    t_scr_bys = K.dtmp("scr_bys", [1024, FULL], BF16)
    t_ccb3_in = K.dtmp("ccb3_in", [1024, TOK], BF16)
    t_ccb3_out = K.dtmp("ccb3_out", [2, 1024, TOK], BF16)

    sx = [K.sbuf(f"x{i}", [128, TOK]) for i in range(NDT)]
    s_ones = K.sbuf("ones", [128, 1])
    s_eye = K.sbuf("eyeS", [128, 128])
    s_eyeb = K.sbuf("eyeSb", [128, 128], BF16)
    s_flags = K.sbuf("flagsS", [1, 2], I32)
    s_WdbcT = K.sbuf("WdbcT_s", [128, NDT, 128], BF16)
    s_WdtT = K.sbuf("WdtT_s", [64, 1024], BF16)
    s_bdt_a = K.sbuf("bdt_a_s", [128, NDT])
    s_Aa = K.sbuf("Aa_s", [128, 4, S])
    s_Da = K.sbuf("Da_s", [128, NDT])
    s_cw = {d: K.sbuf(f"cw{d}_s", [128, NDT, 4]) for d in "fb"}
    s_cb = {d: K.sbuf(f"cb{d}_s", [128, NDT]) for d in "fb"}
    s_xpT = {d: K.sbuf(f"xp{d}T_s", [128, NDT, 128], BF16) for d in "fb"}
    s_dtwT = {d: K.sbuf(f"dtw{d}T_s", [64, 1024], BF16) for d in "fb"}
    s_dtb = {d: K.sbuf(f"dtb{d}_s", [128, NDT]) for d in "fb"}
    s_Ab = {d: K.sbuf(f"Ab{d}_s", [128, NDT, S]) for d in "fb"}
    s_Db = {d: K.sbuf(f"Db{d}_s", [128, NDT]) for d in "fb"}
    s_bpu = K.sbuf("bpu_s", [128, 32])
    s_bpc = K.sbuf("bpc_s", [128, NDT])
    s_bfB = {c: K.sbuf(f"bfB{c}_s", [128, 5, 4, 16]) for c in "ud"}

    ps = [K.psum(f"ps{i}", [128, 512]) for i in range(8)]

    def _selview(base, like_ap):
        sh = list(like_ap.shape)
        v = base
        if len(sh) == 2:
            return v[0:sh[0], 0:sh[1]]
        if len(sh) == 3:
            return v[:].rearrange("p (g f) -> p g f", g=sh[1])[0:sh[0], :, 0:sh[2]]
        raise ValueError(sh)

    def _selfree(ap):
        sh = list(ap.shape)
        n = 1
        for x in sh[1:]:
            n *= x
        return sh[0], n

    def stage_sel(c0_ap, c1_ap, pre):
        rows, free = _selfree(c0_ap)
        t1 = K.ld(_selview(s_bnc, c0_ap), c0_ap, pre, sem="sel1")
        t2 = K.stg(_selview(s_bnc2, c1_ap), c1_ap, pre, sem="sel2")
        return [t1, t2, (rows, free)]

    def sel_read(swap, st_):
        t1, t2, (rows, free) = st_
        a_, b_ = (0, 1) if not swap else (1, 0)
        t3 = K.v(lambda e, r=rows, f=free, a=a_: e.tensor_scalar(
            s_bnc[0:r, 0:f], s_bnc[0:r, 0:f], s_hm[0:r, a:a + 1], None, ALU.mult),
            [t1, t2, tok_regs])
        return K.v(lambda e, r=rows, f=free, b=b_: e.scalar_tensor_tensor(
            s_bnc[0:r, 0:f], s_bnc2[0:r, 0:f], s_hm[0:r, b:b + 1], s_bnc[0:r, 0:f],
            ALU.mult, ALU.add), [t3])

    def select1(dst_fn, c0_ap, c1_ap, pre):
        st_ = stage_sel(c0_ap, c1_ap, pre)
        t4 = sel_read(False, st_)
        o, i = dst_fn(s_bnc)
        return K.stg(o, i, [t4], sem="dg2")

    tok_x = [K.ld(sx[i][:], d_xT.rearrange("(a p) t -> a p t", p=128)[i]) for i in range(NDT)]
    tok_fl = K.ld(s_flags[:], d_flags.unsqueeze(0))
    tok_eye = K.ld(s_eye[:], d_eye)
    tok_eyeb = K.s(lambda e: e.copy(s_eyeb[:], s_eye[:]), [tok_eye])
    tok_ones = K.v(lambda e: e.memset(s_ones[:], 1.0))
    s_eps = K.sbuf("epsS", [1, 1])
    s_bnc = K.sbuf("bounceS", [128, 8192], BF16)
    s_bnc2 = K.sbuf("bounceS2", [128, 8192], BF16)
    s_hm = K.sbuf("hmaskS", [128, 2])
    tok_eps = K.v(lambda e: e.memset(s_eps[:], EPS))

    t_dbg = K.dtmp("t_dbg", [1, 2], I32)
    tok_regs = K.ld(s_hm[:], d_hmask)

    def drain_dg(pre=()):
        cnt = K.p.cnt["dg"]
        K.p.op("gpsimd", None, tuple(pre) + (("dg", cnt),))
        return K.p.op("gpsimd", lambda e: e.dma_start(t_dbg, s_flags[:]), (), "bar", 16)

    boot = contextlib.ExitStack()
    s_bs = K.sbuf("bootstage", [128, 4096], scope=boot)
    prev_cast = [None, None]

    def load_cast(dst_ap, src_ap, rows, cols):
        tk = K.ld(s_bs[0:rows, 0:cols], src_ap, [prev_cast[0]])
        prev_cast[0] = K.s(lambda e, d=dst_ap, r=rows, c=cols: e.copy(d, s_bs[0:r, 0:c]), [tk])
        return prev_cast[0]

    load_cast(s_WdbcT[:], d_WdbcT.rearrange("(a p) c -> p a c", p=128), 128, NDT * 128)
    load_cast(s_WdtT[:], d_WdtT, 64, 1024)
    for d in "fb":
        load_cast(s_xpT[d][:], d_xpT[d].rearrange("(a p) c -> p a c", p=128), 128, NDT * 128)
        load_cast(s_dtwT[d][:], d_dtwT[d], 64, 1024)
    tk = K.ld(s_bs[0:128, 0:4 * S], d_Alog_a.rearrange("(a p) s -> p a s", p=128),
              [prev_cast[0], prev_cast[1]])
    tk = K.s(lambda e: e.activation(s_bs[0:128, 128:128 + 4 * S], s_bs[0:128, 0:4 * S], AF.Exp), [tk])
    prev_cast[0] = K.v(lambda e: e.tensor_scalar_mul(
        s_Aa[:].rearrange("p a s -> p (a s)"), s_bs[0:128, 128:128 + 4 * S], -1.0), [tk])
    for d in "fb":
        tk = K.ld(s_bs[0:128, 0:NDT * S], d_Alog[d].rearrange("(a p) s -> p a s", p=128),
                  [prev_cast[0], prev_cast[1]])
        tk = K.s(lambda e: e.activation(s_bs[0:128, 256:256 + NDT * S], s_bs[0:128, 0:NDT * S], AF.Exp), [tk])
        prev_cast[0] = K.v(lambda e, d_=d: e.tensor_scalar_mul(
            s_Ab[d_][:].rearrange("p a s -> p (a s)"), s_bs[0:128, 256:256 + NDT * S], -1.0), [tk])

    K.ld(s_bdt_a[:], d_bdt_a)
    K.ld(s_Da[:], d_Da)
    for d in "fb":
        K.ld(s_cw[d][:], d_cw[d].rearrange("(a p) k -> p a k", p=128))
        K.ld(s_cb[d][:], d_cb[d])
        K.ld(s_dtb[d][:], d_dtb[d])
        K.ld(s_Db[d][:], d_Dp[d])
    tok_bc1 = K.ld(s_bpc[:], d_bcon)
    tok_db = K.ld(s_bs[0:128, 512:512 + NDT], d_downbias,
                  [tok_bc1, prev_cast[0], prev_cast[1]])
    tok_bpc = K.v(lambda e: e.tensor_add(s_bpc[:], s_bpc[:], s_bs[0:128, 512:512 + NDT]),
                  [tok_db, tok_bc1])
    for c in "ud":
        K.ld(s_bfB[c][:], d_bfB[c])

    boot.close()

    comp = contextlib.ExitStack()
    sZbU = K.sbuf("ZcbU", [128, 4096], BF16, scope=comp)
    sZbD = K.sbuf("ZcbD", [128, 4096], BF16, scope=comp)
    sbvec = K.sbuf("bvecc", [128, 32], scope=comp)
    sbvec2 = K.sbuf("bvecc2", [128, 32], scope=comp)
    zbsc = contextlib.ExitStack()
    sZ = K.sbuf("Zc", [128, 4096], scope=zbsc)
    sZ2 = K.sbuf("Zc2", [128, 4096], scope=zbsc)
    sT1 = K.sbuf("Tc1", [128, 2048], scope=zbsc)
    sWrep = K.sbuf("Wrep", [128, 2, 2048], scope=zbsc)
    sAvec = K.sbuf("Avec", [28, 2048], scope=zbsc)

    def build_Z(chain, pre):
        tka = K.ld(sAvec[:], d_bfA[chain].rearrange("a b c -> (a b) c"), pre)
        tz = K.stg(sZ[:].rearrange("p (q i) -> p q i", q=32),
                   s_eye[:].unsqueeze(1).broadcast_to([128, 32, 128]), pre + [tok_eye], sem="rep")
        cur, nxt = sZ, sZ2
        tok = None
        for st_ in range(7):
            sig = 1 << st_
            nh = 64 // sig
            def pos(buf, two, nh=nh, sig=sig):
                return buf[:].rearrange("p (q h two l) -> p q h two l",
                                        q=32, h=nh, two=2)[:, :, :, two, :]

            def wv(quad, nh=nh, sig=sig):
                return sWrep[:, quad, :].rearrange("p (q h l) -> p q h l", q=32, h=nh)

            c_, n_ = cur, nxt
            trep = None
            for qd in range(2):
                trep = K.stg(
                    sWrep[:, qd, :],
                    sAvec[st_ * 4 + qd:st_ * 4 + qd + 1, :].unsqueeze(1)
                    .broadcast_to([1, 128, 2048]),
                    ([tka, tz] if tok is None else [tok]) if qd == 0 else [trep], sem="rep")
            K.v(lambda e, c=c_, n=n_, pos=pos, wv=wv: e.tensor_tensor(
                pos(n, 0), pos(c, 0), wv(0), ALU.mult),
                ([trep, tz, tka] if tok is None else [trep, tok]))
            K.v(lambda e, c=c_, nh=nh, pos=pos, wv=wv: e.tensor_tensor(
                sT1[:].rearrange("p (q h l) -> p q h l", q=32, h=nh), pos(c, 1), wv(1), ALU.mult))
            tok0 = K.v(lambda e, n=n_, nh=nh, pos=pos: e.tensor_tensor(
                pos(n, 0), pos(n, 0),
                sT1[:].rearrange("p (q h l) -> p q h l", q=32, h=nh), ALU.add))
            trep = None
            for qd in range(2):
                trep = K.stg(
                    sWrep[:, qd, :],
                    sAvec[st_ * 4 + 2 + qd:st_ * 4 + 3 + qd, :].unsqueeze(1)
                    .broadcast_to([1, 128, 2048]),
                    [tok0] if qd == 0 else [trep], sem="rep")
            K.v(lambda e, c=c_, n=n_, pos=pos, wv=wv: e.tensor_tensor(
                pos(n, 1), pos(c, 0), wv(0), ALU.mult), [trep])
            K.v(lambda e, c=c_, nh=nh, pos=pos, wv=wv: e.tensor_tensor(
                sT1[:].rearrange("p (q h l) -> p q h l", q=32, h=nh), pos(c, 1), wv(1), ALU.mult))
            tok = K.v(lambda e, n=n_, nh=nh, pos=pos: e.tensor_tensor(
                pos(n, 1), pos(n, 1),
                sT1[:].rearrange("p (q h l) -> p q h l", q=32, h=nh), ALU.add))
            cur, nxt = nxt, cur
        return cur, tok

    def bstep_pairs(chain, cur, get_a, width, pre):
        tok = None
        for st_ in range(5):
            sigq = 1 << st_
            for hi in range(16 // sigq):
                for lo in range(sigq):
                    pr = hi * sigq + lo
                    qa = hi * 2 * sigq + lo
                    qb = qa + sigq

                    def w(quad, p_=pr, s__=st_, c_=chain):
                        return s_bfB[c_][:, s__, quad, p_:p_ + 1]
                    deps = (pre if tok is None else [tok])
                    K.v(lambda e, a=get_a(cur, qa), w=w: e.tensor_scalar(
                        sTb[0:128, 0:width], a, w(0), None, ALU.mult), deps)
                    t2 = K.v(lambda e, b=get_a(cur, qb), w=w: e.scalar_tensor_tensor(
                        sTb[0:128, 0:width], b, w(1), sTb[0:128, 0:width], ALU.mult, ALU.add))
                    K.v(lambda e, a=get_a(cur, qa), w=w: e.tensor_scalar(
                        sTb2[0:128, 0:width], a, w(2), None, ALU.mult))
                    t4 = K.v(lambda e, b=get_a(cur, qb), w=w: e.scalar_tensor_tensor(
                        sTb2[0:128, 0:width], b, w(3), sTb2[0:128, 0:width], ALU.mult, ALU.add))
                    t5 = K.v(lambda e, a=get_a(cur, qa): e.tensor_copy(a, sTb[0:128, 0:width]), [t2])
                    tok = K.v(lambda e, b=get_a(cur, qb): e.tensor_copy(b, sTb2[0:128, 0:width]), [t4, t5])
        return cur, tok

    Zu, tokZu = build_Z("u", [tok_ones, tok_bpc])
    tokZub = K.v(lambda e, c=Zu: e.tensor_copy(sZbU[:], c[:]), [tokZu])
    tkb = K.ld(sbvec[:], d_bexp)
    bt = None
    for q in range(32):
        tmm = K.t(lambda e, q_=q: e.matmul(
            ps[3][0:128, 0:1], Zu[:, q_ * 128:(q_ + 1) * 128], sbvec[:, q_:q_ + 1],
            start=True, stop=True), [tkb, tokZu] + ([bt] if bt else []))
        bt = K.s(lambda e, q_=q: e.copy(sbvec2[:, q_:q_ + 1], ps[3][0:128, 0:1]), [tmm])
    Zd, tokZd = build_Z("d", [tokZub, bt])
    tokZdb = K.v(lambda e, c=Zd: e.tensor_copy(sZbD[:], c[:]), [tokZd])
    zbsc.close()

    sWq = [K.sbuf(f"Wq{i}", [128, 1024], BF16, scope=comp) for i in range(32)]
    sTb = K.sbuf("Tcb", [128, 1024], BF16, scope=comp)
    sTb2 = K.sbuf("Tcb2", [128, 1024], BF16, scope=comp)
    sWl = K.sbuf("Wload", [128, 1024], scope=comp)
    sWlb = K.sbuf("Wloadb", [128, 1024], BF16, scope=comp)
    sEv = K.sbuf("Evac", [128, 512], BF16, scope=comp)

    tprev = tokZdb
    for q in range(32):
        tkl = K.ld(sWl[:], d_WexpT[q * 128:(q + 1) * 128, :], [tprev, tokZdb])
        tkl = K.s(lambda e: e.copy(sWlb[:], sWl[:]), [tkl])
        evs = []
        for nch in range(2):
            tmm = K.t(lambda e, q_=q, n_=nch: e.matmul(
                ps[n_][:], sZbU[:, q_ * 128:(q_ + 1) * 128],
                sWlb[:, n_ * 512:(n_ + 1) * 512],
                start=True, stop=True), [tkl, tokZub])
            evs.append(K.s(lambda e, q_=q, n_=nch: e.copy(
                sWq[q_][:, n_ * 512:(n_ + 1) * 512], ps[n_][:]), [tmm]))
        tprev = evs[-1]
    Wu, tok_bu = bstep_pairs("u", sWq, lambda L, q: L[q][:], 1024, [tprev])
    tok_wpe = []
    for q in range(32):
        for mt in range(NDT):
            tt = K.t(lambda e, q_=q, m_=mt: e.transpose(
                ps[2][0:128, 0:64].bitcast(BF16), Wu[q_][:, m_ * 128:(m_ + 1) * 128], s_eyeb[:]),
                [tok_bu, tok_eyeb] + ([tok_wpe[-1]] if tok_wpe else []))
            te = K.s(lambda e: e.copy(sEv[0:128, 0:128], ps[2][0:128, 0:64].bitcast(BF16)), [tt])
            tok_wpe.append(K.stg(t_wpe[mt * 128:(mt + 1) * 128, q * 128:(q + 1) * 128],
                                 sEv[0:128, 0:128], [te]))
    bias_buf, btok = bstep_pairs("u", sbvec2, lambda L, q: L[:, q:q + 1], 1, [bt, tok_bu])
    tk_upb = K.ld(sWl[0:128, 0:32], d_upbias)
    tok_bpu = K.v(lambda e, bb=bias_buf: e.tensor_add(s_bpu[:], bb[:], sWl[0:128, 0:32]),
                  [btok, tk_upb])

    tprev = None
    _dpre = [tok_bpu, tok_wpe[-1]]
    for q in range(32):
        tkl = K.ld(sWl[:], d_Wcon[q * 128:(q + 1) * 128, :], ([tprev] if tprev else _dpre))
        tprev = K.s(lambda e, q_=q: e.copy(sWq[q_][:], sWl[:]), [tkl] + ([] if tprev else _dpre))
    Wd1, tok_bd = bstep_pairs("d", sWq, lambda L, q: L[q][:], 1024, [tprev])
    tok_wpc = []
    for q in range(32):
        for nch in range(2):
            tmm = K.t(lambda e, q_=q, n_=nch: e.matmul(
                ps[4][:], sZbD[:, q_ * 128:(q_ + 1) * 128], Wd1[q_][:, n_ * 512:(n_ + 1) * 512],
                start=True, stop=True), [tok_bd, tokZdb] + ([tok_wpc[-1]] if tok_wpc else []))
            te = K.s(lambda e: e.copy(sEv[:], ps[4][:]), [tmm])
            tok_wpc.append(K.stg(t_wpc[q * 128:(q + 1) * 128, nch * 512:(nch + 1) * 512],
                                 sEv[:], [te]))
    comp.close()
    tok_compose = tok_wpc[-1]

    ln_idx = [0]

    def layernorm_make(pre, dst_fn):
        ln_idx[0] += 1
        nm = ln_idx[0]
        sc = contextlib.ExitStack()
        stg_ = K.sbuf(f"lnstage{nm}", [128, TOK], scope=sc)
        arep = K.sbuf(f"lnarep{nm}", [128, TOK], scope=sc)
        brep = K.sbuf(f"lnbrep{nm}", [128, TOK], scope=sc)
        s_stat = K.sbuf(f"lnstat{nm}", [1, 2, TOK], scope=sc)
        ev = []
        for ch in range(4):
            tmm = None
            for i in range(NDT):
                tmm = K.t(lambda e, i_=i, c_=ch: e.matmul(
                    ps[4 + c_][0:1, :], s_ones[:], sx[i_][:, c_ * 512:(c_ + 1) * 512],
                    start=(i_ == 0), stop=(i_ == NDT - 1)), pre + [tok_ones])
            ev.append(K.s(lambda e, c_=ch: e.copy(s_stat[0:1, 0, c_ * 512:(c_ + 1) * 512],
                                                  ps[4 + c_][0:1, :]), [tmm]))
        tsq = None
        last_mm = {}
        for i in range(NDT):
            tsq = K.s(lambda e, i_=i: e.activation(stg_[:], sx[i_][:], AF.Square),
                      pre + ([last_mm.get(3)] if i > 0 else [ev[-1]]))
            for ch in range(4):
                last_mm[ch] = K.t(lambda e, i_=i, c_=ch: e.matmul(
                    ps[c_][0:1, :], s_ones[:], stg_[:, c_ * 512:(c_ + 1) * 512],
                    start=(i_ == 0), stop=(i_ == NDT - 1)), [tsq])
        for ch in range(4):
            ev.append(K.s(lambda e, c_=ch: e.copy(s_stat[0:1, 1, c_ * 512:(c_ + 1) * 512],
                                                  ps[c_][0:1, :]), [last_mm[ch]]))
        t1 = K.v(lambda e: e.tensor_scalar_mul(s_stat[0:1, 0, :], s_stat[0:1, 0, :], 1.0 / 1024), [ev[-1], ev[3]])
        t1 = K.v(lambda e: e.tensor_scalar_mul(s_stat[0:1, 1, :], s_stat[0:1, 1, :], 1.0 / 1024), [t1])
        t1 = K.v(lambda e: e.scalar_tensor_tensor(
            arep[0:1, :], s_stat[0:1, 0, :], -1.0, s_stat[0:1, 0, :], ALU.mult, ALU.mult), [t1])
        t1 = K.v(lambda e: e.tensor_add(s_stat[0:1, 1, :], s_stat[0:1, 1, :], arep[0:1, :]), [t1])
        t1 = K.s(lambda e: e.activation(s_stat[0:1, 1, :], s_stat[0:1, 1, :], AF.Sqrt, bias=s_eps[0:1, 0:1]), [t1, tok_eps])
        t1 = K.v(lambda e: e.reciprocal(s_stat[0:1, 1, :], s_stat[0:1, 1, :]), [t1])
        t1 = K.v(lambda e: e.tensor_tensor(s_stat[0:1, 0, :], s_stat[0:1, 0, :], s_stat[0:1, 1, :], ALU.mult), [t1])
        t1 = K.v(lambda e: e.tensor_scalar_mul(s_stat[0:1, 0, :], s_stat[0:1, 0, :], -1.0), [t1])
        tr1 = K.stg(arep[:], s_stat[0:1, 1, :].unsqueeze(1).broadcast_to([1, 128, TOK]), [t1], sem="rep")
        tr2 = K.stg(brep[:], s_stat[0:1, 0, :].unsqueeze(1).broadcast_to([1, 128, TOK]), [t1], sem="rep")
        out_toks = []
        for i in range(NDT):
            tn1 = K.v(lambda e, i_=i: e.tensor_tensor(stg_[:], sx[i_][:], arep[:], ALU.mult),
                      [tr1, tr2] + ([out_toks[-1]] if out_toks else []))
            out_toks.append(dst_fn(i, stg_, brep, tn1))
        sc.close()
        return out_toks[-1]

    a1 = contextlib.ExitStack()
    sM = [K.sbuf(f"Ma{i}", [128, TOK], BF16, scope=a1) for i in range(NDT)]
    s_dlr = K.sbuf("dlr_a", [64, TOK], BF16, scope=a1)
    s_bB = K.sbuf("bB_a", [16, TOK], BF16, scope=a1)
    s_bC = K.sbuf("bC_a", [16, TOK], BF16, scope=a1)

    def dst_a(i, stg_, brep, tn1):
        t2 = K.v(lambda e, i_=i: e.tensor_tensor(sM[i_][:], stg_[:], brep[:], ALU.add), [tn1])
        return K.v(lambda e, i_=i: e.scalar_tensor_tensor(
            sx[i_][:], sM[i_][:], s_Da[:, i_:i_ + 1], sx[i_][:], ALU.mult, ALU.add), [t2])

    tok_ln1 = layernorm_make(tok_x + [tok_compose, prev_cast[0]], dst_a)

    dbc_ev = []
    for ch in range(4):
        tmm = None
        for i in range(NDT):
            tmm = K.t(lambda e, i_=i, c_=ch: e.matmul(
                ps[c_][:], s_WdbcT[:, i_, :], sM[i_][:, c_ * 512:(c_ + 1) * 512],
                start=(i_ == 0), stop=(i_ == NDT - 1)), [tok_ln1])
        dbc_ev.append(K.s(lambda e, c_=ch: e.copy(s_dlr[:, c_ * 512:(c_ + 1) * 512], ps[c_][0:64, :]), [tmm]))
        dbc_ev.append(K.s(lambda e, c_=ch: e.copy(s_bB[:, c_ * 512:(c_ + 1) * 512], ps[c_][64:80, :]), [tmm]))
        dbc_ev.append(K.s(lambda e, c_=ch: e.copy(s_bC[:, c_ * 512:(c_ + 1) * 512], ps[c_][96:112, :]), [tmm]))

    a1b = contextlib.ExitStack()
    s_dloc = [K.sbuf(f"dloc{i}", [128, TOK], BF16, scope=a1b) for i in range(2)]
    s_wloc = [K.sbuf(f"wloc{i}", [128, TOK], BF16, scope=a1b) for i in range(2)]
    s_dstg = K.sbuf("dstg_a", [128, 512], scope=a1b)
    dw_last = None
    snd = []
    st_hist = []
    for i in range(NDT):
        par2 = i % 2
        for ch in range(4):
            par = 4 + (ch % 2)
            tmm = K.t(lambda e, i_=i, c_=ch, p_=par: e.matmul(
                ps[p_][:], s_WdtT[0:64, i_ * 128:(i_ + 1) * 128], s_dlr[:, c_ * 512:(c_ + 1) * 512],
                start=True, stop=True), [dbc_ev[-1]] + ([dw_last] if dw_last else []))
            t2 = K.s(lambda e, i_=i, p_=par: e.activation(
                s_dstg[:], ps[p_][:], AF.Exp, bias=s_bdt_a[:, i_:i_ + 1]), [tmm])
            dw_last = K.s(lambda e, i_=i, c_=ch, p2=par2: e.activation(
                s_dloc[p2][:, c_ * 512:(c_ + 1) * 512], s_dstg[:],
                AF.Ln, bias=1.0), [t2] + (st_hist[-2:-1] if len(st_hist) >= 2 else []))
        dw_last = K.v(lambda e, i_=i, p2=par2: e.tensor_tensor(
            s_wloc[p2][:], s_dloc[p2][:], sM[i_][:], ALU.mult), [dw_last])
        if i < 4:
            dd = t_scr_ad[i * 128:(i + 1) * 128, :]
            dw = t_scr_aw[i * 128:(i + 1) * 128, :]
        else:
            dd = t_cc1_in[(i - 4) * 128:(i - 3) * 128, :]
            dw = t_cc1_in[512 + (i - 4) * 128:512 + (i - 3) * 128, :]
        s1 = K.stg(dd, s_dloc[par2][:], [dw_last])
        s2 = K.stg(dw, s_wloc[par2][:], [s1])
        snd.append(s2)
        st_hist.append(s2)
    snd.append(K.stg(t_cc1_in[1024:1040, :], s_bB[:], [dbc_ev[-1]]))
    snd.append(K.stg(t_cc1_in[1040:1056, :], s_bC[:], [dbc_ev[-1]]))
    tok_ag1 = K.cc("AllGather", ALU.bypass, [t_cc1_in], [t_cc1_out], snd + [tok_regs])

    asm = []
    g4 = lambda t: t.rearrange("(g p) c -> p g c", p=128)
    b4 = lambda b: b[:].rearrange("p (g f) -> p g f", g=4)

    def asml(dst, local_ap, slot0_ap, slot1_ap, pre, bview=None, gview=None):
        bv = bview or b4
        gv = gview or (lambda t: t)
        t1 = select1(lambda b: (gv(dst[0]), bv(b)), local_ap, slot0_ap, pre)
        t2 = select1(lambda b: (gv(dst[1]), bv(b)), slot1_ap, local_ap, pre + [t1])
        return t2

    asm.append(asml((g4(t_af[:, 0:TOK]), g4(t_af[:, TOK:FULL])), g4(t_scr_ad),
                    g4(t_cc1_out[0, 0:512, :]), g4(t_cc1_out[1, 0:512, :]), [tok_ag1]))
    asm.append(asml((g4(t_wf[:, 0:TOK]), g4(t_wf[:, TOK:FULL])), g4(t_scr_aw),
                    g4(t_cc1_out[0, 512:1024, :]), g4(t_cc1_out[1, 512:1024, :]),
                    [tok_ag1, asm[-1]]))
    asm.append(asml((t_bcf[:, 0:TOK], t_bcf[:, TOK:FULL]), t_cc1_in[1024:1056, :],
                    t_cc1_out[0, 1024:1056, :], t_cc1_out[1, 1024:1056, :],
                    [tok_ag1, asm[-1]],
                    bview=lambda b: b[0:32, 0:TOK], gview=lambda t: t))
    tok_asm = drain_dg([asm[-1]])
    a1b.close()
    a1.close()

    def make_scanbufs(scope, tag):
        sets = []
        for par in range(2):
            B = {}
            B["dA"] = K.sbuf(f"dA_{tag}{par}", [128, FT], BF16, scope=scope)
            B["BX"] = K.sbuf(f"BX_{tag}{par}", [128, FT], BF16, scope=scope)
            B["h"] = K.sbuf(f"h_{tag}{par}", [128, FT], BF16, scope=scope)
            B["tmp"] = K.sbuf(f"tmp_{tag}{par}", [128, FT], BF16, scope=scope)
            B["inj"] = K.sbuf(f"inj_{tag}{par}", [128, S], scope=scope)
            B["yr"] = K.sbuf(f"yr_{tag}{par}", [128, TS], scope=scope)
            B["yrb"] = K.sbuf(f"yrb_{tag}{par}", [128, TS], BF16, scope=scope)
            sets.append(B)
        R = []
        for par in range(2):
            D = {}
            D["fl1"] = K.sbuf(f"fl1_{tag}{par}", [1, FT], BF16, scope=scope)
            D["fl2"] = K.sbuf(f"fl2_{tag}{par}", [1, FT], BF16, scope=scope)
            D["Brep"] = K.sbuf(f"Brep_{tag}{par}", [128, FT], BF16, scope=scope)
            D["Crep"] = K.sbuf(f"Crep_{tag}{par}", [128, FT], BF16, scope=scope)
            R.append(D)
        return sets, R

    def rep_build(D, bc_B, bc_C, j, pre):
        tf1 = K.stg(D["fl1"][:].rearrange("o (s t) -> o s t", s=S),
                    bc_B[:, j * TS:(j + 1) * TS].unsqueeze(1), pre, sem="rep")
        tf2 = K.stg(D["fl2"][:].rearrange("o (s t) -> o s t", s=S),
                    bc_C[:, j * TS:(j + 1) * TS].unsqueeze(1), pre, sem="rep")
        tr1 = K.stg(D["Brep"][:], D["fl1"][0:1, :].unsqueeze(1).broadcast_to([1, 128, FT]), [tf1], sem="rep")
        tr2 = K.stg(D["Crep"][:], D["fl2"][0:1, :].unsqueeze(1).broadcast_to([1, 128, FT]), [tf2], sem="rep")
        return [tr1, tr2]

    last_red = {0: None, 1: None}

    def scan_itemA(B, D, Arow, d_ap, w_ap, pre_v, rev, par):
        dA3 = sm(B["dA"][:])
        ts1 = None
        for si in range(S):
            w_ = (list(pre_v) + [last_red[par]]) if si == 0 else ()
            ts1 = K.s(lambda e, si=si: e.activation(
                dA3[:, si, :], d_ap, AF.Exp, scale=Arow[:, si:si + 1]), w_)
        brep_ap = sm(D["Brep"][:])[:, :, ::-1] if rev else sm(D["Brep"][:])
        tv1 = K.v(lambda e, ba=brep_ap: e.tensor_tensor(
            sm(B["BX"][:]), w_ap.unsqueeze(1).broadcast_to([128, S, TS]), ba, ALU.mult), pre_v)
        return dict(B=B, D=D, ts1=ts1, tv1=tv1, rev=rev)

    def scan_itemB(ctx, hp_ap, hp_pre):
        B, D, rev = ctx["B"], ctx["D"], ctx["rev"]
        crep_ap = sm(D["Crep"][:])[:, :, ::-1] if rev else sm(D["Crep"][:])
        dA3, BX3 = sm(B["dA"][:]), sm(B["BX"][:])
        tv2 = K.v(lambda e: e.tensor_tensor(B["inj"][:], dA3[:, :, 0], hp_ap, ALU.mult),
                  [ctx["ts1"], ctx["tv1"]] + hp_pre)
        tv3 = K.v(lambda e: e.tensor_tensor(BX3[:, :, 0], BX3[:, :, 0], B["inj"][:], ALU.add), [tv2])
        tv4 = K.v(lambda e: e.memset(dA3[:, :, 0], 0.0), [tv3])
        tv5 = K.v(lambda e: e.tensor_tensor_scan(B["h"][:], B["dA"][:], B["BX"][:], 0.0, ALU.mult, ALU.add), [tv4])
        tv6 = K.v(lambda e: e.tensor_copy(hp_ap, sm(B["h"][:])[:, :, TS - 1]), [tv5])
        tv7 = K.v(lambda e, ca=crep_ap: e.tensor_tensor(sm(B["tmp"][:]), sm(B["h"][:]), ca, ALU.mult), [tv5])

        def _red(e, B=B):
            with K.nc.allow_low_precision(reason="DVE reduce accumulates fp32 internally"):
                return e.tensor_reduce(
                    B["yr"][:], B["tmp"][:].rearrange("p (s t) -> p t s", s=S), AX.X, ALU.add)
        tg2 = K.v(_red, [tv7])
        return tv6, tg2

    a2 = contextlib.ExitStack()
    SBs, SRs = make_scanbufs(a2, "a")

    s_dsl = K.sbuf("dsl_a", [128, 2, 4, TS], BF16, scope=a2)
    s_wsl = K.sbuf("wsl_a", [128, 2, 4, TS], BF16, scope=a2)
    s_hp4 = K.sbuf("hp_a", [128, 4, S], scope=a2)

    tok_bcS = tok_asm
    tok_hp0 = K.v(lambda e: e.memset(s_hp4[:].rearrange("p a s -> p (a s)"), 0.0), [tok_asm])
    item_hist = []
    rep_hist = []
    tsl_hist = []
    pend = [None]
    nitem = 0

    def flushA(ctx):
        if ctx is None:
            return
        hpw, red = scan_itemB(ctx, ctx["hp_ap"], [tok_hp0])
        last_red[ctx["par"]] = red
        tcb = K.v(lambda e, B_=ctx["B"]: e.tensor_copy(B_["yrb"][:], B_["yr"][:]), [red])
        tyc = K.stg(t_scr_ay[ctx["i"] * 128:(ctx["i"] + 1) * 128,
                             ctx["j"] * TS:(ctx["j"] + 1) * TS], ctx["B"]["yrb"][:], [tcb], sem="dg2")
        item_hist.append(tyc)

    for j in range(NJ):
        jp = j % 2
        rguard = rep_hist[-1] if len(rep_hist) >= 2 else None
        reps = rep_build(SRs[jp], t_bcf[0:16, :], t_bcf[16:32, :],
                         j, [tok_bcS] + ([rguard] if rguard else []))
        for i in range(4):
            par = nitem % 2
            SB = SBs[par]
            bsel = par
            guard = tsl_hist[-2][1] if len(tsl_hist) >= 2 else None
            bufg = tsl_hist[-2][0] if len(tsl_hist) >= 2 else None
            tl1 = K.ld(s_dsl[:, bsel, i, :].unsqueeze(1),
                       t_af[i * 128:(i + 1) * 128, j * TS:(j + 1) * TS].unsqueeze(1),
                       [tok_asm, guard, bufg], sem=f"dy{par}")
            tl2 = K.ld(s_wsl[:, bsel, i, :].unsqueeze(1),
                       t_wf[i * 128:(i + 1) * 128, j * TS:(j + 1) * TS].unsqueeze(1),
                       [tl1], sem=f"dy{par}")
            pv = [tl2] + reps + ([bufg] if bufg else [])
            ctx = scan_itemA(SBs[par], SRs[jp], s_Aa[:, i, :], s_dsl[:, bsel, i, :],
                             s_wsl[:, bsel, i, :], pv, rev=False, par=par)
            ctx.update(i=i, j=j, hp_ap=s_hp4[:, i, :], par=par)
            tsl_hist.append((ctx["ts1"], ctx["tv1"]))
            flushA(pend[0])
            pend[0] = ctx
            nitem += 1
        rep_hist.append(item_hist[-1] if item_hist else None)
    flushA(pend[0])
    pend[0] = None
    prev_item = item_hist[-1]
    snd2 = []
    tok_aydone = drain_dg([prev_item])
    snd2.append(select1(lambda b: (g4(t_cc2_in), b4(b)),
                        t_scr_ay[:, TOK:FULL].rearrange("(g p) c -> p g c", p=128),
                        t_scr_ay[:, 0:TOK].rearrange("(g p) c -> p g c", p=128), [tok_aydone]))
    tok_ag2 = K.cc("AllGather", ALU.bypass, [t_cc2_in], [t_cc2_out], snd2)
    a2.close()

    a3 = contextlib.ExitStack()
    s_yfx = [K.sbuf(f"yfx{i}", [128, TOK], BF16, scope=a3) for i in range(NDT)]
    ytoks = []
    t1 = select1(lambda b: (s_yfx[0][:], b[:].rearrange("p (g f) -> p g f", g=4)[:, 0, :]),
                 t_scr_ay[:, 0:TOK].rearrange("(g p) c -> p g c", p=128),
                 t_scr_ay[:, TOK:FULL].rearrange("(g p) c -> p g c", p=128),
                 [tok_ag2, tok_aydone])
    ytoks.append(t1)
    for i in range(1, 4):
        ytoks.append(K.stg(s_yfx[i][:], s_bnc[:].rearrange("p (g f) -> p g f", g=4)[:, i, :], [t1], sem="dg2"))
    t2 = select1(lambda b: (s_yfx[4][:], b[:].rearrange("p (g f) -> p g f", g=4)[:, 0, :]),
                 g4(t_cc2_out[1]), g4(t_cc2_out[0]), ytoks)
    ytoks.append(t2)
    for i in range(1, 4):
        ytoks.append(K.stg(s_yfx[4 + i][:], s_bnc[:].rearrange("p (g f) -> p g f", g=4)[:, i, :], [t2], sem="dg2"))
    tok_attn_a = None
    for i in range(NDT):
        tok_attn_a = K.v(lambda e, i_=i: e.tensor_add(sx[i_][:], sx[i_][:], s_yfx[i_][:]),
                         [ytoks[-1]])
    a3.close()

    def ffn(pre):
        fsc0 = contextlib.ExitStack()
        sMt = K.sbuf(f"Mf{ln_idx[0]}", [128, TOK], BF16, scope=fsc0)
        mspill = []

        def dst_f(i, stg_, brep, tn1):
            t2 = K.v(lambda e: e.tensor_tensor(sMt[:], stg_[:], brep[:], ALU.add), [tn1])
            tk = K.stg(t_scr_M[i * 128:(i + 1) * 128, :], sMt[:], [t2])
            mspill.append(tk)
            return tk

        tok_ln = layernorm_make(pre, dst_f)
        fsc0.close()

        fsc = contextlib.ExitStack()
        nm = ln_idx[0]
        sWe = [K.sbuf(f"We{nm}_{k}", [128, 2048], BF16, scope=fsc) for k in range(NDT)]
        sWc = [K.sbuf(f"Wc{nm}_{q}", [128, 1024], BF16, scope=fsc) for q in range(16)]
        sH = [K.sbuf(f"H{nm}_{q}", [128, 256], BF16, scope=fsc) for q in range(16)]
        sMc = K.sbuf(f"Mc{nm}", [128, NDT, 256], BF16, scope=fsc)
        last_x = tok_ln
        for grp in range(2):
            hbase = grp * 16
            wl = [K.ld(sWe[k][:], t_wpe[k * 128:(k + 1) * 128, hbase * 128:(hbase + 16) * 128],
                       [tok_ln, last_x]) for k in range(NDT)]
            wl += [K.ld(sWc[q][:], t_wpc[(hbase + q) * 128:(hbase + q + 1) * 128, :],
                        [tok_ln, last_x]) for q in range(16)]
            for ch in range(8):
                tmc = K.ld(sMc[:],
                           t_scr_M.rearrange("(a p) t -> p a t", p=128)[:, :, ch * 256:(ch + 1) * 256],
                           [tok_ln, last_x])
                ev_up = {}
                for ht in range(16):
                    par = ht % 2
                    tmm = None
                    for k in range(NDT):
                        tmm = K.t(lambda e, h_=ht, k_=k, p_=par: e.matmul(
                            ps[p_][:, 0:256], sWe[k_][:, h_ * 128:(h_ + 1) * 128], sMc[:, k_, :],
                            start=(k_ == 0), stop=(k_ == NDT - 1)),
                            wl + [tmc] + ([ev_up.get(ht - 2)] if ev_up.get(ht - 2) else []))
                    ev_up[ht] = K.s(lambda e, h_=ht, p_=par, hb=hbase: e.activation(
                        sH[h_][:], ps[p_][:, 0:256], AF.Gelu,
                        bias=s_bpu[:, hb + h_:hb + h_ + 1]), [tmm])
                ev_dn = {}
                for mt in range(NDT):
                    par = 2 + (mt % 2)
                    tmm = None
                    for q in range(16):
                        tmm = K.t(lambda e, m_=mt, q_=q, p_=par: e.matmul(
                            ps[p_][:, 0:256], sWc[q_][:, m_ * 128:(m_ + 1) * 128], sH[q_][:],
                            start=(q_ == 0), stop=(q_ == 15)),
                            [ev_up[15]] + ([ev_dn.get(mt - 2)] if ev_dn.get(mt - 2) else []))
                    if grp == 0:
                        ev_dn[mt] = K.v(lambda e, m_=mt, p_=par, c_=ch: e.scalar_tensor_tensor(
                            sx[m_][:, c_ * 256:(c_ + 1) * 256], ps[p_][:, 0:256],
                            s_bpc[:, m_:m_ + 1], sx[m_][:, c_ * 256:(c_ + 1) * 256],
                            ALU.add, ALU.add), [tmm, mspill[mt]])
                    else:
                        ev_dn[mt] = K.v(lambda e, m_=mt, p_=par, c_=ch: e.tensor_add(
                            sx[m_][:, c_ * 256:(c_ + 1) * 256],
                            sx[m_][:, c_ * 256:(c_ + 1) * 256], ps[p_][:, 0:256]), [tmm])
                last_x = ev_dn[NDT - 1]
        fsc.close()
        return last_x

    tok_ffn1 = ffn([tok_attn_a])

    b0 = contextlib.ExitStack()
    sMB = [K.sbuf(f"Mb{i}", [128, TOK], BF16, scope=b0) for i in range(NDT)]

    def dst_b(i, stg_, brep, tn1):
        return K.v(lambda e, i_=i: e.tensor_tensor(sMB[i_][:], stg_[:], brep[:], ALU.add), [tn1])

    tok_ln3 = layernorm_make([tok_ffn1], dst_b)

    sWinL = K.sbuf("WinL", [128, 2, NDT, 128], scope=b0)
    sWinB = K.sbuf("WinB", [128, 2, NDT, 128], BF16, scope=b0)
    sOutT = K.sbuf("OutT_b", [128, 2, TOK], BF16, scope=b0)
    ip_hist = []
    tcast_prev = None
    last_evs = None
    for tix in range(32):
        par = tix % 2
        guard = ip_hist[-2] if len(ip_hist) >= 2 else None
        tkl = K.ld(sWinL[:, par, :, :],
                   d_WinT.rearrange("(a p) c -> p a c", p=128)[:, :, tix * 128:(tix + 1) * 128],
                   [tok_ln3, guard], sem=f"dy{par}")
        tcast = K.s(lambda e, p_=par: e.copy(sWinB[:, p_, :, :], sWinL[:, p_, :, :]),
                    [tkl] + ([tcast_prev] if tcast_prev else []))
        tcast_prev = tcast
        evs = []
        for ch in range(4):
            pp = 4 + (ch % 2)
            tmm = None
            for k in range(NDT):
                tmm = K.t(lambda e, p_=par, k_=k, c_=ch, q_=pp: e.matmul(
                    ps[q_][:], sWinB[:, p_, k_, :], sMB[k_][:, c_ * 512:(c_ + 1) * 512],
                    start=(k_ == 0), stop=(k_ == NDT - 1)),
                    [tcast] + ([evs[-2]] if len(evs) >= 2 else ([last_evs] if last_evs else [])))
            if tix < 16:
                evs.append(K.s(lambda e, p_=par, c_=ch, q_=pp: e.copy(
                    sOutT[:, p_, c_ * 512:(c_ + 1) * 512], ps[q_][:]), [tmm]))
            else:
                evs.append(K.s(lambda e, p_=par, c_=ch, q_=pp: e.activation(
                    sOutT[:, p_, c_ * 512:(c_ + 1) * 512], ps[q_][:], AF.Silu), [tmm]))
        if tix < 8:
            dst = t_scr_bxin[tix * 128:(tix + 1) * 128, :]
        elif tix < 16:
            dst = t_ccb1_in[(tix - 8) * 128:(tix - 7) * 128, :]
        elif tix < 24:
            dst = t_scr_bz[(tix - 16) * 128:(tix - 15) * 128, :]
        else:
            dst = t_ccb1_in[1024 + (tix - 24) * 128:1024 + (tix - 23) * 128, :]
        ip_hist.append(K.stg(dst, sOutT[:, par, :], [evs[-1]]))
        last_evs = evs[-1]
    tok_agb1 = K.cc("AllGather", ALU.bypass, [t_ccb1_in], [t_ccb1_out], [ip_hist[-1], ip_hist[-2]])
    b0.close()

    b1 = contextlib.ExitStack()
    s_xinF = [K.sbuf(f"xinF{i}", [128, 6 + FULL], BF16, scope=b1) for i in range(NDT)]
    s_cacc = K.sbuf("cacc_b", [128, FULL], scope=b1)
    s_pev1 = K.sbuf("pev1_b", [64, 512], scope=b1)
    s_pev2 = K.sbuf("pev2_b", [16, 512], scope=b1)
    s_pev3 = K.sbuf("pev3_b", [16, 512], scope=b1)
    s_xct = K.sbuf("xct_b", [128, FULL], BF16, scope=b1)

    asmb = [K.v(lambda e, i_=i: e.memset(s_xinF[i_][:, 0:3], 0.0), [tok_agb1]) for i in range(NDT)]
    asmb += [K.v(lambda e, i_=i: e.memset(s_xinF[i_][:, 3 + FULL:6 + FULL], 0.0), [tok_agb1]) for i in range(NDT)]
    g4b = lambda t, g0: t[g0 * 512:(g0 + 1) * 512, :].rearrange("(g p) c -> p g c", p=128)
    for g0 in range(2):
        stg_ = stage_sel(g4b(t_scr_bxin, g0), g4b(t_ccb1_out[0, 0:1024, :], g0),
                         [tok_agb1, asmb[-1]])
        tX = sel_read(False, stg_)
        for ii in range(4):
            i = g0 * 4 + ii
            asmb.append(K.stg(s_xinF[i][:, 3:3 + TOK],
                              s_bnc[:].rearrange("p (g f) -> p g f", g=4)[:, ii, :], [tX], sem="dg2"))
        stg_ = stage_sel(g4b(t_ccb1_out[1, 0:1024, :], g0), g4b(t_scr_bxin, g0), asmb[-4:])
        tY = sel_read(False, stg_)
        for ii in range(4):
            i = g0 * 4 + ii
            asmb.append(K.stg(s_xinF[i][:, 3 + TOK:3 + FULL],
                              s_bnc[:].rearrange("p (g f) -> p g f", g=4)[:, ii, :], [tY], sem="dg2"))
        stg_ = stage_sel(g4b(t_scr_bz, g0), g4b(t_ccb1_out[0, 1024:2048, :], g0), asmb[-4:])
        tX = sel_read(False, stg_)
        asmb.append(K.stg(g4b(t_sz[:, 0:TOK], g0), s_bnc[:].rearrange("p (g f) -> p g f", g=4), [tX], sem="dg2"))
        stg_ = stage_sel(g4b(t_ccb1_out[1, 1024:2048, :], g0), g4b(t_scr_bz, g0), [asmb[-1]])
        tY = sel_read(False, stg_)
        asmb.append(K.stg(g4b(t_sz[:, TOK:FULL], g0), s_bnc[:].rearrange("p (g f) -> p g f", g=4), [tY], sem="dg2"))
    tok_asmb = drain_dg([asmb[-1]])

    proj_ev = {}
    xc_done = {}
    xcst = {}
    for dj, d in enumerate("fb"):
        pe = []
        for i in range(NDT):
            offs = [k for k in range(4)] if d == "f" else [6 - k for k in range(4)]
            tv = K.v(lambda e, i_=i, o_=offs[0], d_=d: e.tensor_scalar(
                s_cacc[:], s_xinF[i_][:, o_:o_ + FULL], s_cw[d_][:, i_, 0:1], None, ALU.mult),
                [tok_asmb] + ([xc_done.get((dj, i - 1))] if xc_done.get((dj, i - 1)) else
                              ([xc_done.get((0, NDT - 1))] if dj == 1 and i == 0 else [])))
            for k in range(1, 4):
                tv = K.v(lambda e, i_=i, o_=offs[k], k_=k, d_=d: e.scalar_tensor_tensor(
                    s_cacc[:], s_xinF[i_][:, o_:o_ + FULL], s_cw[d_][:, i_, k_:k_ + 1],
                    s_cacc[:], ALU.mult, ALU.add), [tv])
            tsil = K.s(lambda e, i_=i, d_=d: e.activation(
                s_xct[:], s_cacc[:], AF.Silu, bias=s_cb[d_][:, i_:i_ + 1]), [tv])
            xcst[(dj, i)] = K.stg(t_xc[d][i * 128:(i + 1) * 128, :], s_xct[:], [tsil])
            mm = []
            for ch in range(8):
                mm.append(K.t(lambda e, i_=i, c_=ch, d_=d: e.matmul(
                    ps[c_][:], s_xpT[d_][:, i_, :], s_xct[:, c_ * 512:(c_ + 1) * 512],
                    start=(i_ == 0), stop=(i_ == NDT - 1)),
                    [tsil] + ([proj_ev.get(dj - 1)] if i == 0 and proj_ev.get(dj - 1) else [])))
            xc_done[(dj, i)] = mm[-1]
            pe = mm
        t1_ = None
        for ch in range(8):
            cl = slice(ch * 512, (ch + 1) * 512)
            tc = K.s(lambda e, p_=ch: e.copy(s_pev1[:], ps[p_][0:64, :]),
                     [pe[ch], xc_done[(dj, NDT - 1)]] + ([t1_] if t1_ else []))
            tc = K.s(lambda e, p_=ch: e.copy(s_pev2[:], ps[p_][64:80, :]), [tc])
            tc = K.s(lambda e, p_=ch: e.copy(s_pev3[:], ps[p_][96:112, :]), [tc])
            t1_ = K.stg(t_ccb2_in[dj * 96:dj * 96 + 64, cl], s_pev1[:], [tc])
            t1_ = K.stg(t_ccb2_in[dj * 96 + 64:dj * 96 + 80, cl], s_pev2[:], [t1_])
            t1_ = K.stg(t_ccb2_in[dj * 96 + 80:dj * 96 + 96, cl], s_pev3[:], [t1_])
        proj_ev[dj] = t1_
    tok_arb2 = K.cc("AllReduce", ALU.add, [t_ccb2_in], [t_ccb2_out], [proj_ev[0], proj_ev[1]])
    b1.close()

    b2 = contextlib.ExitStack()
    SCs, SRb = make_scanbufs(b2, "b")
    s_dlrs = K.sbuf("dlrs_b", [64, 2, 512], BF16, scope=b2)

    s_dsl2 = K.sbuf("dsl_b", [128, 2, TS], BF16, scope=b2)
    s_xcsl = K.sbuf("xcsl_b", [128, 2, TS], BF16, scope=b2)
    s_yfsl = K.sbuf("yfsl_b", [128, 2, TS], BF16, scope=b2)
    s_szsl = K.sbuf("szsl_b", [128, 2, TS], BF16, scope=b2)
    s_yout = K.sbuf("yout_b", [128, TS], BF16, scope=b2)
    s_hpB = K.sbuf("hpB", [128, NDT, S], scope=b2)
    s_dstgB = s_bnc[:].bitcast(F32)[:, 1024:1536]

    b2p = contextlib.ExitStack()
    s_pl1 = s_bnc2[:].bitcast(F32)[0:64, 0:1024]
    s_pl1b = s_bnc2[0:64, 2048:3072]
    s_pl2b = s_bnc2[0:16, 3072:4096]
    s_pl2 = s_bnc2[:].bitcast(F32)[0:16, 2048:3072]
    s_pl3 = s_bnc2[:].bitcast(F32)[0:16, 3072:4096]
    pcast = []
    for dj, d in enumerate("fb"):
        for hf in range(4):
            cl = slice(hf * 1024, (hf + 1) * 1024)
            tk1 = K.ld(s_pl1, t_ccb2_out[dj * 96:dj * 96 + 64, cl], [tok_arb2] + pcast)
            tk2 = K.ld(s_pl2, t_ccb2_out[dj * 96 + 64:dj * 96 + 80, cl], [tk1])
            tk3 = K.ld(s_pl3, t_ccb2_out[dj * 96 + 80:dj * 96 + 96, cl], [tk2])
            tcp = K.s(lambda e: e.copy(s_pl1b, s_pl1), [tk1])
            pcast.append(K.stg(t_dlr[dj, :, cl], s_pl1b, [tcp]))
            tcp = K.s(lambda e: e.copy(s_pl2b, s_pl2), [tk2])
            pcast.append(K.stg(t_bcb[dj, 0:16, cl], s_pl2b[0:16, :], [tcp]))
            tcp = K.s(lambda e: e.copy(s_pl2b, s_pl3), [tk3, pcast[-1]])
            pcast.append(K.stg(t_bcb[dj, 16:32, cl], s_pl2b[0:16, :], [tcp]))
    tok_pcast = pcast[-1]
    b2p.close()

    tok_ysum_last = None
    for dj, d in enumerate("fb"):
        dt_st = {}
        last = None
        for i in range(NDT):
            for ch in range(8):
                par = ch % 2
                tld = K.ld(s_dlrs[:, ch % 2, :].unsqueeze(1),
                           t_dlr[dj, :, ch * 512:(ch + 1) * 512].unsqueeze(1),
                           [tok_pcast, tok_asmb] + ([last] if last else []), sem=f"dy{ch % 2}")
                tmm = K.t(lambda e, i_=i, p_=par, d_=d, c2=ch % 2: e.matmul(
                    ps[p_][:], s_dtwT[d_][0:64, i_ * 128:(i_ + 1) * 128],
                    s_dlrs[:, c2, :],
                    start=True, stop=True),
                    [tld] + ([last] if last else []) +
                    ([tok_ysum_last] if tok_ysum_last else []))
                t2 = K.s(lambda e, p_=par, i_=i, d_=d: e.activation(
                    s_dstgB, ps[p_][:], AF.Exp, bias=s_dtb[d_][:, i_:i_ + 1]), [tmm])
                last = K.s(lambda e: e.activation(s_bnc[:, 4096:4608], s_dstgB, AF.Ln, bias=1.0), [t2])
                last = K.stg(t_dt[i * 128:(i + 1) * 128, ch * 512:(ch + 1) * 512], s_bnc[:, 4096:4608], [last])
            dt_st[i] = last
        tok_dtb = drain_dg([dt_st[NDT - 1]])
        rev = (d == "b")
        tok_hp0b = K.v(lambda e: e.memset(s_hpB[:].rearrange("p a s -> p (a s)"), 0.0),
                       [dt_st[NDT - 1]] + ([tok_ysum_last] if tok_ysum_last else []))
        jorder = list(range(NJ)) if not rev else list(range(NJ - 1, -1, -1))
        prev_item = None
        hist = []
        rep_hist = []
        tslh = []
        pend = [None]
        nitem = 0

        def flushB(ctx):
            nonlocal prev_item
            if ctx is None:
                return
            hpw, red = scan_itemB(ctx, ctx["hp_ap"], [tok_hp0b])
            last_red[ctx["par"]] = red
            i_, j_, bsel_, SC_ = ctx["i"], ctx["j"], ctx["bsel"], ctx["B"]
            if d == "f":
                tyf = K.v(lambda e, i2=i_, b2=bsel_, B2=SC_: e.scalar_tensor_tensor(
                    s_yout[:], s_xcsl[:, b2, :], s_Db["f"][:, i2:i2 + 1], B2["yr"][:],
                    ALU.mult, ALU.add), [red] + ([prev_item] if prev_item else []))
                prev_item = K.stg(t_yf[i_ * 128:(i_ + 1) * 128, j_ * TS:(j_ + 1) * TS],
                                  s_yout[:], [tyf], sem="dg2")
            else:
                t1 = K.v(lambda e, b2=bsel_, B2=SC_: e.tensor_tensor(
                    s_bnc[:].bitcast(F32)[:, 1536:1536 + TS], s_yfsl[:, b2, :], B2["yr"][:, ::-1], ALU.add),
                    [red, ctx["tl2"], ctx["tsz"]] + ([prev_item] if prev_item else []))
                t2 = K.v(lambda e, i2=i_, b2=bsel_: e.scalar_tensor_tensor(
                    s_bnc[:].bitcast(F32)[:, 1536:1536 + TS], s_xcsl[:, b2, ::-1], s_Db["b"][:, i2:i2 + 1], s_bnc[:].bitcast(F32)[:, 1536:1536 + TS],
                    ALU.mult, ALU.add), [t1])
                t3 = K.v(lambda e, b2=bsel_: e.tensor_tensor(
                    s_yout[:], s_bnc[:].bitcast(F32)[:, 1536:1536 + TS], s_szsl[:, b2, :], ALU.mult), [t2])
                prev_item = K.stg(t_scr_bys[i_ * 128:(i_ + 1) * 128, j_ * TS:(j_ + 1) * TS],
                                  s_yout[:], [t3], sem="dg2")
            hist.append(prev_item)

        for j in jorder:
            jp = j % 2
            rguard = rep_hist[-1] if len(rep_hist) >= 2 else None
            reps = rep_build(SRb[jp], t_bcb[dj, 0:16, :], t_bcb[dj, 16:32, :], j,
                             [tok_pcast] + ([rguard] if rguard else []) +
                             ([tok_ysum_last] if tok_ysum_last and not hist else []))
            for i in range(NDT):
                par = nitem % 2
                SC = SCs[par]
                bsel = par
                guard = tslh[-2][1] if len(tslh) >= 2 else None
                bufg = tslh[-2][0] if len(tslh) >= 2 else None
                fing = hist[-1] if len(hist) >= 1 else None
                tl0 = K.ld(s_dsl2[:, bsel, :].unsqueeze(1),
                           t_dt[i * 128:(i + 1) * 128, j * TS:(j + 1) * TS].unsqueeze(1),
                           [tok_dtb, guard, fing, bufg], sem=f"dy{par}")
                tl1 = K.ld(s_xcsl[:, bsel, :].unsqueeze(1),
                           t_xc[d][i * 128:(i + 1) * 128, j * TS:(j + 1) * TS].unsqueeze(1),
                           [tok_dtb, guard, fing, bufg], sem=f"dy{par}")
                if rev:
                    d_ap = s_dsl2[:, bsel, ::-1]
                    w_src = s_xcsl[:, bsel, ::-1]
                else:
                    d_ap = s_dsl2[:, bsel, :]
                    w_src = s_xcsl[:, bsel, :]
                pv = [tl0, tl1] + reps + ([bufg] if bufg else [])
                ctx = scan_itemA(SC, SRb[jp], s_Ab[d][:, i, :], d_ap, w_src, pv, rev=rev, par=par)
                ctx.update(i=i, j=j, bsel=bsel, hp_ap=s_hpB[:, i, :], tl2=None, tsz=None, par=par)
                if d == "b":
                    ctx["tl2"] = K.ld(s_yfsl[:, bsel, :].unsqueeze(1),
                                      t_yf[i * 128:(i + 1) * 128, j * TS:(j + 1) * TS].unsqueeze(1),
                                      [yf_done, fing], sem=f"dy{par}")
                    ctx["tsz"] = K.ld(s_szsl[:, bsel, :].unsqueeze(1),
                                      t_sz[i * 128:(i + 1) * 128, j * TS:(j + 1) * TS].unsqueeze(1),
                                      [tok_asmb, fing], sem=f"dy{par}")
                tslh.append((ctx["ts1"], ctx["tv1"]))
                flushB(pend[0])
                pend[0] = ctx
                nitem += 1
            rep_hist.append(hist[-1] if hist else None)
        flushB(pend[0])
        pend[0] = None
        if d == "f":
            cnt2 = K.p.cnt["dg2"]
            K.p.op("gpsimd", None, ((("dg2", cnt2),) + (("dg2", prev_item[1]),)))
            yf_done = K.p.op("gpsimd", lambda e: e.dma_start(t_dbg, s_flags[:]), (), "bar", 16)
        tok_ysum_last = prev_item
    snd3 = []
    cnt2 = K.p.cnt["dg2"]
    K.p.op("gpsimd", None, ((("dg2", cnt2),),) if False else (("dg2", cnt2),))
    tok_bysdone = K.p.op("gpsimd", lambda e: e.dma_start(t_dbg, s_flags[:]), (), "bar", 16)
    for g0 in range(2):
        stg_ = stage_sel(
            t_scr_bys[g0 * 512:(g0 + 1) * 512, TOK:FULL].rearrange("(g p) c -> p g c", p=128),
            t_scr_bys[g0 * 512:(g0 + 1) * 512, 0:TOK].rearrange("(g p) c -> p g c", p=128),
            [tok_bysdone] + snd3)
        tX = sel_read(False, stg_)
        snd3.append(K.stg(t_ccb3_in[g0 * 512:(g0 + 1) * 512, :].rearrange("(g p) c -> p g c", p=128),
                          s_bnc[:].rearrange("p (g f) -> p g f", g=4), [tX]))
    tok_agb3 = K.cc("AllGather", ALU.bypass, [t_ccb3_in], [t_ccb3_out], snd3)
    b2.close()

    b3 = contextlib.ExitStack()
    s_yasm = [K.sbuf(f"yasm{i}", [128, TOK], BF16, scope=b3) for i in range(16)]
    s_Wo = [K.sbuf(f"Wo{k}", [128, 1024], BF16, scope=b3) for k in range(14)]
    s_Wo.append(s_bnc2[:, 0:1024])
    s_Wo.append(s_bnc2[:, 1024:2048])
    s_WoL = s_bnc[:].bitcast(F32)[:, 0:1024]
    ya = []
    for g0 in range(2):
        stg_ = stage_sel(
            t_scr_bys[g0 * 512:(g0 + 1) * 512, 0:TOK].rearrange("(g p) c -> p g c", p=128),
            t_scr_bys[g0 * 512:(g0 + 1) * 512, TOK:FULL].rearrange("(g p) c -> p g c", p=128),
            [tok_agb3, tok_bysdone] + ya)
        tX = sel_read(False, stg_)
        for ii in range(4):
            ya.append(K.stg(s_yasm[g0 * 4 + ii][:],
                            s_bnc[:].rearrange("p (g f) -> p g f", g=4)[:, ii, :], [tX], sem="dg2"))
        stg_ = stage_sel(
            t_ccb3_out[1, g0 * 512:(g0 + 1) * 512, :].rearrange("(g p) c -> p g c", p=128),
            t_ccb3_out[0, g0 * 512:(g0 + 1) * 512, :].rearrange("(g p) c -> p g c", p=128),
            ya[-4:])
        tX = sel_read(False, stg_)
        for ii in range(4):
            ya.append(K.stg(s_yasm[8 + g0 * 4 + ii][:],
                            s_bnc[:].rearrange("p (g f) -> p g f", g=4)[:, ii, :], [tX], sem="dg2"))
    wo_prev = None
    wo_toks = []
    for k in range(16):
        tkl = K.ld(s_WoL, d_WoutT[k * 128:(k + 1) * 128, :],
                   [wo_prev] if wo_prev else [ya[-1]])
        wo_prev = K.s(lambda e, k_=k: e.copy(s_Wo[k_][:], s_WoL), [tkl])
        wo_toks.append(wo_prev)
    tok_attn_b = None
    for ch in range(4):
        mm = {}
        for k in range(16):
            for mt in range(NDT):
                mm[mt] = K.t(lambda e, k_=k, m_=mt, c_=ch: e.matmul(
                    ps[m_][:], s_Wo[k_][:, m_ * 128:(m_ + 1) * 128],
                    s_yasm[k_][:, c_ * 512:(c_ + 1) * 512],
                    start=(k_ == 0), stop=(k_ == 15)),
                    [wo_toks[-1], ya[-1]] + ([tok_attn_b] if tok_attn_b else []))
        for mt in range(NDT):
            tok_attn_b = K.v(lambda e, m_=mt, c_=ch: e.tensor_add(
                sx[m_][:, c_ * 512:(c_ + 1) * 512], sx[m_][:, c_ * 512:(c_ + 1) * 512],
                ps[m_][:]), [mm[mt]])
    b3.close()

    tok_ffn2 = ffn([tok_attn_b])

    out = contextlib.ExitStack()
    s_out = K.sbuf("outstage", [128, NDT, 128], scope=out)
    tok_out = []
    for ch in range(16):
        evs = []
        for i in range(NDT):
            par = i % 2
            tt = K.t(lambda e, i_=i, c_=ch, p_=par: e.transpose(
                ps[p_][0:128, 0:128], sx[i_][:, c_ * 128:(c_ + 1) * 128], s_eye[:]),
                [tok_ffn2, tok_eye] + ([evs[-2]] if len(evs) >= 2 else
                                       ([tok_out[-1]] if tok_out else [])))
            evs.append(K.s(lambda e, i_=i, p_=par: e.copy(s_out[:, i_, :], ps[p_][0:128, 0:128]), [tt]))
        tok_out.append(K.stg(o_x[ch * 128:(ch + 1) * 128, :],
                             s_out[:].rearrange("p a c -> p (a c)"), [evs[-1]]))
    out.close()
    tfin = K.stg(t_dbg, s_flags[:], [tok_out[-1]])
    K.p.ops["gpsimd"].append((None, ((("dg", tfin[1]),)), None, 1))

    blk = K.st.enter_context(K.nc.Block())

    @blk.sync
    def _(eng):
        K.p.emit("sync", eng)

    @blk.vector
    def _(eng):
        K.p.emit("vector", eng)

    @blk.scalar
    def _(eng):
        K.p.emit("scalar", eng)

    @blk.tensor
    def _(eng):
        K.p.emit("tensor", eng)

    @blk.gpsimd
    def _(eng):
        K.p.emit("gpsimd", eng)

    K.st.close()
    return K.nc



def _perm_d(a, axis, hh):
    if hh == 0:
        return a
    n = a.shape[axis]
    idx = np.concatenate([np.arange(n // 2, n), np.arange(0, n // 2)])
    return np.take(a, idx, axis=axis)


def make_in_maps(I):
    f32 = lambda a: np.ascontiguousarray(np.asarray(a), dtype=np.float32)
    x = f32(I["L0_attn_a_x"])
    upw = f32(I["L0_ffn_up_weights"])
    dnw = f32(I["L0_ffn_down_weights"])

    bfAu = np.empty((7, 4, 2048), np.float32)
    for s_ in range(7):
        w = upw[s_]
        bfAu[s_] = np.stack([w[:, 0, 0], w[:, 0, 1], w[:, 1, 0], w[:, 1, 1]])
    bfAd = np.empty((7, 4, 2048), np.float32)
    for j, s_ in enumerate(range(6, -1, -1)):
        w = dnw[s_]
        bfAd[j] = np.stack([w[:, 0, 0], w[:, 1, 0], w[:, 0, 1], w[:, 1, 1]])
    bfBu = np.empty((5, 4, 16, 128), np.float32)
    for j, s_ in enumerate(range(7, 12)):
        w = upw[s_].reshape(16, 128, 2, 2)
        bfBu[j] = np.stack([w[:, :, 0, 0], w[:, :, 0, 1], w[:, :, 1, 0], w[:, :, 1, 1]])
    bfBd = np.empty((5, 4, 16, 128), np.float32)
    for j, s_ in enumerate(range(11, 6, -1)):
        w = dnw[s_].reshape(16, 128, 2, 2)
        bfBd[j] = np.stack([w[:, :, 0, 0], w[:, :, 1, 0], w[:, :, 0, 1], w[:, :, 1, 1]])

    Wdbc = f32(I["L0_attn_a_W_deltaBC"])
    Wdt = f32(I["L0_attn_a_W_dt"])
    bdt_a = f32(I["L0_attn_a_b_dt"])
    Alog_a = f32(I["L0_attn_a_A_log"])
    Da = f32(I["L0_attn_a_D_param"])
    Win = f32(I["L0_attn_b_in_proj_w"])
    Wout = f32(I["L0_attn_b_out_proj_w"])
    Wexp = f32(I["L0_ffn_W_exp"])
    Wcon = f32(I["L0_ffn_W_con"])

    in_maps = []
    for c in range(8):
        b, hh = c // 2, c % 2
        m = {}
        m["flags"] = np.array([1 - hh, hh], np.int32)
        hm = np.zeros((128, 2), np.float32)
        hm[:, 0] = 1.0 - hh
        hm[:, 1] = float(hh)
        m["hmask"] = hm
        m["xT"] = np.ascontiguousarray(_perm_d(x[b, hh * TOK:(hh + 1) * TOK, :].T, 0, hh))
        WdbcTp = np.zeros((1024, 128), np.float32)
        wt_ = _perm_d(Wdbc.T, 0, hh)
        WdbcTp[:, 0:64] = wt_[:, 0:64]
        WdbcTp[:, 64:80] = wt_[:, 64:80]
        WdbcTp[:, 96:112] = wt_[:, 80:96]
        m["WdbcT"] = WdbcTp
        m["WdtT"] = np.ascontiguousarray(_perm_d(Wdt.T, 1, hh))
        m["bdt_a"] = np.ascontiguousarray(_perm_d(bdt_a, 0, hh).reshape(8, 128).T)
        half_a = slice(512 * hh, 512 * hh + 512)
        m["Alog_a"] = Alog_a[half_a].copy()
        m["Da"] = np.ascontiguousarray(_perm_d(Da, 0, hh).reshape(8, 128).T)
        di_idx = np.concatenate([np.arange(1024 * hh, 1024 * hh + 1024),
                                 np.arange(1024 * (1 - hh), 1024 * (1 - hh) + 1024)])
        WinT = _perm_d(Win.T, 0, hh)
        xin_cols = WinT[:, 0:2048][:, di_idx]
        z_cols = WinT[:, 2048:4096][:, di_idx]
        m["WinT"] = np.ascontiguousarray(np.concatenate([xin_cols, z_cols], axis=1))
        WoutT = _perm_d(Wout.T, 1, hh)
        m["WoutT"] = np.ascontiguousarray(WoutT[di_idx, :])
        half_b = slice(1024 * hh, 1024 * hh + 1024)
        for d, tag in (("fwd", "f"), ("bwd", "b")):
            m[f"cw{tag}"] = f32(I[f"L0_attn_b_{d}_conv_w"])[half_b, 0, :].copy()
            m[f"cb{tag}"] = np.ascontiguousarray(f32(I[f"L0_attn_b_{d}_conv_b"])[half_b].reshape(8, 128).T)
            xpt_ = f32(I[f"L0_attn_b_{d}_x_proj_w"]).T[half_b, :]
            xptp = np.zeros((1024, 128), np.float32)
            xptp[:, 0:64] = xpt_[:, 0:64]
            xptp[:, 64:80] = xpt_[:, 64:80]
            xptp[:, 96:112] = xpt_[:, 80:96]
            m[f"xp{tag}T"] = xptp
            m[f"dtw{tag}T"] = np.ascontiguousarray(f32(I[f"L0_attn_b_{d}_dt_w"]).T[:, half_b])
            m[f"dtb{tag}"] = np.ascontiguousarray(f32(I[f"L0_attn_b_{d}_dt_b"])[half_b].reshape(8, 128).T)
            m[f"Alog{tag}"] = f32(I[f"L0_attn_b_{d}_A_log"])[half_b].copy()
            m[f"Dp{tag}"] = np.ascontiguousarray(f32(I[f"L0_attn_b_{d}_D"])[half_b].reshape(8, 128).T)
        m["WexpT"] = np.ascontiguousarray(_perm_d(Wexp.T, 1, hh))
        m["Wcon"] = np.ascontiguousarray(_perm_d(Wcon, 1, hh))
        m["bexp"] = np.ascontiguousarray(f32(I["L0_ffn_b_exp"]).reshape(32, 128).T)
        m["upbias"] = np.ascontiguousarray(f32(I["L0_ffn_up_bias"]).reshape(32, 128).T)
        m["bcon"] = np.ascontiguousarray(_perm_d(f32(I["L0_ffn_b_con"]), 0, hh).reshape(8, 128).T)
        m["downbias"] = np.ascontiguousarray(_perm_d(f32(I["L0_ffn_down_bias"]), 0, hh).reshape(8, 128).T)
        m["eye"] = np.eye(128, dtype=np.float32)
        m["bfAu"] = bfAu
        m["bfAd"] = bfAd
        m["bfBu"] = np.ascontiguousarray(bfBu.transpose(3, 0, 1, 2))
        m["bfBd"] = np.ascontiguousarray(bfBd.transpose(3, 0, 1, 2))
        in_maps.append(m)
    return in_maps


_CACHE = {}


def kernel(**inputs):
    in_maps = make_in_maps(inputs)
    if "nc" not in _CACHE:
        _CACHE["nc"] = build_program()
    nc = _CACHE["nc"]
    res = run_bass_kernel_spmd(nc, in_maps, core_ids=list(range(8)))
    out = np.empty((4, FULL, 1024), np.float32)
    for c in range(8):
        b, hh = c // 2, c % 2
        r = np.asarray(res.results[c]["out_xT"], dtype=np.float32)
        if hh == 1:
            r = np.concatenate([r[:, 512:], r[:, :512]], axis=1)
        out[b, hh * TOK:(hh + 1) * TOK, :] = r
    return out


# version 20
# speedup vs baseline: 1.1127x; 1.0537x over previous
import contextlib
import numpy as np
import ml_dtypes
import concourse.bass as bass
import concourse.mybir as mybir
from concourse.bass_utils import run_bass_kernel_spmd

F32 = mybir.dt.float32
BF16 = mybir.dt.bfloat16
I32 = mybir.dt.int32
ALU = mybir.AluOpType
AF = mybir.ActivationFunctionType
AX = mybir.AxisListType

TOK = 2048
FULL = 4096
NDT = 8
S = 16
TS = 128
FT = S * TS
NJ = FULL // TS
EPS = 1e-5
GROUPS = [[0, 1], [2, 3], [4, 5], [6, 7]]
ENGS = ("sync", "vector", "scalar", "tensor", "gpsimd")


class Plan:
    def __init__(self):
        self.ops = {e: [] for e in ENGS}
        self.cnt = {n: 0 for n in ("dy", "dy0", "dy1", "dg", "dg2", "sel1", "sel2", "rep", "bar", "v", "s", "t", "g", "c")}
        self.sems = {}

    def op(self, eng, fn, waits=(), inc=None, amt=1):
        tok = None
        if inc is not None:
            self.cnt[inc] += amt
            tok = (inc, self.cnt[inc])
        self.ops[eng].append((fn, tuple(w for w in waits if w is not None), inc, amt))
        return tok

    def emit(self, eng_name, eng):
        last = {}
        for fn, waits, inc, amt in self.ops[eng_name]:
            for sn, vv in waits:
                if last.get(sn, 0) >= vv:
                    continue
                eng.wait_ge(self.sems[sn], vv)
                last[sn] = vv
            if fn is None:
                continue
            ins = fn(eng)
            if inc is not None:
                ins.then_inc(self.sems[inc], amt)


class Kern:
    def __init__(self):
        nc = bass.Bass("TRN2", target_bir_lowering=False, debug=False)
        nc.detect_race_conditions = False
        self.nc = nc
        self.st = contextlib.ExitStack()
        self.p = Plan()
        self.regs = {}

    def din(self, n, sh, dt=F32):
        return self.nc.dram_tensor(n, sh, dt, kind="ExternalInput").ap()

    def dout(self, n, sh, dt=F32):
        return self.nc.dram_tensor(n, sh, dt, kind="ExternalOutput").ap()

    def dtmp(self, n, sh, dt=F32):
        return self.nc.dram_tensor(n, sh, dt, kind="Internal").ap()

    def sbuf(self, n, sh, dt=F32, scope=None):
        return (scope or self.st).enter_context(self.nc.sbuf_tensor(n, sh, dt))

    def psum(self, n, sh, dt=F32):
        return self.st.enter_context(self.nc.psum_tensor(n, sh, dt))

    def semaphore(self, n):
        h = self.st.enter_context(self.nc.semaphore(n))
        self.p.sems[n] = h
        return h

    def ld(self, out, in_, waits=(), sem="dy"):
        return self.p.op("sync", lambda e, o=out, i=in_: e.dma_start(o, i), waits, sem, 16)

    def stg(self, out, in_, waits=(), sem="dg"):
        return self.p.op("gpsimd", lambda e, o=out, i=in_: e.dma_start(o, i), waits, sem, 16)

    def cstg(self, out, in_, flag, waits=()):
        return self.p.op("gpsimd",
                         lambda e, o=out, i=in_, f=flag: e.dma_start(o, i, cond=self.regs[f]),
                         waits, "dg", 16)

    def v(self, fn, waits=()):
        return self.p.op("vector", fn, waits, "v", 1)

    def s(self, fn, waits=()):
        return self.p.op("scalar", fn, waits, "s", 1)

    def t(self, fn, waits=()):
        return self.p.op("tensor", fn, waits, "t", 1)

    def g(self, fn, waits=()):
        return self.p.op("gpsimd", fn, waits, "g", 1)

    def cc(self, kind, op, ins, outs, waits=()):
        return self.p.op(
            "gpsimd",
            lambda e, k=kind, o=op, i=ins, u=outs: e.collective_compute(
                k, o, replica_groups=GROUPS, ins=i, outs=u),
            waits, "c", 1)


def sm(ap, s_=S):
    return ap.rearrange("p (s t) -> p s t", s=s_)


def build_program():
    K = Kern()
    for sn in ("dy", "dy0", "dy1", "dg", "dg2", "sel1", "sel2", "rep", "bar", "v", "s", "t", "g", "c"):
        K.semaphore(sn)

    d_xT = K.din("xT", [1024, TOK])
    d_flags = K.din("flags", [2], I32)
    d_hmask = K.din("hmask", [128, 2])
    d_WdbcT = K.din("WdbcT", [1024, 128])
    d_WdtT = K.din("WdtT", [64, 1024])
    d_bdt_a = K.din("bdt_a", [128, 8])
    d_Alog_a = K.din("Alog_a", [512, S])
    d_Da = K.din("Da", [128, 8])
    d_WinT = K.din("WinT", [1024, 4096])
    d_WoutT = K.din("WoutT", [2048, 1024])
    d_cw = {d: K.din(f"cw{d}", [1024, 4]) for d in "fb"}
    d_cb = {d: K.din(f"cb{d}", [128, 8]) for d in "fb"}
    d_xpT = {d: K.din(f"xp{d}T", [1024, 128]) for d in "fb"}
    d_dtwT = {d: K.din(f"dtw{d}T", [64, 1024]) for d in "fb"}
    d_dtb = {d: K.din(f"dtb{d}", [128, 8]) for d in "fb"}
    d_Alog = {d: K.din(f"Alog{d}", [1024, S]) for d in "fb"}
    d_Dp = {d: K.din(f"Dp{d}", [128, 8]) for d in "fb"}
    d_WexpT = K.din("WexpT", [4096, 1024])
    d_Wcon = K.din("Wcon", [4096, 1024])
    d_bexp = K.din("bexp", [128, 32])
    d_upbias = K.din("upbias", [128, 32])
    d_bcon = K.din("bcon", [128, 8])
    d_downbias = K.din("downbias", [128, 8])
    d_eye = K.din("eye", [128, 128])
    d_bfA = {c: K.din(f"bfA{c}", [7, 4, 2048]) for c in "ud"}
    d_bfB = {c: K.din(f"bfB{c}", [128, 5, 4, 16]) for c in "ud"}
    o_x = K.dout("out_xT", [TOK, 1024])

    t_scr_ad = K.dtmp("scr_ad", [512, TOK], BF16)
    t_scr_aw = K.dtmp("scr_aw", [512, TOK], BF16)
    t_cc1_in = K.dtmp("cc1_in", [1056, TOK], BF16)
    t_cc1_out = K.dtmp("cc1_out", [2, 1056, TOK], BF16)
    t_af = K.dtmp("scr_af", [512, FULL], BF16)
    t_wf = K.dtmp("scr_wf", [512, FULL], BF16)
    t_bcf = K.dtmp("scr_bcf", [32, FULL], BF16)
    t_scr_ay = K.dtmp("scr_ay", [512, FULL], BF16)
    t_cc2_in = K.dtmp("cc2_in", [512, TOK], BF16)
    t_cc2_out = K.dtmp("cc2_out", [2, 512, TOK], BF16)
    t_scr_x = K.dtmp("scr_x", [1024, TOK], F32)
    t_scr_M = K.dtmp("scr_M", [1024, TOK], BF16)
    t_wpe = K.dtmp("scr_wpe", [1024, 4096], BF16)
    t_wpc = K.dtmp("scr_wpc", [4096, 1024], BF16)
    t_scr_bxin = K.dtmp("scr_bxin", [1024, TOK], BF16)
    t_scr_bz = K.dtmp("scr_bz", [1024, TOK], BF16)
    t_dt = K.dtmp("scr_dt", [1024, FULL], BF16)
    t_dlr = K.dtmp("scr_dlr", [2, 64, FULL], BF16)
    t_bcb = K.dtmp("scr_bcb", [2, 32, FULL], BF16)
    t_yf = K.dtmp("scr_yf", [1024, FULL], BF16)
    t_ccb1_in = K.dtmp("ccb1_in", [2048, TOK], BF16)
    t_ccb1_out = K.dtmp("ccb1_out", [2, 2048, TOK], BF16)
    t_sz = K.dtmp("scr_sz", [1024, FULL], BF16)
    t_xc = {d: K.dtmp(f"scr_xc{d}", [1024, FULL], BF16) for d in "fb"}
    t_ccb2_in = K.dtmp("ccb2_in", [192, FULL], F32)
    t_ccb2_out = K.dtmp("ccb2_out", [192, FULL], F32)
    t_scr_bys = K.dtmp("scr_bys", [1024, FULL], BF16)
    t_ccb3_in = K.dtmp("ccb3_in", [1024, TOK], BF16)
    t_ccb3_out = K.dtmp("ccb3_out", [2, 1024, TOK], BF16)

    sx = [K.sbuf(f"x{i}", [128, TOK]) for i in range(NDT)]
    s_ones = K.sbuf("ones", [128, 1])
    s_eye = K.sbuf("eyeS", [128, 128])
    s_eyeb = K.sbuf("eyeSb", [128, 128], BF16)
    s_flags = K.sbuf("flagsS", [1, 2], I32)
    s_WdbcT = K.sbuf("WdbcT_s", [128, NDT, 128], BF16)
    s_WdtT = K.sbuf("WdtT_s", [64, 1024], BF16)
    s_bdt_a = K.sbuf("bdt_a_s", [128, NDT])
    s_Aa = K.sbuf("Aa_s", [128, 4, S])
    s_Da = K.sbuf("Da_s", [128, NDT])
    s_cw = {d: K.sbuf(f"cw{d}_s", [128, NDT, 4]) for d in "fb"}
    s_cb = {d: K.sbuf(f"cb{d}_s", [128, NDT]) for d in "fb"}
    s_xpT = {d: K.sbuf(f"xp{d}T_s", [128, NDT, 128], BF16) for d in "fb"}
    s_dtwT = {d: K.sbuf(f"dtw{d}T_s", [64, 1024], BF16) for d in "fb"}
    s_dtb = {d: K.sbuf(f"dtb{d}_s", [128, NDT]) for d in "fb"}
    s_Ab = {d: K.sbuf(f"Ab{d}_s", [128, NDT, S]) for d in "fb"}
    s_Db = {d: K.sbuf(f"Db{d}_s", [128, NDT]) for d in "fb"}
    s_bpu = K.sbuf("bpu_s", [128, 32])
    s_bpc = K.sbuf("bpc_s", [128, NDT])
    s_bfB = {c: K.sbuf(f"bfB{c}_s", [128, 5, 4, 16]) for c in "ud"}

    ps = [K.psum(f"ps{i}", [128, 512]) for i in range(8)]

    def _selview(base, like_ap):
        sh = list(like_ap.shape)
        v = base
        if len(sh) == 2:
            return v[0:sh[0], 0:sh[1]]
        if len(sh) == 3:
            return v[:].rearrange("p (g f) -> p g f", g=sh[1])[0:sh[0], :, 0:sh[2]]
        raise ValueError(sh)

    def _selfree(ap):
        sh = list(ap.shape)
        n = 1
        for x in sh[1:]:
            n *= x
        return sh[0], n

    def stage_sel(c0_ap, c1_ap, pre):
        rows, free = _selfree(c0_ap)
        t1 = K.ld(_selview(s_bnc, c0_ap), c0_ap, pre, sem="sel1")
        t2 = K.stg(_selview(s_bnc2, c1_ap), c1_ap, pre, sem="sel2")
        return [t1, t2, (rows, free)]

    def sel_read(swap, st_):
        t1, t2, (rows, free) = st_
        a_, b_ = (0, 1) if not swap else (1, 0)
        t3 = K.v(lambda e, r=rows, f=free, a=a_: e.tensor_scalar(
            s_bnc[0:r, 0:f], s_bnc[0:r, 0:f], s_hm[0:r, a:a + 1], None, ALU.mult),
            [t1, t2, tok_regs])
        return K.v(lambda e, r=rows, f=free, b=b_: e.scalar_tensor_tensor(
            s_bnc[0:r, 0:f], s_bnc2[0:r, 0:f], s_hm[0:r, b:b + 1], s_bnc[0:r, 0:f],
            ALU.mult, ALU.add), [t3])

    def select1(dst_fn, c0_ap, c1_ap, pre):
        st_ = stage_sel(c0_ap, c1_ap, pre)
        t4 = sel_read(False, st_)
        o, i = dst_fn(s_bnc)
        return K.stg(o, i, [t4], sem="dg2")

    tok_x = [K.ld(sx[i][:], d_xT.rearrange("(a p) t -> a p t", p=128)[i]) for i in range(NDT)]
    tok_fl = K.ld(s_flags[:], d_flags.unsqueeze(0))
    tok_eye = K.ld(s_eye[:], d_eye)
    tok_eyeb = K.s(lambda e: e.copy(s_eyeb[:], s_eye[:]), [tok_eye])
    tok_ones = K.v(lambda e: e.memset(s_ones[:], 1.0))
    s_eps = K.sbuf("epsS", [1, 1])
    s_bnc = K.sbuf("bounceS", [128, 8192], BF16)
    s_bnc2 = K.sbuf("bounceS2", [128, 8192], BF16)
    s_hm = K.sbuf("hmaskS", [128, 2])
    tok_eps = K.v(lambda e: e.memset(s_eps[:], EPS))

    t_dbg = K.dtmp("t_dbg", [1, 2], I32)
    tok_regs = K.ld(s_hm[:], d_hmask)

    def drain_dg(pre=()):
        cnt = K.p.cnt["dg"]
        K.p.op("gpsimd", None, tuple(pre) + (("dg", cnt),))
        return K.p.op("gpsimd", lambda e: e.dma_start(t_dbg, s_flags[:]), (), "bar", 16)

    boot = contextlib.ExitStack()
    s_bs = K.sbuf("bootstage", [128, 4096], scope=boot)
    prev_cast = [None, None]

    def load_cast(dst_ap, src_ap, rows, cols):
        tk = K.ld(s_bs[0:rows, 0:cols], src_ap, [prev_cast[0]])
        prev_cast[0] = K.s(lambda e, d=dst_ap, r=rows, c=cols: e.copy(d, s_bs[0:r, 0:c]), [tk])
        return prev_cast[0]

    load_cast(s_WdbcT[:], d_WdbcT.rearrange("(a p) c -> p a c", p=128), 128, NDT * 128)
    load_cast(s_WdtT[:], d_WdtT, 64, 1024)
    for d in "fb":
        load_cast(s_xpT[d][:], d_xpT[d].rearrange("(a p) c -> p a c", p=128), 128, NDT * 128)
        load_cast(s_dtwT[d][:], d_dtwT[d], 64, 1024)
    tk = K.ld(s_bs[0:128, 0:4 * S], d_Alog_a.rearrange("(a p) s -> p a s", p=128),
              [prev_cast[0], prev_cast[1]])
    tk = K.s(lambda e: e.activation(s_bs[0:128, 128:128 + 4 * S], s_bs[0:128, 0:4 * S], AF.Exp), [tk])
    prev_cast[0] = K.v(lambda e: e.tensor_scalar_mul(
        s_Aa[:].rearrange("p a s -> p (a s)"), s_bs[0:128, 128:128 + 4 * S], -1.0), [tk])
    for d in "fb":
        tk = K.ld(s_bs[0:128, 0:NDT * S], d_Alog[d].rearrange("(a p) s -> p a s", p=128),
                  [prev_cast[0], prev_cast[1]])
        tk = K.s(lambda e: e.activation(s_bs[0:128, 256:256 + NDT * S], s_bs[0:128, 0:NDT * S], AF.Exp), [tk])
        prev_cast[0] = K.v(lambda e, d_=d: e.tensor_scalar_mul(
            s_Ab[d_][:].rearrange("p a s -> p (a s)"), s_bs[0:128, 256:256 + NDT * S], -1.0), [tk])

    K.ld(s_bdt_a[:], d_bdt_a)
    K.ld(s_Da[:], d_Da)
    for d in "fb":
        K.ld(s_cw[d][:], d_cw[d].rearrange("(a p) k -> p a k", p=128))
        K.ld(s_cb[d][:], d_cb[d])
        K.ld(s_dtb[d][:], d_dtb[d])
        K.ld(s_Db[d][:], d_Dp[d])
    tok_bc1 = K.ld(s_bpc[:], d_bcon)
    tok_db = K.ld(s_bs[0:128, 512:512 + NDT], d_downbias,
                  [tok_bc1, prev_cast[0], prev_cast[1]])
    tok_bpc = K.v(lambda e: e.tensor_add(s_bpc[:], s_bpc[:], s_bs[0:128, 512:512 + NDT]),
                  [tok_db, tok_bc1])
    for c in "ud":
        K.ld(s_bfB[c][:], d_bfB[c])

    boot.close()

    comp = contextlib.ExitStack()
    sZbU = K.sbuf("ZcbU", [128, 4096], BF16, scope=comp)
    sZbD = K.sbuf("ZcbD", [128, 4096], BF16, scope=comp)
    sbvec = K.sbuf("bvecc", [128, 32], scope=comp)
    sbvec2 = K.sbuf("bvecc2", [128, 32], scope=comp)
    zbsc = contextlib.ExitStack()
    sZ = K.sbuf("Zc", [128, 4096], scope=zbsc)
    sZ2 = K.sbuf("Zc2", [128, 4096], scope=zbsc)
    sT1 = K.sbuf("Tc1", [128, 2048], scope=zbsc)
    sWrep = K.sbuf("Wrep", [128, 2, 2048], scope=zbsc)
    sAvec = K.sbuf("Avec", [28, 2048], scope=zbsc)

    def build_Z(chain, pre):
        tka = K.ld(sAvec[:], d_bfA[chain].rearrange("a b c -> (a b) c"), pre)
        tz = K.stg(sZ[:].rearrange("p (q i) -> p q i", q=32),
                   s_eye[:].unsqueeze(1).broadcast_to([128, 32, 128]), pre + [tok_eye], sem="rep")
        cur, nxt = sZ, sZ2
        tok = None
        for st_ in range(7):
            sig = 1 << st_
            nh = 64 // sig
            def pos(buf, two, nh=nh, sig=sig):
                return buf[:].rearrange("p (q h two l) -> p q h two l",
                                        q=32, h=nh, two=2)[:, :, :, two, :]

            def wv(quad, nh=nh, sig=sig):
                return sWrep[:, quad, :].rearrange("p (q h l) -> p q h l", q=32, h=nh)

            c_, n_ = cur, nxt
            trep = None
            for qd in range(2):
                trep = K.stg(
                    sWrep[:, qd, :],
                    sAvec[st_ * 4 + qd:st_ * 4 + qd + 1, :].unsqueeze(1)
                    .broadcast_to([1, 128, 2048]),
                    ([tka, tz] if tok is None else [tok]) if qd == 0 else [trep], sem="rep")
            K.v(lambda e, c=c_, n=n_, pos=pos, wv=wv: e.tensor_tensor(
                pos(n, 0), pos(c, 0), wv(0), ALU.mult),
                ([trep, tz, tka] if tok is None else [trep, tok]))
            K.v(lambda e, c=c_, nh=nh, pos=pos, wv=wv: e.tensor_tensor(
                sT1[:].rearrange("p (q h l) -> p q h l", q=32, h=nh), pos(c, 1), wv(1), ALU.mult))
            tok0 = K.v(lambda e, n=n_, nh=nh, pos=pos: e.tensor_tensor(
                pos(n, 0), pos(n, 0),
                sT1[:].rearrange("p (q h l) -> p q h l", q=32, h=nh), ALU.add))
            trep = None
            for qd in range(2):
                trep = K.stg(
                    sWrep[:, qd, :],
                    sAvec[st_ * 4 + 2 + qd:st_ * 4 + 3 + qd, :].unsqueeze(1)
                    .broadcast_to([1, 128, 2048]),
                    [tok0] if qd == 0 else [trep], sem="rep")
            K.v(lambda e, c=c_, n=n_, pos=pos, wv=wv: e.tensor_tensor(
                pos(n, 1), pos(c, 0), wv(0), ALU.mult), [trep])
            K.v(lambda e, c=c_, nh=nh, pos=pos, wv=wv: e.tensor_tensor(
                sT1[:].rearrange("p (q h l) -> p q h l", q=32, h=nh), pos(c, 1), wv(1), ALU.mult))
            tok = K.v(lambda e, n=n_, nh=nh, pos=pos: e.tensor_tensor(
                pos(n, 1), pos(n, 1),
                sT1[:].rearrange("p (q h l) -> p q h l", q=32, h=nh), ALU.add))
            cur, nxt = nxt, cur
        return cur, tok

    def bstep_pairs(chain, cur, get_a, width, pre):
        tok = None
        for st_ in range(5):
            sigq = 1 << st_
            for hi in range(16 // sigq):
                for lo in range(sigq):
                    pr = hi * sigq + lo
                    qa = hi * 2 * sigq + lo
                    qb = qa + sigq

                    def w(quad, p_=pr, s__=st_, c_=chain):
                        return s_bfB[c_][:, s__, quad, p_:p_ + 1]
                    deps = (pre if tok is None else [tok])
                    K.v(lambda e, a=get_a(cur, qa), w=w: e.tensor_scalar(
                        sTb[0:128, 0:width], a, w(0), None, ALU.mult), deps)
                    t2 = K.v(lambda e, b=get_a(cur, qb), w=w: e.scalar_tensor_tensor(
                        sTb[0:128, 0:width], b, w(1), sTb[0:128, 0:width], ALU.mult, ALU.add))
                    K.v(lambda e, a=get_a(cur, qa), w=w: e.tensor_scalar(
                        sTb2[0:128, 0:width], a, w(2), None, ALU.mult))
                    t4 = K.v(lambda e, b=get_a(cur, qb), w=w: e.scalar_tensor_tensor(
                        sTb2[0:128, 0:width], b, w(3), sTb2[0:128, 0:width], ALU.mult, ALU.add))
                    t5 = K.v(lambda e, a=get_a(cur, qa): e.tensor_copy(a, sTb[0:128, 0:width]), [t2])
                    tok = K.v(lambda e, b=get_a(cur, qb): e.tensor_copy(b, sTb2[0:128, 0:width]), [t4, t5])
        return cur, tok

    Zu, tokZu = build_Z("u", [tok_ones, tok_bpc])
    tokZub = K.v(lambda e, c=Zu: e.tensor_copy(sZbU[:], c[:]), [tokZu])
    tkb = K.ld(sbvec[:], d_bexp)
    bt = None
    for q in range(32):
        tmm = K.t(lambda e, q_=q: e.matmul(
            ps[3][0:128, 0:1], Zu[:, q_ * 128:(q_ + 1) * 128], sbvec[:, q_:q_ + 1],
            start=True, stop=True), [tkb, tokZu] + ([bt] if bt else []))
        bt = K.s(lambda e, q_=q: e.copy(sbvec2[:, q_:q_ + 1], ps[3][0:128, 0:1]), [tmm])
    Zd, tokZd = build_Z("d", [tokZub, bt])
    tokZdb = K.v(lambda e, c=Zd: e.tensor_copy(sZbD[:], c[:]), [tokZd])
    zbsc.close()

    sWq = [K.sbuf(f"Wq{i}", [128, 1024], BF16, scope=comp) for i in range(32)]
    sTb = K.sbuf("Tcb", [128, 1024], BF16, scope=comp)
    sTb2 = K.sbuf("Tcb2", [128, 1024], BF16, scope=comp)
    sWl = K.sbuf("Wload", [128, 1024], scope=comp)
    sWlb = K.sbuf("Wloadb", [128, 1024], BF16, scope=comp)
    sEv = K.sbuf("Evac", [128, 512], BF16, scope=comp)

    tprev = tokZdb
    for q in range(32):
        tkl = K.ld(sWl[:], d_WexpT[q * 128:(q + 1) * 128, :], [tprev, tokZdb])
        tkl = K.s(lambda e: e.copy(sWlb[:], sWl[:]), [tkl])
        evs = []
        for nch in range(2):
            tmm = K.t(lambda e, q_=q, n_=nch: e.matmul(
                ps[n_][:], sZbU[:, q_ * 128:(q_ + 1) * 128],
                sWlb[:, n_ * 512:(n_ + 1) * 512],
                start=True, stop=True), [tkl, tokZub])
            evs.append(K.s(lambda e, q_=q, n_=nch: e.copy(
                sWq[q_][:, n_ * 512:(n_ + 1) * 512], ps[n_][:]), [tmm]))
        tprev = evs[-1]
    Wu, tok_bu = bstep_pairs("u", sWq, lambda L, q: L[q][:], 1024, [tprev])
    tok_wpe = []
    for q in range(32):
        for mt in range(NDT):
            tt = K.t(lambda e, q_=q, m_=mt: e.transpose(
                ps[2][0:128, 0:64].bitcast(BF16), Wu[q_][:, m_ * 128:(m_ + 1) * 128], s_eyeb[:]),
                [tok_bu, tok_eyeb] + ([tok_wpe[-1]] if tok_wpe else []))
            te = K.s(lambda e: e.copy(sEv[0:128, 0:128], ps[2][0:128, 0:64].bitcast(BF16)), [tt])
            tok_wpe.append(K.stg(t_wpe[mt * 128:(mt + 1) * 128, q * 128:(q + 1) * 128],
                                 sEv[0:128, 0:128], [te]))
    bias_buf, btok = bstep_pairs("u", sbvec2, lambda L, q: L[:, q:q + 1], 1, [bt, tok_bu])
    tk_upb = K.ld(sWl[0:128, 0:32], d_upbias)
    tok_bpu = K.v(lambda e, bb=bias_buf: e.tensor_add(s_bpu[:], bb[:], sWl[0:128, 0:32]),
                  [btok, tk_upb])

    tprev = None
    _dpre = [tok_bpu, tok_wpe[-1]]
    for q in range(32):
        tkl = K.ld(sWl[:], d_Wcon[q * 128:(q + 1) * 128, :], ([tprev] if tprev else _dpre))
        tprev = K.s(lambda e, q_=q: e.copy(sWq[q_][:], sWl[:]), [tkl] + ([] if tprev else _dpre))
    Wd1, tok_bd = bstep_pairs("d", sWq, lambda L, q: L[q][:], 1024, [tprev])
    tok_wpc = []
    for q in range(32):
        for nch in range(2):
            tmm = K.t(lambda e, q_=q, n_=nch: e.matmul(
                ps[4][:], sZbD[:, q_ * 128:(q_ + 1) * 128], Wd1[q_][:, n_ * 512:(n_ + 1) * 512],
                start=True, stop=True), [tok_bd, tokZdb] + ([tok_wpc[-1]] if tok_wpc else []))
            te = K.s(lambda e: e.copy(sEv[:], ps[4][:]), [tmm])
            tok_wpc.append(K.stg(t_wpc[q * 128:(q + 1) * 128, nch * 512:(nch + 1) * 512],
                                 sEv[:], [te]))
    comp.close()
    tok_compose = tok_wpc[-1]

    ln_idx = [0]

    def layernorm_make(pre, dst_fn):
        ln_idx[0] += 1
        nm = ln_idx[0]
        sc = contextlib.ExitStack()
        stg_ = K.sbuf(f"lnstage{nm}", [128, TOK], scope=sc)
        arep = K.sbuf(f"lnarep{nm}", [128, TOK], scope=sc)
        brep = K.sbuf(f"lnbrep{nm}", [128, TOK], scope=sc)
        s_stat = K.sbuf(f"lnstat{nm}", [1, 2, TOK], scope=sc)
        ev = []
        for ch in range(4):
            tmm = None
            for i in range(NDT):
                tmm = K.t(lambda e, i_=i, c_=ch: e.matmul(
                    ps[4 + c_][0:1, :], s_ones[:], sx[i_][:, c_ * 512:(c_ + 1) * 512],
                    start=(i_ == 0), stop=(i_ == NDT - 1)), pre + [tok_ones])
            ev.append(K.s(lambda e, c_=ch: e.copy(s_stat[0:1, 0, c_ * 512:(c_ + 1) * 512],
                                                  ps[4 + c_][0:1, :]), [tmm]))
        tsq = None
        last_mm = {}
        for i in range(NDT):
            tsq = K.s(lambda e, i_=i: e.activation(stg_[:], sx[i_][:], AF.Square),
                      pre + ([last_mm.get(3)] if i > 0 else [ev[-1]]))
            for ch in range(4):
                last_mm[ch] = K.t(lambda e, i_=i, c_=ch: e.matmul(
                    ps[c_][0:1, :], s_ones[:], stg_[:, c_ * 512:(c_ + 1) * 512],
                    start=(i_ == 0), stop=(i_ == NDT - 1)), [tsq])
        for ch in range(4):
            ev.append(K.s(lambda e, c_=ch: e.copy(s_stat[0:1, 1, c_ * 512:(c_ + 1) * 512],
                                                  ps[c_][0:1, :]), [last_mm[ch]]))
        t1 = K.v(lambda e: e.tensor_scalar_mul(s_stat[0:1, 0, :], s_stat[0:1, 0, :], 1.0 / 1024), [ev[-1], ev[3]])
        t1 = K.v(lambda e: e.tensor_scalar_mul(s_stat[0:1, 1, :], s_stat[0:1, 1, :], 1.0 / 1024), [t1])
        t1 = K.v(lambda e: e.scalar_tensor_tensor(
            arep[0:1, :], s_stat[0:1, 0, :], -1.0, s_stat[0:1, 0, :], ALU.mult, ALU.mult), [t1])
        t1 = K.v(lambda e: e.tensor_add(s_stat[0:1, 1, :], s_stat[0:1, 1, :], arep[0:1, :]), [t1])
        t1 = K.s(lambda e: e.activation(s_stat[0:1, 1, :], s_stat[0:1, 1, :], AF.Sqrt, bias=s_eps[0:1, 0:1]), [t1, tok_eps])
        t1 = K.v(lambda e: e.reciprocal(s_stat[0:1, 1, :], s_stat[0:1, 1, :]), [t1])
        t1 = K.v(lambda e: e.tensor_tensor(s_stat[0:1, 0, :], s_stat[0:1, 0, :], s_stat[0:1, 1, :], ALU.mult), [t1])
        t1 = K.v(lambda e: e.tensor_scalar_mul(s_stat[0:1, 0, :], s_stat[0:1, 0, :], -1.0), [t1])
        tr1 = K.stg(arep[:], s_stat[0:1, 1, :].unsqueeze(1).broadcast_to([1, 128, TOK]), [t1], sem="rep")
        tr2 = K.stg(brep[:], s_stat[0:1, 0, :].unsqueeze(1).broadcast_to([1, 128, TOK]), [t1], sem="rep")
        out_toks = []
        for i in range(NDT):
            tn1 = K.v(lambda e, i_=i: e.tensor_tensor(stg_[:], sx[i_][:], arep[:], ALU.mult),
                      [tr1, tr2] + ([out_toks[-1]] if out_toks else []))
            out_toks.append(dst_fn(i, stg_, brep, tn1))
        sc.close()
        return out_toks[-1]

    a1 = contextlib.ExitStack()
    sM = [K.sbuf(f"Ma{i}", [128, TOK], BF16, scope=a1) for i in range(NDT)]
    s_dlr = K.sbuf("dlr_a", [64, TOK], BF16, scope=a1)
    s_bB = K.sbuf("bB_a", [16, TOK], BF16, scope=a1)
    s_bC = K.sbuf("bC_a", [16, TOK], BF16, scope=a1)

    def dst_a(i, stg_, brep, tn1):
        t2 = K.v(lambda e, i_=i: e.tensor_tensor(sM[i_][:], stg_[:], brep[:], ALU.add), [tn1])
        return K.v(lambda e, i_=i: e.scalar_tensor_tensor(
            sx[i_][:], sM[i_][:], s_Da[:, i_:i_ + 1], sx[i_][:], ALU.mult, ALU.add), [t2])

    tok_ln1 = layernorm_make(tok_x + [tok_compose, prev_cast[0]], dst_a)

    dbc_ev = []
    for ch in range(4):
        tmm = None
        for i in range(NDT):
            tmm = K.t(lambda e, i_=i, c_=ch: e.matmul(
                ps[c_][:], s_WdbcT[:, i_, :], sM[i_][:, c_ * 512:(c_ + 1) * 512],
                start=(i_ == 0), stop=(i_ == NDT - 1)), [tok_ln1])
        dbc_ev.append(K.s(lambda e, c_=ch: e.copy(s_dlr[:, c_ * 512:(c_ + 1) * 512], ps[c_][0:64, :]), [tmm]))
        dbc_ev.append(K.s(lambda e, c_=ch: e.copy(s_bB[:, c_ * 512:(c_ + 1) * 512], ps[c_][64:80, :]), [tmm]))
        dbc_ev.append(K.s(lambda e, c_=ch: e.copy(s_bC[:, c_ * 512:(c_ + 1) * 512], ps[c_][96:112, :]), [tmm]))

    a1b = contextlib.ExitStack()
    s_dloc = [K.sbuf(f"dloc{i}", [128, TOK], BF16, scope=a1b) for i in range(2)]
    s_wloc = [K.sbuf(f"wloc{i}", [128, TOK], BF16, scope=a1b) for i in range(2)]
    s_dstg = K.sbuf("dstg_a", [128, 512], scope=a1b)
    dw_last = None
    snd = []
    st_hist = []
    for i in range(NDT):
        par2 = i % 2
        for ch in range(4):
            par = 4 + (ch % 2)
            tmm = K.t(lambda e, i_=i, c_=ch, p_=par: e.matmul(
                ps[p_][:], s_WdtT[0:64, i_ * 128:(i_ + 1) * 128], s_dlr[:, c_ * 512:(c_ + 1) * 512],
                start=True, stop=True), [dbc_ev[-1]] + ([dw_last] if dw_last else []))
            t2 = K.s(lambda e, i_=i, p_=par: e.activation(
                s_dstg[:], ps[p_][:], AF.Exp, bias=s_bdt_a[:, i_:i_ + 1]), [tmm])
            dw_last = K.s(lambda e, i_=i, c_=ch, p2=par2: e.activation(
                s_dloc[p2][:, c_ * 512:(c_ + 1) * 512], s_dstg[:],
                AF.Ln, bias=1.0), [t2] + (st_hist[-2:-1] if len(st_hist) >= 2 else []))
        dw_last = K.v(lambda e, i_=i, p2=par2: e.tensor_tensor(
            s_wloc[p2][:], s_dloc[p2][:], sM[i_][:], ALU.mult), [dw_last])
        if i < 4:
            dd = t_scr_ad[i * 128:(i + 1) * 128, :]
            dw = t_scr_aw[i * 128:(i + 1) * 128, :]
        else:
            dd = t_cc1_in[(i - 4) * 128:(i - 3) * 128, :]
            dw = t_cc1_in[512 + (i - 4) * 128:512 + (i - 3) * 128, :]
        s1 = K.stg(dd, s_dloc[par2][:], [dw_last])
        s2 = K.stg(dw, s_wloc[par2][:], [s1])
        snd.append(s2)
        st_hist.append(s2)
    snd.append(K.stg(t_cc1_in[1024:1040, :], s_bB[:], [dbc_ev[-1]]))
    snd.append(K.stg(t_cc1_in[1040:1056, :], s_bC[:], [dbc_ev[-1]]))
    tok_ag1 = K.cc("AllGather", ALU.bypass, [t_cc1_in], [t_cc1_out], snd + [tok_regs])

    asm = []
    g4 = lambda t: t.rearrange("(g p) c -> p g c", p=128)
    b4 = lambda b: b[:].rearrange("p (g f) -> p g f", g=4)

    def asml(dst, local_ap, slot0_ap, slot1_ap, pre, bview=None, gview=None):
        bv = bview or b4
        gv = gview or (lambda t: t)
        t1 = select1(lambda b: (gv(dst[0]), bv(b)), local_ap, slot0_ap, pre)
        t2 = select1(lambda b: (gv(dst[1]), bv(b)), slot1_ap, local_ap, pre + [t1])
        return t2

    asm.append(asml((g4(t_af[:, 0:TOK]), g4(t_af[:, TOK:FULL])), g4(t_scr_ad),
                    g4(t_cc1_out[0, 0:512, :]), g4(t_cc1_out[1, 0:512, :]), [tok_ag1]))
    asm.append(asml((g4(t_wf[:, 0:TOK]), g4(t_wf[:, TOK:FULL])), g4(t_scr_aw),
                    g4(t_cc1_out[0, 512:1024, :]), g4(t_cc1_out[1, 512:1024, :]),
                    [tok_ag1, asm[-1]]))
    asm.append(asml((t_bcf[:, 0:TOK], t_bcf[:, TOK:FULL]), t_cc1_in[1024:1056, :],
                    t_cc1_out[0, 1024:1056, :], t_cc1_out[1, 1024:1056, :],
                    [tok_ag1, asm[-1]],
                    bview=lambda b: b[0:32, 0:TOK], gview=lambda t: t))
    tok_asm = drain_dg([asm[-1]])
    a1b.close()
    a1.close()

    def make_scanbufs(scope, tag):
        sets = []
        for par in range(2):
            B = {}
            B["dA"] = K.sbuf(f"dA_{tag}{par}", [128, FT], BF16, scope=scope)
            B["BX"] = K.sbuf(f"BX_{tag}{par}", [128, FT], BF16, scope=scope)
            B["h"] = K.sbuf(f"h_{tag}{par}", [128, FT], BF16, scope=scope)
            B["tmp"] = K.sbuf(f"tmp_{tag}{par}", [128, FT], BF16, scope=scope)
            B["inj"] = K.sbuf(f"inj_{tag}{par}", [128, S], scope=scope)
            B["yr"] = K.sbuf(f"yr_{tag}{par}", [128, TS], scope=scope)
            B["yrb"] = K.sbuf(f"yrb_{tag}{par}", [128, TS], BF16, scope=scope)
            sets.append(B)
        R = []
        for par in range(2):
            D = {}
            D["fl1"] = K.sbuf(f"fl1_{tag}{par}", [1, FT], BF16, scope=scope)
            D["fl2"] = K.sbuf(f"fl2_{tag}{par}", [1, FT], BF16, scope=scope)
            D["Brep"] = K.sbuf(f"Brep_{tag}{par}", [128, FT], BF16, scope=scope)
            D["Crep"] = K.sbuf(f"Crep_{tag}{par}", [128, FT], BF16, scope=scope)
            R.append(D)
        return sets, R

    def rep_build(D, bc_B, bc_C, j, pre):
        tf1 = K.stg(D["fl1"][:].rearrange("o (s t) -> o s t", s=S),
                    bc_B[:, j * TS:(j + 1) * TS].unsqueeze(1), pre, sem="rep")
        tf2 = K.stg(D["fl2"][:].rearrange("o (s t) -> o s t", s=S),
                    bc_C[:, j * TS:(j + 1) * TS].unsqueeze(1), pre, sem="rep")
        tr1 = K.stg(D["Brep"][:], D["fl1"][0:1, :].unsqueeze(1).broadcast_to([1, 128, FT]), [tf1], sem="rep")
        tr2 = K.stg(D["Crep"][:], D["fl2"][0:1, :].unsqueeze(1).broadcast_to([1, 128, FT]), [tf2], sem="rep")
        return [tr1, tr2]

    last_red = {0: None, 1: None}

    def scan_itemA(B, D, Arow, d_ap, w_ap, pre_v, rev, par):
        dA3 = sm(B["dA"][:])
        ts1 = None
        for si in range(S):
            w_ = (list(pre_v) + [last_red[par]]) if si == 0 else ()
            ts1 = K.s(lambda e, si=si: e.activation(
                dA3[:, si, :], d_ap, AF.Exp, scale=Arow[:, si:si + 1]), w_)
        brep_ap = sm(D["Brep"][:])[:, :, ::-1] if rev else sm(D["Brep"][:])
        tv1 = K.v(lambda e, ba=brep_ap: e.tensor_tensor(
            sm(B["BX"][:]), w_ap.unsqueeze(1).broadcast_to([128, S, TS]), ba, ALU.mult), pre_v)
        return dict(B=B, D=D, ts1=ts1, tv1=tv1, rev=rev)

    def scan_itemB(ctx, hp_ap, hp_pre):
        B, D, rev = ctx["B"], ctx["D"], ctx["rev"]
        crep_ap = sm(D["Crep"][:])[:, :, ::-1] if rev else sm(D["Crep"][:])
        dA3, BX3 = sm(B["dA"][:]), sm(B["BX"][:])
        tv2 = K.v(lambda e: e.tensor_tensor(B["inj"][:], dA3[:, :, 0], hp_ap, ALU.mult),
                  [ctx["ts1"], ctx["tv1"]] + hp_pre)
        tv3 = K.v(lambda e: e.tensor_tensor(BX3[:, :, 0], BX3[:, :, 0], B["inj"][:], ALU.add), [tv2])
        tv4 = K.v(lambda e: e.memset(dA3[:, :, 0], 0.0), [tv3])
        tv5 = K.v(lambda e: e.tensor_tensor_scan(B["h"][:], B["dA"][:], B["BX"][:], 0.0, ALU.mult, ALU.add), [tv4])
        tv6 = K.v(lambda e: e.tensor_copy(hp_ap, sm(B["h"][:])[:, :, TS - 1]), [tv5])
        tv7 = K.v(lambda e, ca=crep_ap: e.tensor_tensor(sm(B["tmp"][:]), sm(B["h"][:]), ca, ALU.mult), [tv5])

        tmpf = B["tmp"][:]
        K.v(lambda e: e.tensor_tensor(
            tmpf[:, 0:1024], tmpf[:, 0:1024], tmpf[:, 1024:2048], ALU.add), [tv7])
        K.v(lambda e: e.tensor_tensor(
            tmpf[:, 0:512], tmpf[:, 0:512], tmpf[:, 512:1024], ALU.add), ())
        K.v(lambda e: e.tensor_tensor(
            tmpf[:, 0:256], tmpf[:, 0:256], tmpf[:, 256:512], ALU.add), ())
        tg2 = K.v(lambda e: e.tensor_tensor(
            B["yr"][:], tmpf[:, 0:128], tmpf[:, 128:256], ALU.add), ())
        return tv6, tg2

    a2 = contextlib.ExitStack()
    SBs, SRs = make_scanbufs(a2, "a")

    s_dsl = K.sbuf("dsl_a", [128, 2, 4, TS], BF16, scope=a2)
    s_wsl = K.sbuf("wsl_a", [128, 2, 4, TS], BF16, scope=a2)
    s_hp4 = K.sbuf("hp_a", [128, 4, S], scope=a2)

    tok_bcS = tok_asm
    tok_hp0 = K.v(lambda e: e.memset(s_hp4[:].rearrange("p a s -> p (a s)"), 0.0), [tok_asm])
    item_hist = []
    rep_hist = []
    tsl_hist = []
    pend = [None]
    nitem = 0

    def flushA(ctx):
        if ctx is None:
            return
        hpw, red = scan_itemB(ctx, ctx["hp_ap"], [tok_hp0])
        last_red[ctx["par"]] = red
        tcb = K.v(lambda e, B_=ctx["B"]: e.tensor_copy(B_["yrb"][:], B_["yr"][:]), [red])
        tyc = K.stg(t_scr_ay[ctx["i"] * 128:(ctx["i"] + 1) * 128,
                             ctx["j"] * TS:(ctx["j"] + 1) * TS], ctx["B"]["yrb"][:], [tcb], sem="dg2")
        item_hist.append(tyc)

    for j in range(NJ):
        jp = j % 2
        rguard = rep_hist[-1] if len(rep_hist) >= 2 else None
        reps = rep_build(SRs[jp], t_bcf[0:16, :], t_bcf[16:32, :],
                         j, [tok_bcS] + ([rguard] if rguard else []))
        for i in range(4):
            par = nitem % 2
            SB = SBs[par]
            bsel = par
            guard = tsl_hist[-2][1] if len(tsl_hist) >= 2 else None
            bufg = tsl_hist[-2][0] if len(tsl_hist) >= 2 else None
            tl1 = K.ld(s_dsl[:, bsel, i, :].unsqueeze(1),
                       t_af[i * 128:(i + 1) * 128, j * TS:(j + 1) * TS].unsqueeze(1),
                       [tok_asm, guard, bufg], sem=f"dy{par}")
            tl2 = K.ld(s_wsl[:, bsel, i, :].unsqueeze(1),
                       t_wf[i * 128:(i + 1) * 128, j * TS:(j + 1) * TS].unsqueeze(1),
                       [tl1], sem=f"dy{par}")
            pv = [tl2] + reps + ([bufg] if bufg else [])
            ctx = scan_itemA(SBs[par], SRs[jp], s_Aa[:, i, :], s_dsl[:, bsel, i, :],
                             s_wsl[:, bsel, i, :], pv, rev=False, par=par)
            ctx.update(i=i, j=j, hp_ap=s_hp4[:, i, :], par=par)
            tsl_hist.append((ctx["ts1"], ctx["tv1"]))
            flushA(pend[0])
            pend[0] = ctx
            nitem += 1
        rep_hist.append(item_hist[-1] if item_hist else None)
    flushA(pend[0])
    pend[0] = None
    prev_item = item_hist[-1]
    snd2 = []
    tok_aydone = drain_dg([prev_item])
    snd2.append(select1(lambda b: (g4(t_cc2_in), b4(b)),
                        t_scr_ay[:, TOK:FULL].rearrange("(g p) c -> p g c", p=128),
                        t_scr_ay[:, 0:TOK].rearrange("(g p) c -> p g c", p=128), [tok_aydone]))
    tok_ag2 = K.cc("AllGather", ALU.bypass, [t_cc2_in], [t_cc2_out], snd2)
    a2.close()

    a3 = contextlib.ExitStack()
    s_yfx = [K.sbuf(f"yfx{i}", [128, TOK], BF16, scope=a3) for i in range(NDT)]
    ytoks = []
    t1 = select1(lambda b: (s_yfx[0][:], b[:].rearrange("p (g f) -> p g f", g=4)[:, 0, :]),
                 t_scr_ay[:, 0:TOK].rearrange("(g p) c -> p g c", p=128),
                 t_scr_ay[:, TOK:FULL].rearrange("(g p) c -> p g c", p=128),
                 [tok_ag2, tok_aydone])
    ytoks.append(t1)
    for i in range(1, 4):
        ytoks.append(K.stg(s_yfx[i][:], s_bnc[:].rearrange("p (g f) -> p g f", g=4)[:, i, :], [t1], sem="dg2"))
    t2 = select1(lambda b: (s_yfx[4][:], b[:].rearrange("p (g f) -> p g f", g=4)[:, 0, :]),
                 g4(t_cc2_out[1]), g4(t_cc2_out[0]), ytoks)
    ytoks.append(t2)
    for i in range(1, 4):
        ytoks.append(K.stg(s_yfx[4 + i][:], s_bnc[:].rearrange("p (g f) -> p g f", g=4)[:, i, :], [t2], sem="dg2"))
    tok_attn_a = None
    for i in range(NDT):
        tok_attn_a = K.v(lambda e, i_=i: e.tensor_add(sx[i_][:], sx[i_][:], s_yfx[i_][:]),
                         [ytoks[-1]])
    a3.close()

    def ffn(pre):
        fsc0 = contextlib.ExitStack()
        sMt = K.sbuf(f"Mf{ln_idx[0]}", [128, TOK], BF16, scope=fsc0)
        mspill = []

        def dst_f(i, stg_, brep, tn1):
            t2 = K.v(lambda e: e.tensor_tensor(sMt[:], stg_[:], brep[:], ALU.add), [tn1])
            tk = K.stg(t_scr_M[i * 128:(i + 1) * 128, :], sMt[:], [t2])
            mspill.append(tk)
            return tk

        tok_ln = layernorm_make(pre, dst_f)
        fsc0.close()

        fsc = contextlib.ExitStack()
        nm = ln_idx[0]
        sWe = [K.sbuf(f"We{nm}_{k}", [128, 2048], BF16, scope=fsc) for k in range(NDT)]
        sWc = [K.sbuf(f"Wc{nm}_{q}", [128, 1024], BF16, scope=fsc) for q in range(16)]
        sH = [K.sbuf(f"H{nm}_{q}", [128, 256], BF16, scope=fsc) for q in range(16)]
        sMc = K.sbuf(f"Mc{nm}", [128, NDT, 256], BF16, scope=fsc)
        last_x = tok_ln
        for grp in range(2):
            hbase = grp * 16
            wl = [K.ld(sWe[k][:], t_wpe[k * 128:(k + 1) * 128, hbase * 128:(hbase + 16) * 128],
                       [tok_ln, last_x]) for k in range(NDT)]
            wl += [K.ld(sWc[q][:], t_wpc[(hbase + q) * 128:(hbase + q + 1) * 128, :],
                        [tok_ln, last_x]) for q in range(16)]
            for ch in range(8):
                tmc = K.ld(sMc[:],
                           t_scr_M.rearrange("(a p) t -> p a t", p=128)[:, :, ch * 256:(ch + 1) * 256],
                           [tok_ln, last_x])
                ev_up = {}
                for ht in range(16):
                    par = ht % 2
                    tmm = None
                    for k in range(NDT):
                        tmm = K.t(lambda e, h_=ht, k_=k, p_=par: e.matmul(
                            ps[p_][:, 0:256], sWe[k_][:, h_ * 128:(h_ + 1) * 128], sMc[:, k_, :],
                            start=(k_ == 0), stop=(k_ == NDT - 1)),
                            wl + [tmc] + ([ev_up.get(ht - 2)] if ev_up.get(ht - 2) else []))
                    ev_up[ht] = K.s(lambda e, h_=ht, p_=par, hb=hbase: e.activation(
                        sH[h_][:], ps[p_][:, 0:256], AF.Gelu,
                        bias=s_bpu[:, hb + h_:hb + h_ + 1]), [tmm])
                ev_dn = {}
                for mt in range(NDT):
                    par = 2 + (mt % 2)
                    tmm = None
                    for q in range(16):
                        tmm = K.t(lambda e, m_=mt, q_=q, p_=par: e.matmul(
                            ps[p_][:, 0:256], sWc[q_][:, m_ * 128:(m_ + 1) * 128], sH[q_][:],
                            start=(q_ == 0), stop=(q_ == 15)),
                            [ev_up[15]] + ([ev_dn.get(mt - 2)] if ev_dn.get(mt - 2) else []))
                    if grp == 0:
                        ev_dn[mt] = K.v(lambda e, m_=mt, p_=par, c_=ch: e.scalar_tensor_tensor(
                            sx[m_][:, c_ * 256:(c_ + 1) * 256], ps[p_][:, 0:256],
                            s_bpc[:, m_:m_ + 1], sx[m_][:, c_ * 256:(c_ + 1) * 256],
                            ALU.add, ALU.add), [tmm, mspill[mt]])
                    else:
                        ev_dn[mt] = K.v(lambda e, m_=mt, p_=par, c_=ch: e.tensor_add(
                            sx[m_][:, c_ * 256:(c_ + 1) * 256],
                            sx[m_][:, c_ * 256:(c_ + 1) * 256], ps[p_][:, 0:256]), [tmm])
                last_x = ev_dn[NDT - 1]
        fsc.close()
        return last_x

    tok_ffn1 = ffn([tok_attn_a])

    b0 = contextlib.ExitStack()
    sMB = [K.sbuf(f"Mb{i}", [128, TOK], BF16, scope=b0) for i in range(NDT)]

    def dst_b(i, stg_, brep, tn1):
        return K.v(lambda e, i_=i: e.tensor_tensor(sMB[i_][:], stg_[:], brep[:], ALU.add), [tn1])

    tok_ln3 = layernorm_make([tok_ffn1], dst_b)

    sWinL = K.sbuf("WinL", [128, 2, NDT, 128], scope=b0)
    sWinB = K.sbuf("WinB", [128, 2, NDT, 128], BF16, scope=b0)
    sOutT = K.sbuf("OutT_b", [128, 2, TOK], BF16, scope=b0)
    ip_hist = []
    tcast_prev = None
    last_evs = None
    for tix in range(32):
        par = tix % 2
        guard = ip_hist[-2] if len(ip_hist) >= 2 else None
        tkl = K.ld(sWinL[:, par, :, :],
                   d_WinT.rearrange("(a p) c -> p a c", p=128)[:, :, tix * 128:(tix + 1) * 128],
                   [tok_ln3, guard], sem=f"dy{par}")
        tcast = K.s(lambda e, p_=par: e.copy(sWinB[:, p_, :, :], sWinL[:, p_, :, :]),
                    [tkl] + ([tcast_prev] if tcast_prev else []))
        tcast_prev = tcast
        evs = []
        for ch in range(4):
            pp = 4 + (ch % 2)
            tmm = None
            for k in range(NDT):
                tmm = K.t(lambda e, p_=par, k_=k, c_=ch, q_=pp: e.matmul(
                    ps[q_][:], sWinB[:, p_, k_, :], sMB[k_][:, c_ * 512:(c_ + 1) * 512],
                    start=(k_ == 0), stop=(k_ == NDT - 1)),
                    [tcast] + ([evs[-2]] if len(evs) >= 2 else ([last_evs] if last_evs else [])))
            if tix < 16:
                evs.append(K.s(lambda e, p_=par, c_=ch, q_=pp: e.copy(
                    sOutT[:, p_, c_ * 512:(c_ + 1) * 512], ps[q_][:]), [tmm]))
            else:
                evs.append(K.s(lambda e, p_=par, c_=ch, q_=pp: e.activation(
                    sOutT[:, p_, c_ * 512:(c_ + 1) * 512], ps[q_][:], AF.Silu), [tmm]))
        if tix < 8:
            dst = t_scr_bxin[tix * 128:(tix + 1) * 128, :]
        elif tix < 16:
            dst = t_ccb1_in[(tix - 8) * 128:(tix - 7) * 128, :]
        elif tix < 24:
            dst = t_scr_bz[(tix - 16) * 128:(tix - 15) * 128, :]
        else:
            dst = t_ccb1_in[1024 + (tix - 24) * 128:1024 + (tix - 23) * 128, :]
        ip_hist.append(K.stg(dst, sOutT[:, par, :], [evs[-1]]))
        last_evs = evs[-1]
    tok_agb1 = K.cc("AllGather", ALU.bypass, [t_ccb1_in], [t_ccb1_out], [ip_hist[-1], ip_hist[-2]])
    b0.close()

    b1 = contextlib.ExitStack()
    s_xinF = [K.sbuf(f"xinF{i}", [128, 6 + FULL], BF16, scope=b1) for i in range(NDT)]
    s_cacc = K.sbuf("cacc_b", [128, FULL], scope=b1)
    s_pev1 = K.sbuf("pev1_b", [64, 512], scope=b1)
    s_pev2 = K.sbuf("pev2_b", [16, 512], scope=b1)
    s_pev3 = K.sbuf("pev3_b", [16, 512], scope=b1)
    s_xct = K.sbuf("xct_b", [128, FULL], BF16, scope=b1)

    asmb = [K.v(lambda e, i_=i: e.memset(s_xinF[i_][:, 0:3], 0.0), [tok_agb1]) for i in range(NDT)]
    asmb += [K.v(lambda e, i_=i: e.memset(s_xinF[i_][:, 3 + FULL:6 + FULL], 0.0), [tok_agb1]) for i in range(NDT)]
    g4b = lambda t, g0: t[g0 * 512:(g0 + 1) * 512, :].rearrange("(g p) c -> p g c", p=128)
    for g0 in range(2):
        stg_ = stage_sel(g4b(t_scr_bxin, g0), g4b(t_ccb1_out[0, 0:1024, :], g0),
                         [tok_agb1, asmb[-1]])
        tX = sel_read(False, stg_)
        for ii in range(4):
            i = g0 * 4 + ii
            asmb.append(K.stg(s_xinF[i][:, 3:3 + TOK],
                              s_bnc[:].rearrange("p (g f) -> p g f", g=4)[:, ii, :], [tX], sem="dg2"))
        stg_ = stage_sel(g4b(t_ccb1_out[1, 0:1024, :], g0), g4b(t_scr_bxin, g0), asmb[-4:])
        tY = sel_read(False, stg_)
        for ii in range(4):
            i = g0 * 4 + ii
            asmb.append(K.stg(s_xinF[i][:, 3 + TOK:3 + FULL],
                              s_bnc[:].rearrange("p (g f) -> p g f", g=4)[:, ii, :], [tY], sem="dg2"))
        stg_ = stage_sel(g4b(t_scr_bz, g0), g4b(t_ccb1_out[0, 1024:2048, :], g0), asmb[-4:])
        tX = sel_read(False, stg_)
        asmb.append(K.stg(g4b(t_sz[:, 0:TOK], g0), s_bnc[:].rearrange("p (g f) -> p g f", g=4), [tX], sem="dg2"))
        stg_ = stage_sel(g4b(t_ccb1_out[1, 1024:2048, :], g0), g4b(t_scr_bz, g0), [asmb[-1]])
        tY = sel_read(False, stg_)
        asmb.append(K.stg(g4b(t_sz[:, TOK:FULL], g0), s_bnc[:].rearrange("p (g f) -> p g f", g=4), [tY], sem="dg2"))
    tok_asmb = drain_dg([asmb[-1]])

    proj_ev = {}
    xc_done = {}
    xcst = {}
    for dj, d in enumerate("fb"):
        pe = []
        for i in range(NDT):
            offs = [k for k in range(4)] if d == "f" else [6 - k for k in range(4)]
            tv = K.v(lambda e, i_=i, o_=offs[0], d_=d: e.tensor_scalar(
                s_cacc[:], s_xinF[i_][:, o_:o_ + FULL], s_cw[d_][:, i_, 0:1], None, ALU.mult),
                [tok_asmb] + ([xc_done.get((dj, i - 1))] if xc_done.get((dj, i - 1)) else
                              ([xc_done.get((0, NDT - 1))] if dj == 1 and i == 0 else [])))
            for k in range(1, 4):
                tv = K.v(lambda e, i_=i, o_=offs[k], k_=k, d_=d: e.scalar_tensor_tensor(
                    s_cacc[:], s_xinF[i_][:, o_:o_ + FULL], s_cw[d_][:, i_, k_:k_ + 1],
                    s_cacc[:], ALU.mult, ALU.add), [tv])
            tsil = K.s(lambda e, i_=i, d_=d: e.activation(
                s_xct[:], s_cacc[:], AF.Silu, bias=s_cb[d_][:, i_:i_ + 1]), [tv])
            xcst[(dj, i)] = K.stg(t_xc[d][i * 128:(i + 1) * 128, :], s_xct[:], [tsil])
            mm = []
            for ch in range(8):
                mm.append(K.t(lambda e, i_=i, c_=ch, d_=d: e.matmul(
                    ps[c_][:], s_xpT[d_][:, i_, :], s_xct[:, c_ * 512:(c_ + 1) * 512],
                    start=(i_ == 0), stop=(i_ == NDT - 1)),
                    [tsil] + ([proj_ev.get(dj - 1)] if i == 0 and proj_ev.get(dj - 1) else [])))
            xc_done[(dj, i)] = mm[-1]
            pe = mm
        t1_ = None
        for ch in range(8):
            cl = slice(ch * 512, (ch + 1) * 512)
            tc = K.s(lambda e, p_=ch: e.copy(s_pev1[:], ps[p_][0:64, :]),
                     [pe[ch], xc_done[(dj, NDT - 1)]] + ([t1_] if t1_ else []))
            tc = K.s(lambda e, p_=ch: e.copy(s_pev2[:], ps[p_][64:80, :]), [tc])
            tc = K.s(lambda e, p_=ch: e.copy(s_pev3[:], ps[p_][96:112, :]), [tc])
            t1_ = K.stg(t_ccb2_in[dj * 96:dj * 96 + 64, cl], s_pev1[:], [tc])
            t1_ = K.stg(t_ccb2_in[dj * 96 + 64:dj * 96 + 80, cl], s_pev2[:], [t1_])
            t1_ = K.stg(t_ccb2_in[dj * 96 + 80:dj * 96 + 96, cl], s_pev3[:], [t1_])
        proj_ev[dj] = t1_
    tok_arb2 = K.cc("AllReduce", ALU.add, [t_ccb2_in], [t_ccb2_out], [proj_ev[0], proj_ev[1]])
    b1.close()

    b2 = contextlib.ExitStack()
    SCs, SRb = make_scanbufs(b2, "b")
    s_dlrs = K.sbuf("dlrs_b", [64, 2, 512], BF16, scope=b2)

    s_dsl2 = K.sbuf("dsl_b", [128, 2, TS], BF16, scope=b2)
    s_xcsl = K.sbuf("xcsl_b", [128, 2, TS], BF16, scope=b2)
    s_yfsl = K.sbuf("yfsl_b", [128, 2, TS], BF16, scope=b2)
    s_szsl = K.sbuf("szsl_b", [128, 2, TS], BF16, scope=b2)
    s_yout = K.sbuf("yout_b", [128, TS], BF16, scope=b2)
    s_hpB = K.sbuf("hpB", [128, NDT, S], scope=b2)
    s_dstgB = s_bnc[:].bitcast(F32)[:, 1024:1536]

    b2p = contextlib.ExitStack()
    s_pl1 = s_bnc2[:].bitcast(F32)[0:64, 0:1024]
    s_pl1b = s_bnc2[0:64, 2048:3072]
    s_pl2b = s_bnc2[0:16, 3072:4096]
    s_pl2 = s_bnc2[:].bitcast(F32)[0:16, 2048:3072]
    s_pl3 = s_bnc2[:].bitcast(F32)[0:16, 3072:4096]
    pcast = []
    for dj, d in enumerate("fb"):
        for hf in range(4):
            cl = slice(hf * 1024, (hf + 1) * 1024)
            tk1 = K.ld(s_pl1, t_ccb2_out[dj * 96:dj * 96 + 64, cl], [tok_arb2] + pcast)
            tk2 = K.ld(s_pl2, t_ccb2_out[dj * 96 + 64:dj * 96 + 80, cl], [tk1])
            tk3 = K.ld(s_pl3, t_ccb2_out[dj * 96 + 80:dj * 96 + 96, cl], [tk2])
            tcp = K.s(lambda e: e.copy(s_pl1b, s_pl1), [tk1])
            pcast.append(K.stg(t_dlr[dj, :, cl], s_pl1b, [tcp]))
            tcp = K.s(lambda e: e.copy(s_pl2b, s_pl2), [tk2])
            pcast.append(K.stg(t_bcb[dj, 0:16, cl], s_pl2b[0:16, :], [tcp]))
            tcp = K.s(lambda e: e.copy(s_pl2b, s_pl3), [tk3, pcast[-1]])
            pcast.append(K.stg(t_bcb[dj, 16:32, cl], s_pl2b[0:16, :], [tcp]))
    tok_pcast = pcast[-1]
    b2p.close()

    tok_ysum_last = None
    for dj, d in enumerate("fb"):
        dt_st = {}
        last = None
        for i in range(NDT):
            for ch in range(8):
                par = ch % 2
                tld = K.ld(s_dlrs[:, ch % 2, :].unsqueeze(1),
                           t_dlr[dj, :, ch * 512:(ch + 1) * 512].unsqueeze(1),
                           [tok_pcast, tok_asmb] + ([last] if last else []), sem=f"dy{ch % 2}")
                tmm = K.t(lambda e, i_=i, p_=par, d_=d, c2=ch % 2: e.matmul(
                    ps[p_][:], s_dtwT[d_][0:64, i_ * 128:(i_ + 1) * 128],
                    s_dlrs[:, c2, :],
                    start=True, stop=True),
                    [tld] + ([last] if last else []) +
                    ([tok_ysum_last] if tok_ysum_last else []))
                t2 = K.s(lambda e, p_=par, i_=i, d_=d: e.activation(
                    s_dstgB, ps[p_][:], AF.Exp, bias=s_dtb[d_][:, i_:i_ + 1]), [tmm])
                last = K.s(lambda e: e.activation(s_bnc[:, 4096:4608], s_dstgB, AF.Ln, bias=1.0), [t2])
                last = K.stg(t_dt[i * 128:(i + 1) * 128, ch * 512:(ch + 1) * 512], s_bnc[:, 4096:4608], [last])
            dt_st[i] = last
        tok_dtb = drain_dg([dt_st[NDT - 1]])
        rev = (d == "b")
        tok_hp0b = K.v(lambda e: e.memset(s_hpB[:].rearrange("p a s -> p (a s)"), 0.0),
                       [dt_st[NDT - 1]] + ([tok_ysum_last] if tok_ysum_last else []))
        jorder = list(range(NJ)) if not rev else list(range(NJ - 1, -1, -1))
        prev_item = None
        hist = []
        rep_hist = []
        tslh = []
        pend = [None]
        nitem = 0

        def flushB(ctx):
            nonlocal prev_item
            if ctx is None:
                return
            hpw, red = scan_itemB(ctx, ctx["hp_ap"], [tok_hp0b])
            last_red[ctx["par"]] = red
            i_, j_, bsel_, SC_ = ctx["i"], ctx["j"], ctx["bsel"], ctx["B"]
            if d == "f":
                tyf = K.v(lambda e, i2=i_, b2=bsel_, B2=SC_: e.scalar_tensor_tensor(
                    s_yout[:], s_xcsl[:, b2, :], s_Db["f"][:, i2:i2 + 1], B2["yr"][:],
                    ALU.mult, ALU.add), [red] + ([prev_item] if prev_item else []))
                prev_item = K.stg(t_yf[i_ * 128:(i_ + 1) * 128, j_ * TS:(j_ + 1) * TS],
                                  s_yout[:], [tyf], sem="dg2")
            else:
                t1 = K.v(lambda e, b2=bsel_, B2=SC_: e.tensor_tensor(
                    s_bnc[:].bitcast(F32)[:, 1536:1536 + TS], s_yfsl[:, b2, :], B2["yr"][:, ::-1], ALU.add),
                    [red, ctx["tl2"], ctx["tsz"]] + ([prev_item] if prev_item else []))
                t2 = K.v(lambda e, i2=i_, b2=bsel_: e.scalar_tensor_tensor(
                    s_bnc[:].bitcast(F32)[:, 1536:1536 + TS], s_xcsl[:, b2, ::-1], s_Db["b"][:, i2:i2 + 1], s_bnc[:].bitcast(F32)[:, 1536:1536 + TS],
                    ALU.mult, ALU.add), [t1])
                t3 = K.v(lambda e, b2=bsel_: e.tensor_tensor(
                    s_yout[:], s_bnc[:].bitcast(F32)[:, 1536:1536 + TS], s_szsl[:, b2, :], ALU.mult), [t2])
                prev_item = K.stg(t_scr_bys[i_ * 128:(i_ + 1) * 128, j_ * TS:(j_ + 1) * TS],
                                  s_yout[:], [t3], sem="dg2")
            hist.append(prev_item)

        for j in jorder:
            jp = j % 2
            rguard = rep_hist[-1] if len(rep_hist) >= 2 else None
            reps = rep_build(SRb[jp], t_bcb[dj, 0:16, :], t_bcb[dj, 16:32, :], j,
                             [tok_pcast] + ([rguard] if rguard else []) +
                             ([tok_ysum_last] if tok_ysum_last and not hist else []))
            for i in range(NDT):
                par = nitem % 2
                SC = SCs[par]
                bsel = par
                guard = tslh[-2][1] if len(tslh) >= 2 else None
                bufg = tslh[-2][0] if len(tslh) >= 2 else None
                fing = hist[-1] if len(hist) >= 1 else None
                tl0 = K.ld(s_dsl2[:, bsel, :].unsqueeze(1),
                           t_dt[i * 128:(i + 1) * 128, j * TS:(j + 1) * TS].unsqueeze(1),
                           [tok_dtb, guard, fing, bufg], sem=f"dy{par}")
                tl1 = K.ld(s_xcsl[:, bsel, :].unsqueeze(1),
                           t_xc[d][i * 128:(i + 1) * 128, j * TS:(j + 1) * TS].unsqueeze(1),
                           [tok_dtb, guard, fing, bufg], sem=f"dy{par}")
                if rev:
                    d_ap = s_dsl2[:, bsel, ::-1]
                    w_src = s_xcsl[:, bsel, ::-1]
                else:
                    d_ap = s_dsl2[:, bsel, :]
                    w_src = s_xcsl[:, bsel, :]
                pv = [tl0, tl1] + reps + ([bufg] if bufg else [])
                ctx = scan_itemA(SC, SRb[jp], s_Ab[d][:, i, :], d_ap, w_src, pv, rev=rev, par=par)
                ctx.update(i=i, j=j, bsel=bsel, hp_ap=s_hpB[:, i, :], tl2=None, tsz=None, par=par)
                if d == "b":
                    ctx["tl2"] = K.ld(s_yfsl[:, bsel, :].unsqueeze(1),
                                      t_yf[i * 128:(i + 1) * 128, j * TS:(j + 1) * TS].unsqueeze(1),
                                      [yf_done, fing], sem=f"dy{par}")
                    ctx["tsz"] = K.ld(s_szsl[:, bsel, :].unsqueeze(1),
                                      t_sz[i * 128:(i + 1) * 128, j * TS:(j + 1) * TS].unsqueeze(1),
                                      [tok_asmb, fing], sem=f"dy{par}")
                tslh.append((ctx["ts1"], ctx["tv1"]))
                flushB(pend[0])
                pend[0] = ctx
                nitem += 1
            rep_hist.append(hist[-1] if hist else None)
        flushB(pend[0])
        pend[0] = None
        if d == "f":
            cnt2 = K.p.cnt["dg2"]
            K.p.op("gpsimd", None, ((("dg2", cnt2),) + (("dg2", prev_item[1]),)))
            yf_done = K.p.op("gpsimd", lambda e: e.dma_start(t_dbg, s_flags[:]), (), "bar", 16)
        tok_ysum_last = prev_item
    snd3 = []
    cnt2 = K.p.cnt["dg2"]
    K.p.op("gpsimd", None, ((("dg2", cnt2),),) if False else (("dg2", cnt2),))
    tok_bysdone = K.p.op("gpsimd", lambda e: e.dma_start(t_dbg, s_flags[:]), (), "bar", 16)
    for g0 in range(2):
        stg_ = stage_sel(
            t_scr_bys[g0 * 512:(g0 + 1) * 512, TOK:FULL].rearrange("(g p) c -> p g c", p=128),
            t_scr_bys[g0 * 512:(g0 + 1) * 512, 0:TOK].rearrange("(g p) c -> p g c", p=128),
            [tok_bysdone] + snd3)
        tX = sel_read(False, stg_)
        snd3.append(K.stg(t_ccb3_in[g0 * 512:(g0 + 1) * 512, :].rearrange("(g p) c -> p g c", p=128),
                          s_bnc[:].rearrange("p (g f) -> p g f", g=4), [tX]))
    tok_agb3 = K.cc("AllGather", ALU.bypass, [t_ccb3_in], [t_ccb3_out], snd3)
    b2.close()

    b3 = contextlib.ExitStack()
    s_yasm = [K.sbuf(f"yasm{i}", [128, TOK], BF16, scope=b3) for i in range(16)]
    s_Wo = [K.sbuf(f"Wo{k}", [128, 1024], BF16, scope=b3) for k in range(14)]
    s_Wo.append(s_bnc2[:, 0:1024])
    s_Wo.append(s_bnc2[:, 1024:2048])
    s_WoL = s_bnc[:].bitcast(F32)[:, 0:1024]
    ya = []
    for g0 in range(2):
        stg_ = stage_sel(
            t_scr_bys[g0 * 512:(g0 + 1) * 512, 0:TOK].rearrange("(g p) c -> p g c", p=128),
            t_scr_bys[g0 * 512:(g0 + 1) * 512, TOK:FULL].rearrange("(g p) c -> p g c", p=128),
            [tok_agb3, tok_bysdone] + ya)
        tX = sel_read(False, stg_)
        for ii in range(4):
            ya.append(K.stg(s_yasm[g0 * 4 + ii][:],
                            s_bnc[:].rearrange("p (g f) -> p g f", g=4)[:, ii, :], [tX], sem="dg2"))
        stg_ = stage_sel(
            t_ccb3_out[1, g0 * 512:(g0 + 1) * 512, :].rearrange("(g p) c -> p g c", p=128),
            t_ccb3_out[0, g0 * 512:(g0 + 1) * 512, :].rearrange("(g p) c -> p g c", p=128),
            ya[-4:])
        tX = sel_read(False, stg_)
        for ii in range(4):
            ya.append(K.stg(s_yasm[8 + g0 * 4 + ii][:],
                            s_bnc[:].rearrange("p (g f) -> p g f", g=4)[:, ii, :], [tX], sem="dg2"))
    wo_prev = None
    wo_toks = []
    for k in range(16):
        tkl = K.ld(s_WoL, d_WoutT[k * 128:(k + 1) * 128, :],
                   [wo_prev] if wo_prev else [ya[-1]])
        wo_prev = K.s(lambda e, k_=k: e.copy(s_Wo[k_][:], s_WoL), [tkl])
        wo_toks.append(wo_prev)
    tok_attn_b = None
    for ch in range(4):
        mm = {}
        for k in range(16):
            for mt in range(NDT):
                mm[mt] = K.t(lambda e, k_=k, m_=mt, c_=ch: e.matmul(
                    ps[m_][:], s_Wo[k_][:, m_ * 128:(m_ + 1) * 128],
                    s_yasm[k_][:, c_ * 512:(c_ + 1) * 512],
                    start=(k_ == 0), stop=(k_ == 15)),
                    [wo_toks[-1], ya[-1]] + ([tok_attn_b] if tok_attn_b else []))
        for mt in range(NDT):
            tok_attn_b = K.v(lambda e, m_=mt, c_=ch: e.tensor_add(
                sx[m_][:, c_ * 512:(c_ + 1) * 512], sx[m_][:, c_ * 512:(c_ + 1) * 512],
                ps[m_][:]), [mm[mt]])
    b3.close()

    tok_ffn2 = ffn([tok_attn_b])

    out = contextlib.ExitStack()
    s_out = K.sbuf("outstage", [128, NDT, 128], scope=out)
    tok_out = []
    for ch in range(16):
        evs = []
        for i in range(NDT):
            par = i % 2
            tt = K.t(lambda e, i_=i, c_=ch, p_=par: e.transpose(
                ps[p_][0:128, 0:128], sx[i_][:, c_ * 128:(c_ + 1) * 128], s_eye[:]),
                [tok_ffn2, tok_eye] + ([evs[-2]] if len(evs) >= 2 else
                                       ([tok_out[-1]] if tok_out else [])))
            evs.append(K.s(lambda e, i_=i, p_=par: e.copy(s_out[:, i_, :], ps[p_][0:128, 0:128]), [tt]))
        tok_out.append(K.stg(o_x[ch * 128:(ch + 1) * 128, :],
                             s_out[:].rearrange("p a c -> p (a c)"), [evs[-1]]))
    out.close()
    tfin = K.stg(t_dbg, s_flags[:], [tok_out[-1]])
    K.p.ops["gpsimd"].append((None, ((("dg", tfin[1]),)), None, 1))

    blk = K.st.enter_context(K.nc.Block())

    @blk.sync
    def _(eng):
        K.p.emit("sync", eng)

    @blk.vector
    def _(eng):
        K.p.emit("vector", eng)

    @blk.scalar
    def _(eng):
        K.p.emit("scalar", eng)

    @blk.tensor
    def _(eng):
        K.p.emit("tensor", eng)

    @blk.gpsimd
    def _(eng):
        K.p.emit("gpsimd", eng)

    K.st.close()
    return K.nc



def _perm_d(a, axis, hh):
    if hh == 0:
        return a
    n = a.shape[axis]
    idx = np.concatenate([np.arange(n // 2, n), np.arange(0, n // 2)])
    return np.take(a, idx, axis=axis)


def make_in_maps(I):
    f32 = lambda a: np.ascontiguousarray(np.asarray(a), dtype=np.float32)
    x = f32(I["L0_attn_a_x"])
    upw = f32(I["L0_ffn_up_weights"])
    dnw = f32(I["L0_ffn_down_weights"])

    bfAu = np.empty((7, 4, 2048), np.float32)
    for s_ in range(7):
        w = upw[s_]
        bfAu[s_] = np.stack([w[:, 0, 0], w[:, 0, 1], w[:, 1, 0], w[:, 1, 1]])
    bfAd = np.empty((7, 4, 2048), np.float32)
    for j, s_ in enumerate(range(6, -1, -1)):
        w = dnw[s_]
        bfAd[j] = np.stack([w[:, 0, 0], w[:, 1, 0], w[:, 0, 1], w[:, 1, 1]])
    bfBu = np.empty((5, 4, 16, 128), np.float32)
    for j, s_ in enumerate(range(7, 12)):
        w = upw[s_].reshape(16, 128, 2, 2)
        bfBu[j] = np.stack([w[:, :, 0, 0], w[:, :, 0, 1], w[:, :, 1, 0], w[:, :, 1, 1]])
    bfBd = np.empty((5, 4, 16, 128), np.float32)
    for j, s_ in enumerate(range(11, 6, -1)):
        w = dnw[s_].reshape(16, 128, 2, 2)
        bfBd[j] = np.stack([w[:, :, 0, 0], w[:, :, 1, 0], w[:, :, 0, 1], w[:, :, 1, 1]])

    Wdbc = f32(I["L0_attn_a_W_deltaBC"])
    Wdt = f32(I["L0_attn_a_W_dt"])
    bdt_a = f32(I["L0_attn_a_b_dt"])
    Alog_a = f32(I["L0_attn_a_A_log"])
    Da = f32(I["L0_attn_a_D_param"])
    Win = f32(I["L0_attn_b_in_proj_w"])
    Wout = f32(I["L0_attn_b_out_proj_w"])
    Wexp = f32(I["L0_ffn_W_exp"])
    Wcon = f32(I["L0_ffn_W_con"])

    in_maps = []
    for c in range(8):
        b, hh = c // 2, c % 2
        m = {}
        m["flags"] = np.array([1 - hh, hh], np.int32)
        hm = np.zeros((128, 2), np.float32)
        hm[:, 0] = 1.0 - hh
        hm[:, 1] = float(hh)
        m["hmask"] = hm
        m["xT"] = np.ascontiguousarray(_perm_d(x[b, hh * TOK:(hh + 1) * TOK, :].T, 0, hh))
        WdbcTp = np.zeros((1024, 128), np.float32)
        wt_ = _perm_d(Wdbc.T, 0, hh)
        WdbcTp[:, 0:64] = wt_[:, 0:64]
        WdbcTp[:, 64:80] = wt_[:, 64:80]
        WdbcTp[:, 96:112] = wt_[:, 80:96]
        m["WdbcT"] = WdbcTp
        m["WdtT"] = np.ascontiguousarray(_perm_d(Wdt.T, 1, hh))
        m["bdt_a"] = np.ascontiguousarray(_perm_d(bdt_a, 0, hh).reshape(8, 128).T)
        half_a = slice(512 * hh, 512 * hh + 512)
        m["Alog_a"] = Alog_a[half_a].copy()
        m["Da"] = np.ascontiguousarray(_perm_d(Da, 0, hh).reshape(8, 128).T)
        di_idx = np.concatenate([np.arange(1024 * hh, 1024 * hh + 1024),
                                 np.arange(1024 * (1 - hh), 1024 * (1 - hh) + 1024)])
        WinT = _perm_d(Win.T, 0, hh)
        xin_cols = WinT[:, 0:2048][:, di_idx]
        z_cols = WinT[:, 2048:4096][:, di_idx]
        m["WinT"] = np.ascontiguousarray(np.concatenate([xin_cols, z_cols], axis=1))
        WoutT = _perm_d(Wout.T, 1, hh)
        m["WoutT"] = np.ascontiguousarray(WoutT[di_idx, :])
        half_b = slice(1024 * hh, 1024 * hh + 1024)
        for d, tag in (("fwd", "f"), ("bwd", "b")):
            m[f"cw{tag}"] = f32(I[f"L0_attn_b_{d}_conv_w"])[half_b, 0, :].copy()
            m[f"cb{tag}"] = np.ascontiguousarray(f32(I[f"L0_attn_b_{d}_conv_b"])[half_b].reshape(8, 128).T)
            xpt_ = f32(I[f"L0_attn_b_{d}_x_proj_w"]).T[half_b, :]
            xptp = np.zeros((1024, 128), np.float32)
            xptp[:, 0:64] = xpt_[:, 0:64]
            xptp[:, 64:80] = xpt_[:, 64:80]
            xptp[:, 96:112] = xpt_[:, 80:96]
            m[f"xp{tag}T"] = xptp
            m[f"dtw{tag}T"] = np.ascontiguousarray(f32(I[f"L0_attn_b_{d}_dt_w"]).T[:, half_b])
            m[f"dtb{tag}"] = np.ascontiguousarray(f32(I[f"L0_attn_b_{d}_dt_b"])[half_b].reshape(8, 128).T)
            m[f"Alog{tag}"] = f32(I[f"L0_attn_b_{d}_A_log"])[half_b].copy()
            m[f"Dp{tag}"] = np.ascontiguousarray(f32(I[f"L0_attn_b_{d}_D"])[half_b].reshape(8, 128).T)
        m["WexpT"] = np.ascontiguousarray(_perm_d(Wexp.T, 1, hh))
        m["Wcon"] = np.ascontiguousarray(_perm_d(Wcon, 1, hh))
        m["bexp"] = np.ascontiguousarray(f32(I["L0_ffn_b_exp"]).reshape(32, 128).T)
        m["upbias"] = np.ascontiguousarray(f32(I["L0_ffn_up_bias"]).reshape(32, 128).T)
        m["bcon"] = np.ascontiguousarray(_perm_d(f32(I["L0_ffn_b_con"]), 0, hh).reshape(8, 128).T)
        m["downbias"] = np.ascontiguousarray(_perm_d(f32(I["L0_ffn_down_bias"]), 0, hh).reshape(8, 128).T)
        m["eye"] = np.eye(128, dtype=np.float32)
        m["bfAu"] = bfAu
        m["bfAd"] = bfAd
        m["bfBu"] = np.ascontiguousarray(bfBu.transpose(3, 0, 1, 2))
        m["bfBd"] = np.ascontiguousarray(bfBd.transpose(3, 0, 1, 2))
        in_maps.append(m)
    return in_maps


_CACHE = {}


def kernel(**inputs):
    in_maps = make_in_maps(inputs)
    if "nc" not in _CACHE:
        _CACHE["nc"] = build_program()
    nc = _CACHE["nc"]
    res = run_bass_kernel_spmd(nc, in_maps, core_ids=list(range(8)))
    out = np.empty((4, FULL, 1024), np.float32)
    for c in range(8):
        b, hh = c // 2, c % 2
        r = np.asarray(res.results[c]["out_xT"], dtype=np.float32)
        if hh == 1:
            r = np.concatenate([r[:, 512:], r[:, :512]], axis=1)
        out[b, hh * TOK:(hh + 1) * TOK, :] = r
    return out


# version 21 (final)
# speedup vs baseline: 1.1175x; 1.0043x over previous
import contextlib
import numpy as np
import ml_dtypes
import concourse.bass as bass
import concourse.mybir as mybir
from concourse.bass_utils import run_bass_kernel_spmd

F32 = mybir.dt.float32
BF16 = mybir.dt.bfloat16
I32 = mybir.dt.int32
ALU = mybir.AluOpType
AF = mybir.ActivationFunctionType
AX = mybir.AxisListType

TOK = 2048
FULL = 4096
NDT = 8
S = 16
TS = 128
FT = S * TS
NJ = FULL // TS
EPS = 1e-5
GROUPS = [[0, 1], [2, 3], [4, 5], [6, 7]]
ENGS = ("sync", "vector", "scalar", "tensor", "gpsimd")


class Plan:
    def __init__(self):
        self.ops = {e: [] for e in ENGS}
        self.cnt = {n: 0 for n in ("dy", "dy0", "dy1", "dg", "dg2", "sel1", "sel2", "rep", "bar", "v", "s", "t", "g", "c")}
        self.sems = {}

    def op(self, eng, fn, waits=(), inc=None, amt=1):
        tok = None
        if inc is not None:
            self.cnt[inc] += amt
            tok = (inc, self.cnt[inc])
        self.ops[eng].append((fn, tuple(w for w in waits if w is not None), inc, amt))
        return tok

    def emit(self, eng_name, eng):
        last = {}
        for fn, waits, inc, amt in self.ops[eng_name]:
            for sn, vv in waits:
                if last.get(sn, 0) >= vv:
                    continue
                eng.wait_ge(self.sems[sn], vv)
                last[sn] = vv
            if fn is None:
                continue
            ins = fn(eng)
            if inc is not None:
                ins.then_inc(self.sems[inc], amt)


class Kern:
    def __init__(self):
        nc = bass.Bass("TRN2", target_bir_lowering=False, debug=False)
        nc.detect_race_conditions = False
        self.nc = nc
        self.st = contextlib.ExitStack()
        self.p = Plan()
        self.regs = {}

    def din(self, n, sh, dt=F32):
        return self.nc.dram_tensor(n, sh, dt, kind="ExternalInput").ap()

    def dout(self, n, sh, dt=F32):
        return self.nc.dram_tensor(n, sh, dt, kind="ExternalOutput").ap()

    def dtmp(self, n, sh, dt=F32):
        return self.nc.dram_tensor(n, sh, dt, kind="Internal").ap()

    def sbuf(self, n, sh, dt=F32, scope=None):
        return (scope or self.st).enter_context(self.nc.sbuf_tensor(n, sh, dt))

    def psum(self, n, sh, dt=F32):
        return self.st.enter_context(self.nc.psum_tensor(n, sh, dt))

    def semaphore(self, n):
        h = self.st.enter_context(self.nc.semaphore(n))
        self.p.sems[n] = h
        return h

    def ld(self, out, in_, waits=(), sem="dy"):
        return self.p.op("sync", lambda e, o=out, i=in_: e.dma_start(o, i), waits, sem, 16)

    def stg(self, out, in_, waits=(), sem="dg"):
        return self.p.op("gpsimd", lambda e, o=out, i=in_: e.dma_start(o, i), waits, sem, 16)

    def cstg(self, out, in_, flag, waits=()):
        return self.p.op("gpsimd",
                         lambda e, o=out, i=in_, f=flag: e.dma_start(o, i, cond=self.regs[f]),
                         waits, "dg", 16)

    def v(self, fn, waits=()):
        return self.p.op("vector", fn, waits, "v", 1)

    def s(self, fn, waits=()):
        return self.p.op("scalar", fn, waits, "s", 1)

    def t(self, fn, waits=()):
        return self.p.op("tensor", fn, waits, "t", 1)

    def g(self, fn, waits=()):
        return self.p.op("gpsimd", fn, waits, "g", 1)

    def cc(self, kind, op, ins, outs, waits=()):
        return self.p.op(
            "gpsimd",
            lambda e, k=kind, o=op, i=ins, u=outs: e.collective_compute(
                k, o, replica_groups=GROUPS, ins=i, outs=u),
            waits, "c", 1)


def sm(ap, s_=S):
    return ap.rearrange("p (s t) -> p s t", s=s_)


def build_program():
    K = Kern()
    for sn in ("dy", "dy0", "dy1", "dg", "dg2", "sel1", "sel2", "rep", "bar", "v", "s", "t", "g", "c"):
        K.semaphore(sn)

    d_xT = K.din("xT", [1024, TOK])
    d_flags = K.din("flags", [2], I32)
    d_hmask = K.din("hmask", [128, 2])
    d_WdbcT = K.din("WdbcT", [1024, 128])
    d_WdtT = K.din("WdtT", [64, 1024])
    d_bdt_a = K.din("bdt_a", [128, 8])
    d_Alog_a = K.din("Alog_a", [512, S])
    d_Da = K.din("Da", [128, 8])
    d_WinT = K.din("WinT", [1024, 4096])
    d_WoutT = K.din("WoutT", [2048, 1024])
    d_cw = {d: K.din(f"cw{d}", [1024, 4]) for d in "fb"}
    d_cb = {d: K.din(f"cb{d}", [128, 8]) for d in "fb"}
    d_xpT = {d: K.din(f"xp{d}T", [1024, 128]) for d in "fb"}
    d_dtwT = {d: K.din(f"dtw{d}T", [64, 1024]) for d in "fb"}
    d_dtb = {d: K.din(f"dtb{d}", [128, 8]) for d in "fb"}
    d_Alog = {d: K.din(f"Alog{d}", [1024, S]) for d in "fb"}
    d_Dp = {d: K.din(f"Dp{d}", [128, 8]) for d in "fb"}
    d_WexpT = K.din("WexpT", [4096, 1024])
    d_Wcon = K.din("Wcon", [4096, 1024])
    d_bexp = K.din("bexp", [128, 32])
    d_upbias = K.din("upbias", [128, 32])
    d_bcon = K.din("bcon", [128, 8])
    d_downbias = K.din("downbias", [128, 8])
    d_eye = K.din("eye", [128, 128])
    d_bfA = {c: K.din(f"bfA{c}", [7, 4, 2048]) for c in "ud"}
    d_bfB = {c: K.din(f"bfB{c}", [128, 5, 4, 16]) for c in "ud"}
    o_x = K.dout("out_xT", [TOK, 1024])

    t_scr_ad = K.dtmp("scr_ad", [512, TOK], BF16)
    t_scr_aw = K.dtmp("scr_aw", [512, TOK], BF16)
    t_cc1_in = K.dtmp("cc1_in", [1056, TOK], BF16)
    t_cc1_out = K.dtmp("cc1_out", [2, 1056, TOK], BF16)
    t_af = K.dtmp("scr_af", [512, FULL], BF16)
    t_wf = K.dtmp("scr_wf", [512, FULL], BF16)
    t_bcf = K.dtmp("scr_bcf", [32, FULL], BF16)
    t_scr_ay = K.dtmp("scr_ay", [512, FULL], BF16)
    t_cc2_in = K.dtmp("cc2_in", [512, TOK], BF16)
    t_cc2_out = K.dtmp("cc2_out", [2, 512, TOK], BF16)
    t_scr_x = K.dtmp("scr_x", [1024, TOK], F32)
    t_scr_M = K.dtmp("scr_M", [1024, TOK], BF16)
    t_wpe = K.dtmp("scr_wpe", [1024, 4096], BF16)
    t_wpc = K.dtmp("scr_wpc", [4096, 1024], BF16)
    t_scr_bxin = K.dtmp("scr_bxin", [1024, TOK], BF16)
    t_scr_bz = K.dtmp("scr_bz", [1024, TOK], BF16)
    t_dt = K.dtmp("scr_dt", [1024, FULL], BF16)
    t_dlr = K.dtmp("scr_dlr", [2, 64, FULL], BF16)
    t_bcb = K.dtmp("scr_bcb", [2, 32, FULL], BF16)
    t_yf = K.dtmp("scr_yf", [1024, FULL], BF16)
    t_ccb1_in = K.dtmp("ccb1_in", [2048, TOK], BF16)
    t_ccb1_out = K.dtmp("ccb1_out", [2, 2048, TOK], BF16)
    t_sz = K.dtmp("scr_sz", [1024, FULL], BF16)
    t_xc = {d: K.dtmp(f"scr_xc{d}", [1024, FULL], BF16) for d in "fb"}
    t_ccb2_in = K.dtmp("ccb2_in", [192, FULL], F32)
    t_ccb2_out = K.dtmp("ccb2_out", [192, FULL], F32)
    t_scr_bys = K.dtmp("scr_bys", [1024, FULL], BF16)
    t_ccb3_in = K.dtmp("ccb3_in", [1024, TOK], BF16)
    t_ccb3_out = K.dtmp("ccb3_out", [2, 1024, TOK], BF16)

    sx = [K.sbuf(f"x{i}", [128, TOK]) for i in range(NDT)]
    s_ones = K.sbuf("ones", [128, 1])
    s_eye = K.sbuf("eyeS", [128, 128])
    s_eyeb = K.sbuf("eyeSb", [128, 128], BF16)
    s_flags = K.sbuf("flagsS", [1, 2], I32)
    s_WdbcT = K.sbuf("WdbcT_s", [128, NDT, 128], BF16)
    s_WdtT = K.sbuf("WdtT_s", [64, 1024], BF16)
    s_bdt_a = K.sbuf("bdt_a_s", [128, NDT])
    s_Aa = K.sbuf("Aa_s", [128, 4, S])
    s_Da = K.sbuf("Da_s", [128, NDT])
    s_cw = {d: K.sbuf(f"cw{d}_s", [128, NDT, 4]) for d in "fb"}
    s_cb = {d: K.sbuf(f"cb{d}_s", [128, NDT]) for d in "fb"}
    s_xpT = {d: K.sbuf(f"xp{d}T_s", [128, NDT, 128], BF16) for d in "fb"}
    s_dtwT = {d: K.sbuf(f"dtw{d}T_s", [64, 1024], BF16) for d in "fb"}
    s_dtb = {d: K.sbuf(f"dtb{d}_s", [128, NDT]) for d in "fb"}
    s_Ab = {d: K.sbuf(f"Ab{d}_s", [128, NDT, S]) for d in "fb"}
    s_Db = {d: K.sbuf(f"Db{d}_s", [128, NDT]) for d in "fb"}
    s_bpu = K.sbuf("bpu_s", [128, 32])
    s_bpc = K.sbuf("bpc_s", [128, NDT])
    s_bfB = {c: K.sbuf(f"bfB{c}_s", [128, 5, 4, 16]) for c in "ud"}

    ps = [K.psum(f"ps{i}", [128, 512]) for i in range(8)]

    def _selview(base, like_ap):
        sh = list(like_ap.shape)
        v = base
        if len(sh) == 2:
            return v[0:sh[0], 0:sh[1]]
        if len(sh) == 3:
            return v[:].rearrange("p (g f) -> p g f", g=sh[1])[0:sh[0], :, 0:sh[2]]
        raise ValueError(sh)

    def _selfree(ap):
        sh = list(ap.shape)
        n = 1
        for x in sh[1:]:
            n *= x
        return sh[0], n

    def stage_sel(c0_ap, c1_ap, pre):
        rows, free = _selfree(c0_ap)
        t1 = K.ld(_selview(s_bnc, c0_ap), c0_ap, pre, sem="sel1")
        t2 = K.stg(_selview(s_bnc2, c1_ap), c1_ap, pre, sem="sel2")
        return [t1, t2, (rows, free)]

    def sel_read(swap, st_):
        t1, t2, (rows, free) = st_
        a_, b_ = (0, 1) if not swap else (1, 0)
        t3 = K.v(lambda e, r=rows, f=free, a=a_: e.tensor_scalar(
            s_bnc[0:r, 0:f], s_bnc[0:r, 0:f], s_hm[0:r, a:a + 1], None, ALU.mult),
            [t1, t2, tok_regs])
        return K.v(lambda e, r=rows, f=free, b=b_: e.scalar_tensor_tensor(
            s_bnc[0:r, 0:f], s_bnc2[0:r, 0:f], s_hm[0:r, b:b + 1], s_bnc[0:r, 0:f],
            ALU.mult, ALU.add), [t3])

    def select1(dst_fn, c0_ap, c1_ap, pre):
        st_ = stage_sel(c0_ap, c1_ap, pre)
        t4 = sel_read(False, st_)
        o, i = dst_fn(s_bnc)
        return K.stg(o, i, [t4], sem="dg2")

    tok_x = [K.ld(sx[i][:], d_xT.rearrange("(a p) t -> a p t", p=128)[i]) for i in range(NDT)]
    tok_fl = K.ld(s_flags[:], d_flags.unsqueeze(0))
    tok_eye = K.ld(s_eye[:], d_eye)
    tok_eyeb = K.s(lambda e: e.copy(s_eyeb[:], s_eye[:]), [tok_eye])
    tok_ones = K.v(lambda e: e.memset(s_ones[:], 1.0))
    s_eps = K.sbuf("epsS", [1, 1])
    s_bnc = K.sbuf("bounceS", [128, 8192], BF16)
    s_bnc2 = K.sbuf("bounceS2", [128, 8192], BF16)
    s_hm = K.sbuf("hmaskS", [128, 2])
    tok_eps = K.v(lambda e: e.memset(s_eps[:], EPS))

    t_dbg = K.dtmp("t_dbg", [1, 2], I32)
    tok_regs = K.ld(s_hm[:], d_hmask)

    def drain_dg(pre=()):
        cnt = K.p.cnt["dg"]
        K.p.op("gpsimd", None, tuple(pre) + (("dg", cnt),))
        return K.p.op("gpsimd", lambda e: e.dma_start(t_dbg, s_flags[:]), (), "bar", 16)

    boot = contextlib.ExitStack()
    bsems = ("dy0", "dy1", "sel1", "sel2")
    s_bs4 = [K.sbuf(f"bootstage{r}", [128, 1024], scope=boot) for r in range(4)]
    prev_cast4 = [None, None, None, None]
    rot = [0]

    def load_cast(dst_ap, src_ap, rows, cols):
        r = rot[0] % 4
        rot[0] += 1
        b = s_bs4[r]
        tk = K.ld(b[0:rows, 0:cols], src_ap, [prev_cast4[r]], sem=bsems[r])
        prev_cast4[r] = K.s(lambda e, d=dst_ap, b=b, r_=rows, c=cols: e.copy(d, b[0:r_, 0:c]), [tk])
        return prev_cast4[r]

    load_cast(s_WdbcT[:], d_WdbcT.rearrange("(a p) c -> p a c", p=128), 128, NDT * 128)
    load_cast(s_WdtT[:], d_WdtT, 64, 1024)
    for d in "fb":
        load_cast(s_xpT[d][:], d_xpT[d].rearrange("(a p) c -> p a c", p=128), 128, NDT * 128)
        load_cast(s_dtwT[d][:], d_dtwT[d], 64, 1024)
    def load_A(dst_ap, src_ap, cols):
        r = rot[0] % 4
        rot[0] += 1
        b = s_bs4[r]
        tk = K.ld(b[0:128, 0:cols], src_ap, [prev_cast4[r]], sem=bsems[r])
        tk = K.s(lambda e, b=b, c=cols: e.activation(
            b[0:128, 512:512 + c], b[0:128, 0:c], AF.Exp), [tk])
        prev_cast4[r] = K.v(lambda e, d=dst_ap, b=b, c=cols: e.tensor_scalar_mul(
            d, b[0:128, 512:512 + c], -1.0), [tk])
        return prev_cast4[r]

    load_A(s_Aa[:].rearrange("p a s -> p (a s)"),
           d_Alog_a.rearrange("(a p) s -> p a s", p=128), 4 * S)
    for d in "fb":
        load_A(s_Ab[d][:].rearrange("p a s -> p (a s)"),
               d_Alog[d].rearrange("(a p) s -> p a s", p=128), NDT * S)

    K.ld(s_bdt_a[:], d_bdt_a)
    K.ld(s_Da[:], d_Da)
    for d in "fb":
        K.ld(s_cw[d][:], d_cw[d].rearrange("(a p) k -> p a k", p=128))
        K.ld(s_cb[d][:], d_cb[d])
        K.ld(s_dtb[d][:], d_dtb[d])
        K.ld(s_Db[d][:], d_Dp[d])
    tok_bc1 = K.ld(s_bpc[:], d_bcon)
    r_db = rot[0] % 4
    rot[0] += 1
    tok_db = K.ld(s_bs4[r_db][0:128, 0:NDT], d_downbias,
                  [tok_bc1, prev_cast4[r_db]], sem=bsems[r_db])
    tok_bpc = K.v(lambda e, b=s_bs4[r_db]: e.tensor_add(s_bpc[:], s_bpc[:], b[0:128, 0:NDT]),
                  [tok_db, tok_bc1])
    prev_cast4[r_db] = tok_bpc
    for c in "ud":
        K.ld(s_bfB[c][:], d_bfB[c])

    boot.close()

    comp = contextlib.ExitStack()
    sZbU = K.sbuf("ZcbU", [128, 4096], BF16, scope=comp)
    sZbD = K.sbuf("ZcbD", [128, 4096], BF16, scope=comp)
    sbvec = K.sbuf("bvecc", [128, 32], scope=comp)
    sbvec2 = K.sbuf("bvecc2", [128, 32], scope=comp)
    zbsc = contextlib.ExitStack()
    sZ = K.sbuf("Zc", [128, 4096], scope=zbsc)
    sZ2 = K.sbuf("Zc2", [128, 4096], scope=zbsc)
    sT1 = K.sbuf("Tc1", [128, 2048], scope=zbsc)
    sWrep = K.sbuf("Wrep", [128, 2, 2048], scope=zbsc)
    sAvec = K.sbuf("Avec", [28, 2048], scope=zbsc)

    def build_Z(chain, pre):
        tka = K.ld(sAvec[:], d_bfA[chain].rearrange("a b c -> (a b) c"), pre)
        tz = K.stg(sZ[:].rearrange("p (q i) -> p q i", q=32),
                   s_eye[:].unsqueeze(1).broadcast_to([128, 32, 128]), pre + [tok_eye], sem="rep")
        cur, nxt = sZ, sZ2
        tok = None
        for st_ in range(7):
            sig = 1 << st_
            nh = 64 // sig
            def pos(buf, two, nh=nh, sig=sig):
                return buf[:].rearrange("p (q h two l) -> p q h two l",
                                        q=32, h=nh, two=2)[:, :, :, two, :]

            def wv(quad, nh=nh, sig=sig):
                return sWrep[:, quad, :].rearrange("p (q h l) -> p q h l", q=32, h=nh)

            c_, n_ = cur, nxt
            trep = None
            for qd in range(2):
                trep = K.stg(
                    sWrep[:, qd, :],
                    sAvec[st_ * 4 + qd:st_ * 4 + qd + 1, :].unsqueeze(1)
                    .broadcast_to([1, 128, 2048]),
                    ([tka, tz] if tok is None else [tok]) if qd == 0 else [trep], sem="rep")
            K.v(lambda e, c=c_, n=n_, pos=pos, wv=wv: e.tensor_tensor(
                pos(n, 0), pos(c, 0), wv(0), ALU.mult),
                ([trep, tz, tka] if tok is None else [trep, tok]))
            K.v(lambda e, c=c_, nh=nh, pos=pos, wv=wv: e.tensor_tensor(
                sT1[:].rearrange("p (q h l) -> p q h l", q=32, h=nh), pos(c, 1), wv(1), ALU.mult))
            tok0 = K.v(lambda e, n=n_, nh=nh, pos=pos: e.tensor_tensor(
                pos(n, 0), pos(n, 0),
                sT1[:].rearrange("p (q h l) -> p q h l", q=32, h=nh), ALU.add))
            trep = None
            for qd in range(2):
                trep = K.stg(
                    sWrep[:, qd, :],
                    sAvec[st_ * 4 + 2 + qd:st_ * 4 + 3 + qd, :].unsqueeze(1)
                    .broadcast_to([1, 128, 2048]),
                    [tok0] if qd == 0 else [trep], sem="rep")
            K.v(lambda e, c=c_, n=n_, pos=pos, wv=wv: e.tensor_tensor(
                pos(n, 1), pos(c, 0), wv(0), ALU.mult), [trep])
            K.v(lambda e, c=c_, nh=nh, pos=pos, wv=wv: e.tensor_tensor(
                sT1[:].rearrange("p (q h l) -> p q h l", q=32, h=nh), pos(c, 1), wv(1), ALU.mult))
            tok = K.v(lambda e, n=n_, nh=nh, pos=pos: e.tensor_tensor(
                pos(n, 1), pos(n, 1),
                sT1[:].rearrange("p (q h l) -> p q h l", q=32, h=nh), ALU.add))
            cur, nxt = nxt, cur
        return cur, tok

    def bstep_pairs(chain, cur, get_a, width, pre):
        tok = None
        for st_ in range(5):
            sigq = 1 << st_
            for hi in range(16 // sigq):
                for lo in range(sigq):
                    pr = hi * sigq + lo
                    qa = hi * 2 * sigq + lo
                    qb = qa + sigq

                    def w(quad, p_=pr, s__=st_, c_=chain):
                        return s_bfB[c_][:, s__, quad, p_:p_ + 1]
                    deps = (pre if tok is None else [tok])
                    K.v(lambda e, a=get_a(cur, qa), w=w: e.tensor_scalar(
                        sTb[0:128, 0:width], a, w(0), None, ALU.mult), deps)
                    t2 = K.v(lambda e, b=get_a(cur, qb), w=w: e.scalar_tensor_tensor(
                        sTb[0:128, 0:width], b, w(1), sTb[0:128, 0:width], ALU.mult, ALU.add))
                    K.v(lambda e, a=get_a(cur, qa), w=w: e.tensor_scalar(
                        sTb2[0:128, 0:width], a, w(2), None, ALU.mult))
                    t4 = K.v(lambda e, b=get_a(cur, qb), w=w: e.scalar_tensor_tensor(
                        sTb2[0:128, 0:width], b, w(3), sTb2[0:128, 0:width], ALU.mult, ALU.add))
                    t5 = K.v(lambda e, a=get_a(cur, qa): e.tensor_copy(a, sTb[0:128, 0:width]), [t2])
                    tok = K.v(lambda e, b=get_a(cur, qb): e.tensor_copy(b, sTb2[0:128, 0:width]), [t4, t5])
        return cur, tok

    Zu, tokZu = build_Z("u", [tok_ones, tok_bpc])
    tokZub = K.v(lambda e, c=Zu: e.tensor_copy(sZbU[:], c[:]), [tokZu])
    tkb = K.ld(sbvec[:], d_bexp)
    bt = None
    for q in range(32):
        tmm = K.t(lambda e, q_=q: e.matmul(
            ps[3][0:128, 0:1], Zu[:, q_ * 128:(q_ + 1) * 128], sbvec[:, q_:q_ + 1],
            start=True, stop=True), [tkb, tokZu] + ([bt] if bt else []))
        bt = K.s(lambda e, q_=q: e.copy(sbvec2[:, q_:q_ + 1], ps[3][0:128, 0:1]), [tmm])
    Zd, tokZd = build_Z("d", [tokZub, bt])
    tokZdb = K.v(lambda e, c=Zd: e.tensor_copy(sZbD[:], c[:]), [tokZd])
    zbsc.close()

    sWq = [K.sbuf(f"Wq{i}", [128, 1024], BF16, scope=comp) for i in range(32)]
    sTb = K.sbuf("Tcb", [128, 1024], BF16, scope=comp)
    sTb2 = K.sbuf("Tcb2", [128, 1024], BF16, scope=comp)
    sWl = K.sbuf("Wload", [128, 1024], scope=comp)
    sWlb = K.sbuf("Wloadb", [128, 1024], BF16, scope=comp)
    sEv = K.sbuf("Evac", [128, 512], BF16, scope=comp)

    tprev = tokZdb
    for q in range(32):
        tkl = K.ld(sWl[:], d_WexpT[q * 128:(q + 1) * 128, :], [tprev, tokZdb])
        tkl = K.s(lambda e: e.copy(sWlb[:], sWl[:]), [tkl])
        evs = []
        for nch in range(2):
            tmm = K.t(lambda e, q_=q, n_=nch: e.matmul(
                ps[n_][:], sZbU[:, q_ * 128:(q_ + 1) * 128],
                sWlb[:, n_ * 512:(n_ + 1) * 512],
                start=True, stop=True), [tkl, tokZub])
            evs.append(K.s(lambda e, q_=q, n_=nch: e.copy(
                sWq[q_][:, n_ * 512:(n_ + 1) * 512], ps[n_][:]), [tmm]))
        tprev = evs[-1]
    Wu, tok_bu = bstep_pairs("u", sWq, lambda L, q: L[q][:], 1024, [tprev])
    tok_wpe = []
    for q in range(32):
        for mt in range(NDT):
            tt = K.t(lambda e, q_=q, m_=mt: e.transpose(
                ps[2][0:128, 0:64].bitcast(BF16), Wu[q_][:, m_ * 128:(m_ + 1) * 128], s_eyeb[:]),
                [tok_bu, tok_eyeb] + ([tok_wpe[-1]] if tok_wpe else []))
            te = K.s(lambda e: e.copy(sEv[0:128, 0:128], ps[2][0:128, 0:64].bitcast(BF16)), [tt])
            tok_wpe.append(K.stg(t_wpe[mt * 128:(mt + 1) * 128, q * 128:(q + 1) * 128],
                                 sEv[0:128, 0:128], [te]))
    bias_buf, btok = bstep_pairs("u", sbvec2, lambda L, q: L[:, q:q + 1], 1, [bt, tok_bu])
    tk_upb = K.ld(sWl[0:128, 0:32], d_upbias)
    tok_bpu = K.v(lambda e, bb=bias_buf: e.tensor_add(s_bpu[:], bb[:], sWl[0:128, 0:32]),
                  [btok, tk_upb])

    tprev = None
    _dpre = [tok_bpu, tok_wpe[-1]]
    for q in range(32):
        tkl = K.ld(sWl[:], d_Wcon[q * 128:(q + 1) * 128, :], ([tprev] if tprev else _dpre))
        tprev = K.s(lambda e, q_=q: e.copy(sWq[q_][:], sWl[:]), [tkl] + ([] if tprev else _dpre))
    Wd1, tok_bd = bstep_pairs("d", sWq, lambda L, q: L[q][:], 1024, [tprev])
    tok_wpc = []
    for q in range(32):
        for nch in range(2):
            tmm = K.t(lambda e, q_=q, n_=nch: e.matmul(
                ps[4][:], sZbD[:, q_ * 128:(q_ + 1) * 128], Wd1[q_][:, n_ * 512:(n_ + 1) * 512],
                start=True, stop=True), [tok_bd, tokZdb] + ([tok_wpc[-1]] if tok_wpc else []))
            te = K.s(lambda e: e.copy(sEv[:], ps[4][:]), [tmm])
            tok_wpc.append(K.stg(t_wpc[q * 128:(q + 1) * 128, nch * 512:(nch + 1) * 512],
                                 sEv[:], [te]))
    comp.close()
    tok_compose = tok_wpc[-1]

    ln_idx = [0]

    def layernorm_make(pre, dst_fn):
        ln_idx[0] += 1
        nm = ln_idx[0]
        sc = contextlib.ExitStack()
        stg_ = K.sbuf(f"lnstage{nm}", [128, TOK], scope=sc)
        arep = K.sbuf(f"lnarep{nm}", [128, TOK], scope=sc)
        brep = K.sbuf(f"lnbrep{nm}", [128, TOK], scope=sc)
        s_stat = K.sbuf(f"lnstat{nm}", [1, 2, TOK], scope=sc)
        ev = []
        for ch in range(4):
            tmm = None
            for i in range(NDT):
                tmm = K.t(lambda e, i_=i, c_=ch: e.matmul(
                    ps[4 + c_][0:1, :], s_ones[:], sx[i_][:, c_ * 512:(c_ + 1) * 512],
                    start=(i_ == 0), stop=(i_ == NDT - 1)), pre + [tok_ones])
            ev.append(K.s(lambda e, c_=ch: e.copy(s_stat[0:1, 0, c_ * 512:(c_ + 1) * 512],
                                                  ps[4 + c_][0:1, :]), [tmm]))
        tsq = None
        last_mm = {}
        for i in range(NDT):
            tsq = K.s(lambda e, i_=i: e.activation(stg_[:], sx[i_][:], AF.Square),
                      pre + ([last_mm.get(3)] if i > 0 else [ev[-1]]))
            for ch in range(4):
                last_mm[ch] = K.t(lambda e, i_=i, c_=ch: e.matmul(
                    ps[c_][0:1, :], s_ones[:], stg_[:, c_ * 512:(c_ + 1) * 512],
                    start=(i_ == 0), stop=(i_ == NDT - 1)), [tsq])
        for ch in range(4):
            ev.append(K.s(lambda e, c_=ch: e.copy(s_stat[0:1, 1, c_ * 512:(c_ + 1) * 512],
                                                  ps[c_][0:1, :]), [last_mm[ch]]))
        t1 = K.v(lambda e: e.tensor_scalar_mul(s_stat[0:1, 0, :], s_stat[0:1, 0, :], 1.0 / 1024), [ev[-1], ev[3]])
        t1 = K.v(lambda e: e.tensor_scalar_mul(s_stat[0:1, 1, :], s_stat[0:1, 1, :], 1.0 / 1024), [t1])
        t1 = K.v(lambda e: e.scalar_tensor_tensor(
            arep[0:1, :], s_stat[0:1, 0, :], -1.0, s_stat[0:1, 0, :], ALU.mult, ALU.mult), [t1])
        t1 = K.v(lambda e: e.tensor_add(s_stat[0:1, 1, :], s_stat[0:1, 1, :], arep[0:1, :]), [t1])
        t1 = K.s(lambda e: e.activation(s_stat[0:1, 1, :], s_stat[0:1, 1, :], AF.Sqrt, bias=s_eps[0:1, 0:1]), [t1, tok_eps])
        t1 = K.v(lambda e: e.reciprocal(s_stat[0:1, 1, :], s_stat[0:1, 1, :]), [t1])
        t1 = K.v(lambda e: e.tensor_tensor(s_stat[0:1, 0, :], s_stat[0:1, 0, :], s_stat[0:1, 1, :], ALU.mult), [t1])
        t1 = K.v(lambda e: e.tensor_scalar_mul(s_stat[0:1, 0, :], s_stat[0:1, 0, :], -1.0), [t1])
        tr1 = K.stg(arep[:], s_stat[0:1, 1, :].unsqueeze(1).broadcast_to([1, 128, TOK]), [t1], sem="rep")
        tr2 = K.stg(brep[:], s_stat[0:1, 0, :].unsqueeze(1).broadcast_to([1, 128, TOK]), [t1], sem="rep")
        out_toks = []
        for i in range(NDT):
            tn1 = K.v(lambda e, i_=i: e.tensor_tensor(stg_[:], sx[i_][:], arep[:], ALU.mult),
                      [tr1, tr2] + ([out_toks[-1]] if out_toks else []))
            out_toks.append(dst_fn(i, stg_, brep, tn1))
        sc.close()
        return out_toks[-1]

    a1 = contextlib.ExitStack()
    sM = [K.sbuf(f"Ma{i}", [128, TOK], BF16, scope=a1) for i in range(NDT)]
    s_dlr = K.sbuf("dlr_a", [64, TOK], BF16, scope=a1)
    s_bB = K.sbuf("bB_a", [16, TOK], BF16, scope=a1)
    s_bC = K.sbuf("bC_a", [16, TOK], BF16, scope=a1)

    def dst_a(i, stg_, brep, tn1):
        t2 = K.v(lambda e, i_=i: e.tensor_tensor(sM[i_][:], stg_[:], brep[:], ALU.add), [tn1])
        return K.v(lambda e, i_=i: e.scalar_tensor_tensor(
            sx[i_][:], sM[i_][:], s_Da[:, i_:i_ + 1], sx[i_][:], ALU.mult, ALU.add), [t2])

    tok_ln1 = layernorm_make(tok_x + [tok_compose] + [t for t in prev_cast4 if t], dst_a)

    dbc_ev = []
    for ch in range(4):
        tmm = None
        for i in range(NDT):
            tmm = K.t(lambda e, i_=i, c_=ch: e.matmul(
                ps[c_][:], s_WdbcT[:, i_, :], sM[i_][:, c_ * 512:(c_ + 1) * 512],
                start=(i_ == 0), stop=(i_ == NDT - 1)), [tok_ln1])
        dbc_ev.append(K.s(lambda e, c_=ch: e.copy(s_dlr[:, c_ * 512:(c_ + 1) * 512], ps[c_][0:64, :]), [tmm]))
        dbc_ev.append(K.s(lambda e, c_=ch: e.copy(s_bB[:, c_ * 512:(c_ + 1) * 512], ps[c_][64:80, :]), [tmm]))
        dbc_ev.append(K.s(lambda e, c_=ch: e.copy(s_bC[:, c_ * 512:(c_ + 1) * 512], ps[c_][96:112, :]), [tmm]))

    a1b = contextlib.ExitStack()
    s_dloc = [K.sbuf(f"dloc{i}", [128, TOK], BF16, scope=a1b) for i in range(2)]
    s_wloc = [K.sbuf(f"wloc{i}", [128, TOK], BF16, scope=a1b) for i in range(2)]
    s_dstg = K.sbuf("dstg_a", [128, 512], scope=a1b)
    dw_last = None
    snd = []
    st_hist = []
    for i in range(NDT):
        par2 = i % 2
        for ch in range(4):
            par = 4 + (ch % 2)
            tmm = K.t(lambda e, i_=i, c_=ch, p_=par: e.matmul(
                ps[p_][:], s_WdtT[0:64, i_ * 128:(i_ + 1) * 128], s_dlr[:, c_ * 512:(c_ + 1) * 512],
                start=True, stop=True), [dbc_ev[-1]] + ([dw_last] if dw_last else []))
            t2 = K.s(lambda e, i_=i, p_=par: e.activation(
                s_dstg[:], ps[p_][:], AF.Exp, bias=s_bdt_a[:, i_:i_ + 1]), [tmm])
            dw_last = K.s(lambda e, i_=i, c_=ch, p2=par2: e.activation(
                s_dloc[p2][:, c_ * 512:(c_ + 1) * 512], s_dstg[:],
                AF.Ln, bias=1.0), [t2] + (st_hist[-2:-1] if len(st_hist) >= 2 else []))
        dw_last = K.v(lambda e, i_=i, p2=par2: e.tensor_tensor(
            s_wloc[p2][:], s_dloc[p2][:], sM[i_][:], ALU.mult), [dw_last])
        if i < 4:
            dd = t_scr_ad[i * 128:(i + 1) * 128, :]
            dw = t_scr_aw[i * 128:(i + 1) * 128, :]
        else:
            dd = t_cc1_in[(i - 4) * 128:(i - 3) * 128, :]
            dw = t_cc1_in[512 + (i - 4) * 128:512 + (i - 3) * 128, :]
        s1 = K.stg(dd, s_dloc[par2][:], [dw_last])
        s2 = K.stg(dw, s_wloc[par2][:], [s1])
        snd.append(s2)
        st_hist.append(s2)
    snd.append(K.stg(t_cc1_in[1024:1040, :], s_bB[:], [dbc_ev[-1]]))
    snd.append(K.stg(t_cc1_in[1040:1056, :], s_bC[:], [dbc_ev[-1]]))
    tok_ag1 = K.cc("AllGather", ALU.bypass, [t_cc1_in], [t_cc1_out], snd + [tok_regs])

    asm = []
    g4 = lambda t: t.rearrange("(g p) c -> p g c", p=128)
    b4 = lambda b: b[:].rearrange("p (g f) -> p g f", g=4)

    def asml(dst, local_ap, slot0_ap, slot1_ap, pre, bview=None, gview=None):
        bv = bview or b4
        gv = gview or (lambda t: t)
        t1 = select1(lambda b: (gv(dst[0]), bv(b)), local_ap, slot0_ap, pre)
        t2 = select1(lambda b: (gv(dst[1]), bv(b)), slot1_ap, local_ap, pre + [t1])
        return t2

    asm.append(asml((g4(t_af[:, 0:TOK]), g4(t_af[:, TOK:FULL])), g4(t_scr_ad),
                    g4(t_cc1_out[0, 0:512, :]), g4(t_cc1_out[1, 0:512, :]), [tok_ag1]))
    asm.append(asml((g4(t_wf[:, 0:TOK]), g4(t_wf[:, TOK:FULL])), g4(t_scr_aw),
                    g4(t_cc1_out[0, 512:1024, :]), g4(t_cc1_out[1, 512:1024, :]),
                    [tok_ag1, asm[-1]]))
    asm.append(asml((t_bcf[:, 0:TOK], t_bcf[:, TOK:FULL]), t_cc1_in[1024:1056, :],
                    t_cc1_out[0, 1024:1056, :], t_cc1_out[1, 1024:1056, :],
                    [tok_ag1, asm[-1]],
                    bview=lambda b: b[0:32, 0:TOK], gview=lambda t: t))
    tok_asm = drain_dg([asm[-1]])
    a1b.close()
    a1.close()

    def make_scanbufs(scope, tag):
        sets = []
        for par in range(2):
            B = {}
            B["dA"] = K.sbuf(f"dA_{tag}{par}", [128, FT], BF16, scope=scope)
            B["BX"] = K.sbuf(f"BX_{tag}{par}", [128, FT], BF16, scope=scope)
            B["h"] = K.sbuf(f"h_{tag}{par}", [128, FT], BF16, scope=scope)
            B["tmp"] = K.sbuf(f"tmp_{tag}{par}", [128, FT], BF16, scope=scope)
            B["inj"] = K.sbuf(f"inj_{tag}{par}", [128, S], scope=scope)
            B["yr"] = K.sbuf(f"yr_{tag}{par}", [128, TS], scope=scope)
            B["yrb"] = K.sbuf(f"yrb_{tag}{par}", [128, TS], BF16, scope=scope)
            sets.append(B)
        R = []
        for par in range(2):
            D = {}
            D["fl1"] = K.sbuf(f"fl1_{tag}{par}", [1, FT], BF16, scope=scope)
            D["fl2"] = K.sbuf(f"fl2_{tag}{par}", [1, FT], BF16, scope=scope)
            D["Brep"] = K.sbuf(f"Brep_{tag}{par}", [128, FT], BF16, scope=scope)
            D["Crep"] = K.sbuf(f"Crep_{tag}{par}", [128, FT], BF16, scope=scope)
            R.append(D)
        return sets, R

    def rep_build(D, bc_B, bc_C, j, pre):
        tf1 = K.stg(D["fl1"][:].rearrange("o (s t) -> o s t", s=S),
                    bc_B[:, j * TS:(j + 1) * TS].unsqueeze(1), pre, sem="rep")
        tf2 = K.stg(D["fl2"][:].rearrange("o (s t) -> o s t", s=S),
                    bc_C[:, j * TS:(j + 1) * TS].unsqueeze(1), pre, sem="rep")
        tr1 = K.stg(D["Brep"][:], D["fl1"][0:1, :].unsqueeze(1).broadcast_to([1, 128, FT]), [tf1], sem="rep")
        tr2 = K.stg(D["Crep"][:], D["fl2"][0:1, :].unsqueeze(1).broadcast_to([1, 128, FT]), [tf2], sem="rep")
        return [tr1, tr2]

    last_red = {0: None, 1: None}

    def scan_itemA(B, D, Arow, d_ap, w_ap, pre_v, rev, par):
        dA3 = sm(B["dA"][:])
        ts1 = None
        for si in range(S):
            w_ = (list(pre_v) + [last_red[par]]) if si == 0 else ()
            ts1 = K.s(lambda e, si=si: e.activation(
                dA3[:, si, :], d_ap, AF.Exp, scale=Arow[:, si:si + 1]), w_)
        brep_ap = sm(D["Brep"][:])[:, :, ::-1] if rev else sm(D["Brep"][:])
        tv1 = K.v(lambda e, ba=brep_ap: e.tensor_tensor(
            sm(B["BX"][:]), w_ap.unsqueeze(1).broadcast_to([128, S, TS]), ba, ALU.mult), pre_v)
        return dict(B=B, D=D, ts1=ts1, tv1=tv1, rev=rev)

    def scan_itemB(ctx, hp_ap, hp_pre):
        B, D, rev = ctx["B"], ctx["D"], ctx["rev"]
        crep_ap = sm(D["Crep"][:])[:, :, ::-1] if rev else sm(D["Crep"][:])
        dA3, BX3 = sm(B["dA"][:]), sm(B["BX"][:])
        tv2 = K.v(lambda e: e.tensor_tensor(B["inj"][:], dA3[:, :, 0], hp_ap, ALU.mult),
                  [ctx["ts1"], ctx["tv1"]] + hp_pre)
        tv3 = K.v(lambda e: e.tensor_tensor(BX3[:, :, 0], BX3[:, :, 0], B["inj"][:], ALU.add), [tv2])
        tv4 = K.v(lambda e: e.memset(dA3[:, :, 0], 0.0), [tv3])
        tv5 = K.v(lambda e: e.tensor_tensor_scan(B["h"][:], B["dA"][:], B["BX"][:], 0.0, ALU.mult, ALU.add), [tv4])
        tv6 = K.v(lambda e: e.tensor_copy(hp_ap, sm(B["h"][:])[:, :, TS - 1]), [tv5])
        tv7 = K.v(lambda e, ca=crep_ap: e.tensor_tensor(sm(B["tmp"][:]), sm(B["h"][:]), ca, ALU.mult), [tv5])

        tmpf = B["tmp"][:]
        K.v(lambda e: e.tensor_tensor(
            tmpf[:, 0:1024], tmpf[:, 0:1024], tmpf[:, 1024:2048], ALU.add), [tv7])
        K.v(lambda e: e.tensor_tensor(
            tmpf[:, 0:512], tmpf[:, 0:512], tmpf[:, 512:1024], ALU.add), ())
        K.v(lambda e: e.tensor_tensor(
            tmpf[:, 0:256], tmpf[:, 0:256], tmpf[:, 256:512], ALU.add), ())
        tg2 = K.v(lambda e: e.tensor_tensor(
            B["yr"][:], tmpf[:, 0:128], tmpf[:, 128:256], ALU.add), ())
        return tv6, tg2

    a2 = contextlib.ExitStack()
    SBs, SRs = make_scanbufs(a2, "a")

    s_dsl = K.sbuf("dsl_a", [128, 2, 4, TS], BF16, scope=a2)
    s_wsl = K.sbuf("wsl_a", [128, 2, 4, TS], BF16, scope=a2)
    s_hp4 = K.sbuf("hp_a", [128, 4, S], scope=a2)

    tok_bcS = tok_asm
    tok_hp0 = K.v(lambda e: e.memset(s_hp4[:].rearrange("p a s -> p (a s)"), 0.0), [tok_asm])
    item_hist = []
    rep_hist = []
    tsl_hist = []
    pend = [None]
    nitem = 0

    def flushA(ctx):
        if ctx is None:
            return
        hpw, red = scan_itemB(ctx, ctx["hp_ap"], [tok_hp0])
        last_red[ctx["par"]] = red
        tcb = K.v(lambda e, B_=ctx["B"]: e.tensor_copy(B_["yrb"][:], B_["yr"][:]), [red])
        tyc = K.stg(t_scr_ay[ctx["i"] * 128:(ctx["i"] + 1) * 128,
                             ctx["j"] * TS:(ctx["j"] + 1) * TS], ctx["B"]["yrb"][:], [tcb], sem="dg2")
        item_hist.append(tyc)

    for j in range(NJ):
        jp = j % 2
        rguard = rep_hist[-1] if len(rep_hist) >= 2 else None
        reps = rep_build(SRs[jp], t_bcf[0:16, :], t_bcf[16:32, :],
                         j, [tok_bcS] + ([rguard] if rguard else []))
        for i in range(4):
            par = nitem % 2
            SB = SBs[par]
            bsel = par
            guard = tsl_hist[-2][1] if len(tsl_hist) >= 2 else None
            bufg = tsl_hist[-2][0] if len(tsl_hist) >= 2 else None
            tl1 = K.ld(s_dsl[:, bsel, i, :].unsqueeze(1),
                       t_af[i * 128:(i + 1) * 128, j * TS:(j + 1) * TS].unsqueeze(1),
                       [tok_asm, guard, bufg], sem=f"dy{par}")
            tl2 = K.ld(s_wsl[:, bsel, i, :].unsqueeze(1),
                       t_wf[i * 128:(i + 1) * 128, j * TS:(j + 1) * TS].unsqueeze(1),
                       [tl1], sem=f"dy{par}")
            pv = [tl2] + reps + ([bufg] if bufg else [])
            ctx = scan_itemA(SBs[par], SRs[jp], s_Aa[:, i, :], s_dsl[:, bsel, i, :],
                             s_wsl[:, bsel, i, :], pv, rev=False, par=par)
            ctx.update(i=i, j=j, hp_ap=s_hp4[:, i, :], par=par)
            tsl_hist.append((ctx["ts1"], ctx["tv1"]))
            flushA(pend[0])
            pend[0] = ctx
            nitem += 1
        rep_hist.append(item_hist[-1] if item_hist else None)
    flushA(pend[0])
    pend[0] = None
    prev_item = item_hist[-1]
    snd2 = []
    tok_aydone = drain_dg([prev_item])
    snd2.append(select1(lambda b: (g4(t_cc2_in), b4(b)),
                        t_scr_ay[:, TOK:FULL].rearrange("(g p) c -> p g c", p=128),
                        t_scr_ay[:, 0:TOK].rearrange("(g p) c -> p g c", p=128), [tok_aydone]))
    tok_ag2 = K.cc("AllGather", ALU.bypass, [t_cc2_in], [t_cc2_out], snd2)
    a2.close()

    a3 = contextlib.ExitStack()
    s_yfx = [K.sbuf(f"yfx{i}", [128, TOK], BF16, scope=a3) for i in range(NDT)]
    ytoks = []
    t1 = select1(lambda b: (s_yfx[0][:], b[:].rearrange("p (g f) -> p g f", g=4)[:, 0, :]),
                 t_scr_ay[:, 0:TOK].rearrange("(g p) c -> p g c", p=128),
                 t_scr_ay[:, TOK:FULL].rearrange("(g p) c -> p g c", p=128),
                 [tok_ag2, tok_aydone])
    ytoks.append(t1)
    for i in range(1, 4):
        ytoks.append(K.stg(s_yfx[i][:], s_bnc[:].rearrange("p (g f) -> p g f", g=4)[:, i, :], [t1], sem="dg2"))
    t2 = select1(lambda b: (s_yfx[4][:], b[:].rearrange("p (g f) -> p g f", g=4)[:, 0, :]),
                 g4(t_cc2_out[1]), g4(t_cc2_out[0]), ytoks)
    ytoks.append(t2)
    for i in range(1, 4):
        ytoks.append(K.stg(s_yfx[4 + i][:], s_bnc[:].rearrange("p (g f) -> p g f", g=4)[:, i, :], [t2], sem="dg2"))
    tok_attn_a = None
    for i in range(NDT):
        tok_attn_a = K.v(lambda e, i_=i: e.tensor_add(sx[i_][:], sx[i_][:], s_yfx[i_][:]),
                         [ytoks[-1]])
    a3.close()

    def ffn(pre):
        fsc0 = contextlib.ExitStack()
        sMt = K.sbuf(f"Mf{ln_idx[0]}", [128, TOK], BF16, scope=fsc0)
        mspill = []

        def dst_f(i, stg_, brep, tn1):
            t2 = K.v(lambda e: e.tensor_tensor(sMt[:], stg_[:], brep[:], ALU.add), [tn1])
            tk = K.stg(t_scr_M[i * 128:(i + 1) * 128, :], sMt[:], [t2])
            mspill.append(tk)
            return tk

        tok_ln = layernorm_make(pre, dst_f)
        fsc0.close()

        fsc = contextlib.ExitStack()
        nm = ln_idx[0]
        sWe = [K.sbuf(f"We{nm}_{k}", [128, 2048], BF16, scope=fsc) for k in range(NDT)]
        sWc = [K.sbuf(f"Wc{nm}_{q}", [128, 1024], BF16, scope=fsc) for q in range(16)]
        sH = [K.sbuf(f"H{nm}_{q}", [128, 256], BF16, scope=fsc) for q in range(16)]
        sMc = K.sbuf(f"Mc{nm}", [128, NDT, 256], BF16, scope=fsc)
        last_x = tok_ln
        for grp in range(2):
            hbase = grp * 16
            wl = [K.ld(sWe[k][:], t_wpe[k * 128:(k + 1) * 128, hbase * 128:(hbase + 16) * 128],
                       [tok_ln, last_x]) for k in range(NDT)]
            wl += [K.ld(sWc[q][:], t_wpc[(hbase + q) * 128:(hbase + q + 1) * 128, :],
                        [tok_ln, last_x]) for q in range(16)]
            for ch in range(8):
                tmc = K.ld(sMc[:],
                           t_scr_M.rearrange("(a p) t -> p a t", p=128)[:, :, ch * 256:(ch + 1) * 256],
                           [tok_ln, last_x])
                ev_up = {}
                for ht in range(16):
                    par = ht % 2
                    tmm = None
                    for k in range(NDT):
                        tmm = K.t(lambda e, h_=ht, k_=k, p_=par: e.matmul(
                            ps[p_][:, 0:256], sWe[k_][:, h_ * 128:(h_ + 1) * 128], sMc[:, k_, :],
                            start=(k_ == 0), stop=(k_ == NDT - 1)),
                            wl + [tmc] + ([ev_up.get(ht - 2)] if ev_up.get(ht - 2) else []))
                    ev_up[ht] = K.s(lambda e, h_=ht, p_=par, hb=hbase: e.activation(
                        sH[h_][:], ps[p_][:, 0:256], AF.Gelu,
                        bias=s_bpu[:, hb + h_:hb + h_ + 1]), [tmm])
                ev_dn = {}
                for mt in range(NDT):
                    par = 2 + (mt % 2)
                    tmm = None
                    for q in range(16):
                        tmm = K.t(lambda e, m_=mt, q_=q, p_=par: e.matmul(
                            ps[p_][:, 0:256], sWc[q_][:, m_ * 128:(m_ + 1) * 128], sH[q_][:],
                            start=(q_ == 0), stop=(q_ == 15)),
                            [ev_up[15]] + ([ev_dn.get(mt - 2)] if ev_dn.get(mt - 2) else []))
                    if grp == 0:
                        ev_dn[mt] = K.v(lambda e, m_=mt, p_=par, c_=ch: e.scalar_tensor_tensor(
                            sx[m_][:, c_ * 256:(c_ + 1) * 256], ps[p_][:, 0:256],
                            s_bpc[:, m_:m_ + 1], sx[m_][:, c_ * 256:(c_ + 1) * 256],
                            ALU.add, ALU.add), [tmm, mspill[mt]])
                    else:
                        ev_dn[mt] = K.v(lambda e, m_=mt, p_=par, c_=ch: e.tensor_add(
                            sx[m_][:, c_ * 256:(c_ + 1) * 256],
                            sx[m_][:, c_ * 256:(c_ + 1) * 256], ps[p_][:, 0:256]), [tmm])
                last_x = ev_dn[NDT - 1]
        fsc.close()
        return last_x

    tok_ffn1 = ffn([tok_attn_a])

    b0 = contextlib.ExitStack()
    sMB = [K.sbuf(f"Mb{i}", [128, TOK], BF16, scope=b0) for i in range(NDT)]

    def dst_b(i, stg_, brep, tn1):
        return K.v(lambda e, i_=i: e.tensor_tensor(sMB[i_][:], stg_[:], brep[:], ALU.add), [tn1])

    tok_ln3 = layernorm_make([tok_ffn1], dst_b)

    sWinL = K.sbuf("WinL", [128, 2, NDT, 128], scope=b0)
    sWinB = K.sbuf("WinB", [128, 2, NDT, 128], BF16, scope=b0)
    sOutT = K.sbuf("OutT_b", [128, 2, TOK], BF16, scope=b0)
    ip_hist = []
    tcast_prev = None
    last_evs = None
    for tix in range(32):
        par = tix % 2
        guard = ip_hist[-2] if len(ip_hist) >= 2 else None
        tkl = K.ld(sWinL[:, par, :, :],
                   d_WinT.rearrange("(a p) c -> p a c", p=128)[:, :, tix * 128:(tix + 1) * 128],
                   [tok_ln3, guard], sem=f"dy{par}")
        tcast = K.s(lambda e, p_=par: e.copy(sWinB[:, p_, :, :], sWinL[:, p_, :, :]),
                    [tkl] + ([tcast_prev] if tcast_prev else []))
        tcast_prev = tcast
        evs = []
        for ch in range(4):
            pp = 4 + (ch % 2)
            tmm = None
            for k in range(NDT):
                tmm = K.t(lambda e, p_=par, k_=k, c_=ch, q_=pp: e.matmul(
                    ps[q_][:], sWinB[:, p_, k_, :], sMB[k_][:, c_ * 512:(c_ + 1) * 512],
                    start=(k_ == 0), stop=(k_ == NDT - 1)),
                    [tcast] + ([evs[-2]] if len(evs) >= 2 else ([last_evs] if last_evs else [])))
            if tix < 16:
                evs.append(K.s(lambda e, p_=par, c_=ch, q_=pp: e.copy(
                    sOutT[:, p_, c_ * 512:(c_ + 1) * 512], ps[q_][:]), [tmm]))
            else:
                evs.append(K.s(lambda e, p_=par, c_=ch, q_=pp: e.activation(
                    sOutT[:, p_, c_ * 512:(c_ + 1) * 512], ps[q_][:], AF.Silu), [tmm]))
        if tix < 8:
            dst = t_scr_bxin[tix * 128:(tix + 1) * 128, :]
        elif tix < 16:
            dst = t_ccb1_in[(tix - 8) * 128:(tix - 7) * 128, :]
        elif tix < 24:
            dst = t_scr_bz[(tix - 16) * 128:(tix - 15) * 128, :]
        else:
            dst = t_ccb1_in[1024 + (tix - 24) * 128:1024 + (tix - 23) * 128, :]
        ip_hist.append(K.stg(dst, sOutT[:, par, :], [evs[-1]]))
        last_evs = evs[-1]
    tok_agb1 = K.cc("AllGather", ALU.bypass, [t_ccb1_in], [t_ccb1_out], [ip_hist[-1], ip_hist[-2]])
    b0.close()

    b1 = contextlib.ExitStack()
    s_xinF = [K.sbuf(f"xinF{i}", [128, 6 + FULL], BF16, scope=b1) for i in range(NDT)]
    s_cacc = K.sbuf("cacc_b", [128, FULL], scope=b1)
    s_pev1 = K.sbuf("pev1_b", [64, 512], scope=b1)
    s_pev2 = K.sbuf("pev2_b", [16, 512], scope=b1)
    s_pev3 = K.sbuf("pev3_b", [16, 512], scope=b1)
    s_xct = K.sbuf("xct_b", [128, FULL], BF16, scope=b1)

    asmb = [K.v(lambda e, i_=i: e.memset(s_xinF[i_][:, 0:3], 0.0), [tok_agb1]) for i in range(NDT)]
    asmb += [K.v(lambda e, i_=i: e.memset(s_xinF[i_][:, 3 + FULL:6 + FULL], 0.0), [tok_agb1]) for i in range(NDT)]
    g4b = lambda t, g0: t[g0 * 512:(g0 + 1) * 512, :].rearrange("(g p) c -> p g c", p=128)
    for g0 in range(2):
        stg_ = stage_sel(g4b(t_scr_bxin, g0), g4b(t_ccb1_out[0, 0:1024, :], g0),
                         [tok_agb1, asmb[-1]])
        tX = sel_read(False, stg_)
        for ii in range(4):
            i = g0 * 4 + ii
            asmb.append(K.stg(s_xinF[i][:, 3:3 + TOK],
                              s_bnc[:].rearrange("p (g f) -> p g f", g=4)[:, ii, :], [tX], sem="dg2"))
        stg_ = stage_sel(g4b(t_ccb1_out[1, 0:1024, :], g0), g4b(t_scr_bxin, g0), asmb[-4:])
        tY = sel_read(False, stg_)
        for ii in range(4):
            i = g0 * 4 + ii
            asmb.append(K.stg(s_xinF[i][:, 3 + TOK:3 + FULL],
                              s_bnc[:].rearrange("p (g f) -> p g f", g=4)[:, ii, :], [tY], sem="dg2"))
        stg_ = stage_sel(g4b(t_scr_bz, g0), g4b(t_ccb1_out[0, 1024:2048, :], g0), asmb[-4:])
        tX = sel_read(False, stg_)
        asmb.append(K.stg(g4b(t_sz[:, 0:TOK], g0), s_bnc[:].rearrange("p (g f) -> p g f", g=4), [tX], sem="dg2"))
        stg_ = stage_sel(g4b(t_ccb1_out[1, 1024:2048, :], g0), g4b(t_scr_bz, g0), [asmb[-1]])
        tY = sel_read(False, stg_)
        asmb.append(K.stg(g4b(t_sz[:, TOK:FULL], g0), s_bnc[:].rearrange("p (g f) -> p g f", g=4), [tY], sem="dg2"))
    tok_asmb = drain_dg([asmb[-1]])

    proj_ev = {}
    xc_done = {}
    xcst = {}
    for dj, d in enumerate("fb"):
        pe = []
        for i in range(NDT):
            offs = [k for k in range(4)] if d == "f" else [6 - k for k in range(4)]
            tv = K.v(lambda e, i_=i, o_=offs[0], d_=d: e.tensor_scalar(
                s_cacc[:], s_xinF[i_][:, o_:o_ + FULL], s_cw[d_][:, i_, 0:1], None, ALU.mult),
                [tok_asmb] + ([xc_done.get((dj, i - 1))] if xc_done.get((dj, i - 1)) else
                              ([xc_done.get((0, NDT - 1))] if dj == 1 and i == 0 else [])))
            for k in range(1, 4):
                tv = K.v(lambda e, i_=i, o_=offs[k], k_=k, d_=d: e.scalar_tensor_tensor(
                    s_cacc[:], s_xinF[i_][:, o_:o_ + FULL], s_cw[d_][:, i_, k_:k_ + 1],
                    s_cacc[:], ALU.mult, ALU.add), [tv])
            tsil = K.s(lambda e, i_=i, d_=d: e.activation(
                s_xct[:], s_cacc[:], AF.Silu, bias=s_cb[d_][:, i_:i_ + 1]), [tv])
            xcst[(dj, i)] = K.stg(t_xc[d][i * 128:(i + 1) * 128, :], s_xct[:], [tsil])
            mm = []
            for ch in range(8):
                mm.append(K.t(lambda e, i_=i, c_=ch, d_=d: e.matmul(
                    ps[c_][:], s_xpT[d_][:, i_, :], s_xct[:, c_ * 512:(c_ + 1) * 512],
                    start=(i_ == 0), stop=(i_ == NDT - 1)),
                    [tsil] + ([proj_ev.get(dj - 1)] if i == 0 and proj_ev.get(dj - 1) else [])))
            xc_done[(dj, i)] = mm[-1]
            pe = mm
        t1_ = None
        for ch in range(8):
            cl = slice(ch * 512, (ch + 1) * 512)
            tc = K.s(lambda e, p_=ch: e.copy(s_pev1[:], ps[p_][0:64, :]),
                     [pe[ch], xc_done[(dj, NDT - 1)]] + ([t1_] if t1_ else []))
            tc = K.s(lambda e, p_=ch: e.copy(s_pev2[:], ps[p_][64:80, :]), [tc])
            tc = K.s(lambda e, p_=ch: e.copy(s_pev3[:], ps[p_][96:112, :]), [tc])
            t1_ = K.stg(t_ccb2_in[dj * 96:dj * 96 + 64, cl], s_pev1[:], [tc])
            t1_ = K.stg(t_ccb2_in[dj * 96 + 64:dj * 96 + 80, cl], s_pev2[:], [t1_])
            t1_ = K.stg(t_ccb2_in[dj * 96 + 80:dj * 96 + 96, cl], s_pev3[:], [t1_])
        proj_ev[dj] = t1_
    tok_arb2 = K.cc("AllReduce", ALU.add, [t_ccb2_in], [t_ccb2_out], [proj_ev[0], proj_ev[1]])
    b1.close()

    b2 = contextlib.ExitStack()
    SCs, SRb = make_scanbufs(b2, "b")
    s_dlrs = K.sbuf("dlrs_b", [64, 2, 512], BF16, scope=b2)

    s_dsl2 = K.sbuf("dsl_b", [128, 2, TS], BF16, scope=b2)
    s_xcsl = K.sbuf("xcsl_b", [128, 2, TS], BF16, scope=b2)
    s_yfsl = K.sbuf("yfsl_b", [128, 2, TS], BF16, scope=b2)
    s_szsl = K.sbuf("szsl_b", [128, 2, TS], BF16, scope=b2)
    s_yout = K.sbuf("yout_b", [128, TS], BF16, scope=b2)
    s_hpB = K.sbuf("hpB", [128, NDT, S], scope=b2)
    s_dstgB = s_bnc[:].bitcast(F32)[:, 1024:1536]

    b2p = contextlib.ExitStack()
    s_pl1 = s_bnc2[:].bitcast(F32)[0:64, 0:1024]
    s_pl1b = s_bnc2[0:64, 2048:3072]
    s_pl2b = s_bnc2[0:16, 3072:4096]
    s_pl2 = s_bnc2[:].bitcast(F32)[0:16, 2048:3072]
    s_pl3 = s_bnc2[:].bitcast(F32)[0:16, 3072:4096]
    pcast = []
    for dj, d in enumerate("fb"):
        for hf in range(4):
            cl = slice(hf * 1024, (hf + 1) * 1024)
            tk1 = K.ld(s_pl1, t_ccb2_out[dj * 96:dj * 96 + 64, cl], [tok_arb2] + pcast)
            tk2 = K.ld(s_pl2, t_ccb2_out[dj * 96 + 64:dj * 96 + 80, cl], [tk1])
            tk3 = K.ld(s_pl3, t_ccb2_out[dj * 96 + 80:dj * 96 + 96, cl], [tk2])
            tcp = K.s(lambda e: e.copy(s_pl1b, s_pl1), [tk1])
            pcast.append(K.stg(t_dlr[dj, :, cl], s_pl1b, [tcp]))
            tcp = K.s(lambda e: e.copy(s_pl2b, s_pl2), [tk2])
            pcast.append(K.stg(t_bcb[dj, 0:16, cl], s_pl2b[0:16, :], [tcp]))
            tcp = K.s(lambda e: e.copy(s_pl2b, s_pl3), [tk3, pcast[-1]])
            pcast.append(K.stg(t_bcb[dj, 16:32, cl], s_pl2b[0:16, :], [tcp]))
    tok_pcast = pcast[-1]
    b2p.close()

    tok_ysum_last = None
    for dj, d in enumerate("fb"):
        dt_st = {}
        last = None
        for i in range(NDT):
            for ch in range(8):
                par = ch % 2
                tld = K.ld(s_dlrs[:, ch % 2, :].unsqueeze(1),
                           t_dlr[dj, :, ch * 512:(ch + 1) * 512].unsqueeze(1),
                           [tok_pcast, tok_asmb] + ([last] if last else []), sem=f"dy{ch % 2}")
                tmm = K.t(lambda e, i_=i, p_=par, d_=d, c2=ch % 2: e.matmul(
                    ps[p_][:], s_dtwT[d_][0:64, i_ * 128:(i_ + 1) * 128],
                    s_dlrs[:, c2, :],
                    start=True, stop=True),
                    [tld] + ([last] if last else []) +
                    ([tok_ysum_last] if tok_ysum_last else []))
                t2 = K.s(lambda e, p_=par, i_=i, d_=d: e.activation(
                    s_dstgB, ps[p_][:], AF.Exp, bias=s_dtb[d_][:, i_:i_ + 1]), [tmm])
                last = K.s(lambda e: e.activation(s_bnc[:, 4096:4608], s_dstgB, AF.Ln, bias=1.0), [t2])
                last = K.stg(t_dt[i * 128:(i + 1) * 128, ch * 512:(ch + 1) * 512], s_bnc[:, 4096:4608], [last])
            dt_st[i] = last
        tok_dtb = drain_dg([dt_st[NDT - 1]])
        rev = (d == "b")
        tok_hp0b = K.v(lambda e: e.memset(s_hpB[:].rearrange("p a s -> p (a s)"), 0.0),
                       [dt_st[NDT - 1]] + ([tok_ysum_last] if tok_ysum_last else []))
        jorder = list(range(NJ)) if not rev else list(range(NJ - 1, -1, -1))
        prev_item = None
        hist = []
        rep_hist = []
        tslh = []
        pend = [None]
        nitem = 0

        def flushB(ctx):
            nonlocal prev_item
            if ctx is None:
                return
            hpw, red = scan_itemB(ctx, ctx["hp_ap"], [tok_hp0b])
            last_red[ctx["par"]] = red
            i_, j_, bsel_, SC_ = ctx["i"], ctx["j"], ctx["bsel"], ctx["B"]
            if d == "f":
                tyf = K.v(lambda e, i2=i_, b2=bsel_, B2=SC_: e.scalar_tensor_tensor(
                    s_yout[:], s_xcsl[:, b2, :], s_Db["f"][:, i2:i2 + 1], B2["yr"][:],
                    ALU.mult, ALU.add), [red] + ([prev_item] if prev_item else []))
                prev_item = K.stg(t_yf[i_ * 128:(i_ + 1) * 128, j_ * TS:(j_ + 1) * TS],
                                  s_yout[:], [tyf], sem="dg2")
            else:
                t1 = K.v(lambda e, b2=bsel_, B2=SC_: e.tensor_tensor(
                    s_bnc[:].bitcast(F32)[:, 1536:1536 + TS], s_yfsl[:, b2, :], B2["yr"][:, ::-1], ALU.add),
                    [red, ctx["tl2"], ctx["tsz"]] + ([prev_item] if prev_item else []))
                t2 = K.v(lambda e, i2=i_, b2=bsel_: e.scalar_tensor_tensor(
                    s_bnc[:].bitcast(F32)[:, 1536:1536 + TS], s_xcsl[:, b2, ::-1], s_Db["b"][:, i2:i2 + 1], s_bnc[:].bitcast(F32)[:, 1536:1536 + TS],
                    ALU.mult, ALU.add), [t1])
                t3 = K.v(lambda e, b2=bsel_: e.tensor_tensor(
                    s_yout[:], s_bnc[:].bitcast(F32)[:, 1536:1536 + TS], s_szsl[:, b2, :], ALU.mult), [t2])
                prev_item = K.stg(t_scr_bys[i_ * 128:(i_ + 1) * 128, j_ * TS:(j_ + 1) * TS],
                                  s_yout[:], [t3], sem="dg2")
            hist.append(prev_item)

        for j in jorder:
            jp = j % 2
            rguard = rep_hist[-1] if len(rep_hist) >= 2 else None
            reps = rep_build(SRb[jp], t_bcb[dj, 0:16, :], t_bcb[dj, 16:32, :], j,
                             [tok_pcast] + ([rguard] if rguard else []) +
                             ([tok_ysum_last] if tok_ysum_last and not hist else []))
            for i in range(NDT):
                par = nitem % 2
                SC = SCs[par]
                bsel = par
                guard = tslh[-2][1] if len(tslh) >= 2 else None
                bufg = tslh[-2][0] if len(tslh) >= 2 else None
                fing = hist[-1] if len(hist) >= 1 else None
                tl0 = K.ld(s_dsl2[:, bsel, :].unsqueeze(1),
                           t_dt[i * 128:(i + 1) * 128, j * TS:(j + 1) * TS].unsqueeze(1),
                           [tok_dtb, guard, fing, bufg], sem=f"dy{par}")
                tl1 = K.ld(s_xcsl[:, bsel, :].unsqueeze(1),
                           t_xc[d][i * 128:(i + 1) * 128, j * TS:(j + 1) * TS].unsqueeze(1),
                           [tok_dtb, guard, fing, bufg], sem=f"dy{par}")
                if rev:
                    d_ap = s_dsl2[:, bsel, ::-1]
                    w_src = s_xcsl[:, bsel, ::-1]
                else:
                    d_ap = s_dsl2[:, bsel, :]
                    w_src = s_xcsl[:, bsel, :]
                pv = [tl0, tl1] + reps + ([bufg] if bufg else [])
                ctx = scan_itemA(SC, SRb[jp], s_Ab[d][:, i, :], d_ap, w_src, pv, rev=rev, par=par)
                ctx.update(i=i, j=j, bsel=bsel, hp_ap=s_hpB[:, i, :], tl2=None, tsz=None, par=par)
                if d == "b":
                    ctx["tl2"] = K.ld(s_yfsl[:, bsel, :].unsqueeze(1),
                                      t_yf[i * 128:(i + 1) * 128, j * TS:(j + 1) * TS].unsqueeze(1),
                                      [yf_done, fing], sem=f"dy{par}")
                    ctx["tsz"] = K.ld(s_szsl[:, bsel, :].unsqueeze(1),
                                      t_sz[i * 128:(i + 1) * 128, j * TS:(j + 1) * TS].unsqueeze(1),
                                      [tok_asmb, fing], sem=f"dy{par}")
                tslh.append((ctx["ts1"], ctx["tv1"]))
                flushB(pend[0])
                pend[0] = ctx
                nitem += 1
            rep_hist.append(hist[-1] if hist else None)
        flushB(pend[0])
        pend[0] = None
        if d == "f":
            cnt2 = K.p.cnt["dg2"]
            K.p.op("gpsimd", None, ((("dg2", cnt2),) + (("dg2", prev_item[1]),)))
            yf_done = K.p.op("gpsimd", lambda e: e.dma_start(t_dbg, s_flags[:]), (), "bar", 16)
        tok_ysum_last = prev_item
    snd3 = []
    cnt2 = K.p.cnt["dg2"]
    K.p.op("gpsimd", None, ((("dg2", cnt2),),) if False else (("dg2", cnt2),))
    tok_bysdone = K.p.op("gpsimd", lambda e: e.dma_start(t_dbg, s_flags[:]), (), "bar", 16)
    for g0 in range(2):
        stg_ = stage_sel(
            t_scr_bys[g0 * 512:(g0 + 1) * 512, TOK:FULL].rearrange("(g p) c -> p g c", p=128),
            t_scr_bys[g0 * 512:(g0 + 1) * 512, 0:TOK].rearrange("(g p) c -> p g c", p=128),
            [tok_bysdone] + snd3)
        tX = sel_read(False, stg_)
        snd3.append(K.stg(t_ccb3_in[g0 * 512:(g0 + 1) * 512, :].rearrange("(g p) c -> p g c", p=128),
                          s_bnc[:].rearrange("p (g f) -> p g f", g=4), [tX]))
    tok_agb3 = K.cc("AllGather", ALU.bypass, [t_ccb3_in], [t_ccb3_out], snd3)
    b2.close()

    b3 = contextlib.ExitStack()
    s_yasm = [K.sbuf(f"yasm{i}", [128, TOK], BF16, scope=b3) for i in range(16)]
    s_Wo = [K.sbuf(f"Wo{k}", [128, 1024], BF16, scope=b3) for k in range(14)]
    s_Wo.append(s_bnc2[:, 0:1024])
    s_Wo.append(s_bnc2[:, 1024:2048])
    s_WoL = s_bnc[:].bitcast(F32)[:, 0:1024]
    ya = []
    for g0 in range(2):
        stg_ = stage_sel(
            t_scr_bys[g0 * 512:(g0 + 1) * 512, 0:TOK].rearrange("(g p) c -> p g c", p=128),
            t_scr_bys[g0 * 512:(g0 + 1) * 512, TOK:FULL].rearrange("(g p) c -> p g c", p=128),
            [tok_agb3, tok_bysdone] + ya)
        tX = sel_read(False, stg_)
        for ii in range(4):
            ya.append(K.stg(s_yasm[g0 * 4 + ii][:],
                            s_bnc[:].rearrange("p (g f) -> p g f", g=4)[:, ii, :], [tX], sem="dg2"))
        stg_ = stage_sel(
            t_ccb3_out[1, g0 * 512:(g0 + 1) * 512, :].rearrange("(g p) c -> p g c", p=128),
            t_ccb3_out[0, g0 * 512:(g0 + 1) * 512, :].rearrange("(g p) c -> p g c", p=128),
            ya[-4:])
        tX = sel_read(False, stg_)
        for ii in range(4):
            ya.append(K.stg(s_yasm[8 + g0 * 4 + ii][:],
                            s_bnc[:].rearrange("p (g f) -> p g f", g=4)[:, ii, :], [tX], sem="dg2"))
    wo_prev = None
    wo_toks = []
    for k in range(16):
        tkl = K.ld(s_WoL, d_WoutT[k * 128:(k + 1) * 128, :],
                   [wo_prev] if wo_prev else [ya[-1]])
        wo_prev = K.s(lambda e, k_=k: e.copy(s_Wo[k_][:], s_WoL), [tkl])
        wo_toks.append(wo_prev)
    tok_attn_b = None
    for ch in range(4):
        mm = {}
        for k in range(16):
            for mt in range(NDT):
                mm[mt] = K.t(lambda e, k_=k, m_=mt, c_=ch: e.matmul(
                    ps[m_][:], s_Wo[k_][:, m_ * 128:(m_ + 1) * 128],
                    s_yasm[k_][:, c_ * 512:(c_ + 1) * 512],
                    start=(k_ == 0), stop=(k_ == 15)),
                    [wo_toks[-1], ya[-1]] + ([tok_attn_b] if tok_attn_b else []))
        for mt in range(NDT):
            tok_attn_b = K.v(lambda e, m_=mt, c_=ch: e.tensor_add(
                sx[m_][:, c_ * 512:(c_ + 1) * 512], sx[m_][:, c_ * 512:(c_ + 1) * 512],
                ps[m_][:]), [mm[mt]])
    b3.close()

    tok_ffn2 = ffn([tok_attn_b])

    out = contextlib.ExitStack()
    s_out = K.sbuf("outstage", [128, NDT, 128], scope=out)
    tok_out = []
    for ch in range(16):
        evs = []
        for i in range(NDT):
            par = i % 2
            tt = K.t(lambda e, i_=i, c_=ch, p_=par: e.transpose(
                ps[p_][0:128, 0:128], sx[i_][:, c_ * 128:(c_ + 1) * 128], s_eye[:]),
                [tok_ffn2, tok_eye] + ([evs[-2]] if len(evs) >= 2 else
                                       ([tok_out[-1]] if tok_out else [])))
            evs.append(K.s(lambda e, i_=i, p_=par: e.copy(s_out[:, i_, :], ps[p_][0:128, 0:128]), [tt]))
        tok_out.append(K.stg(o_x[ch * 128:(ch + 1) * 128, :],
                             s_out[:].rearrange("p a c -> p (a c)"), [evs[-1]]))
    out.close()
    tfin = K.stg(t_dbg, s_flags[:], [tok_out[-1]])
    K.p.ops["gpsimd"].append((None, ((("dg", tfin[1]),)), None, 1))

    blk = K.st.enter_context(K.nc.Block())

    @blk.sync
    def _(eng):
        K.p.emit("sync", eng)

    @blk.vector
    def _(eng):
        K.p.emit("vector", eng)

    @blk.scalar
    def _(eng):
        K.p.emit("scalar", eng)

    @blk.tensor
    def _(eng):
        K.p.emit("tensor", eng)

    @blk.gpsimd
    def _(eng):
        K.p.emit("gpsimd", eng)

    K.st.close()
    return K.nc



def _perm_d(a, axis, hh):
    if hh == 0:
        return a
    n = a.shape[axis]
    idx = np.concatenate([np.arange(n // 2, n), np.arange(0, n // 2)])
    return np.take(a, idx, axis=axis)


def make_in_maps(I):
    f32 = lambda a: np.ascontiguousarray(np.asarray(a), dtype=np.float32)
    x = f32(I["L0_attn_a_x"])
    upw = f32(I["L0_ffn_up_weights"])
    dnw = f32(I["L0_ffn_down_weights"])

    bfAu = np.empty((7, 4, 2048), np.float32)
    for s_ in range(7):
        w = upw[s_]
        bfAu[s_] = np.stack([w[:, 0, 0], w[:, 0, 1], w[:, 1, 0], w[:, 1, 1]])
    bfAd = np.empty((7, 4, 2048), np.float32)
    for j, s_ in enumerate(range(6, -1, -1)):
        w = dnw[s_]
        bfAd[j] = np.stack([w[:, 0, 0], w[:, 1, 0], w[:, 0, 1], w[:, 1, 1]])
    bfBu = np.empty((5, 4, 16, 128), np.float32)
    for j, s_ in enumerate(range(7, 12)):
        w = upw[s_].reshape(16, 128, 2, 2)
        bfBu[j] = np.stack([w[:, :, 0, 0], w[:, :, 0, 1], w[:, :, 1, 0], w[:, :, 1, 1]])
    bfBd = np.empty((5, 4, 16, 128), np.float32)
    for j, s_ in enumerate(range(11, 6, -1)):
        w = dnw[s_].reshape(16, 128, 2, 2)
        bfBd[j] = np.stack([w[:, :, 0, 0], w[:, :, 1, 0], w[:, :, 0, 1], w[:, :, 1, 1]])

    Wdbc = f32(I["L0_attn_a_W_deltaBC"])
    Wdt = f32(I["L0_attn_a_W_dt"])
    bdt_a = f32(I["L0_attn_a_b_dt"])
    Alog_a = f32(I["L0_attn_a_A_log"])
    Da = f32(I["L0_attn_a_D_param"])
    Win = f32(I["L0_attn_b_in_proj_w"])
    Wout = f32(I["L0_attn_b_out_proj_w"])
    Wexp = f32(I["L0_ffn_W_exp"])
    Wcon = f32(I["L0_ffn_W_con"])

    in_maps = []
    for c in range(8):
        b, hh = c // 2, c % 2
        m = {}
        m["flags"] = np.array([1 - hh, hh], np.int32)
        hm = np.zeros((128, 2), np.float32)
        hm[:, 0] = 1.0 - hh
        hm[:, 1] = float(hh)
        m["hmask"] = hm
        m["xT"] = np.ascontiguousarray(_perm_d(x[b, hh * TOK:(hh + 1) * TOK, :].T, 0, hh))
        WdbcTp = np.zeros((1024, 128), np.float32)
        wt_ = _perm_d(Wdbc.T, 0, hh)
        WdbcTp[:, 0:64] = wt_[:, 0:64]
        WdbcTp[:, 64:80] = wt_[:, 64:80]
        WdbcTp[:, 96:112] = wt_[:, 80:96]
        m["WdbcT"] = WdbcTp
        m["WdtT"] = np.ascontiguousarray(_perm_d(Wdt.T, 1, hh))
        m["bdt_a"] = np.ascontiguousarray(_perm_d(bdt_a, 0, hh).reshape(8, 128).T)
        half_a = slice(512 * hh, 512 * hh + 512)
        m["Alog_a"] = Alog_a[half_a].copy()
        m["Da"] = np.ascontiguousarray(_perm_d(Da, 0, hh).reshape(8, 128).T)
        di_idx = np.concatenate([np.arange(1024 * hh, 1024 * hh + 1024),
                                 np.arange(1024 * (1 - hh), 1024 * (1 - hh) + 1024)])
        WinT = _perm_d(Win.T, 0, hh)
        xin_cols = WinT[:, 0:2048][:, di_idx]
        z_cols = WinT[:, 2048:4096][:, di_idx]
        m["WinT"] = np.ascontiguousarray(np.concatenate([xin_cols, z_cols], axis=1))
        WoutT = _perm_d(Wout.T, 1, hh)
        m["WoutT"] = np.ascontiguousarray(WoutT[di_idx, :])
        half_b = slice(1024 * hh, 1024 * hh + 1024)
        for d, tag in (("fwd", "f"), ("bwd", "b")):
            m[f"cw{tag}"] = f32(I[f"L0_attn_b_{d}_conv_w"])[half_b, 0, :].copy()
            m[f"cb{tag}"] = np.ascontiguousarray(f32(I[f"L0_attn_b_{d}_conv_b"])[half_b].reshape(8, 128).T)
            xpt_ = f32(I[f"L0_attn_b_{d}_x_proj_w"]).T[half_b, :]
            xptp = np.zeros((1024, 128), np.float32)
            xptp[:, 0:64] = xpt_[:, 0:64]
            xptp[:, 64:80] = xpt_[:, 64:80]
            xptp[:, 96:112] = xpt_[:, 80:96]
            m[f"xp{tag}T"] = xptp
            m[f"dtw{tag}T"] = np.ascontiguousarray(f32(I[f"L0_attn_b_{d}_dt_w"]).T[:, half_b])
            m[f"dtb{tag}"] = np.ascontiguousarray(f32(I[f"L0_attn_b_{d}_dt_b"])[half_b].reshape(8, 128).T)
            m[f"Alog{tag}"] = f32(I[f"L0_attn_b_{d}_A_log"])[half_b].copy()
            m[f"Dp{tag}"] = np.ascontiguousarray(f32(I[f"L0_attn_b_{d}_D"])[half_b].reshape(8, 128).T)
        m["WexpT"] = np.ascontiguousarray(_perm_d(Wexp.T, 1, hh))
        m["Wcon"] = np.ascontiguousarray(_perm_d(Wcon, 1, hh))
        m["bexp"] = np.ascontiguousarray(f32(I["L0_ffn_b_exp"]).reshape(32, 128).T)
        m["upbias"] = np.ascontiguousarray(f32(I["L0_ffn_up_bias"]).reshape(32, 128).T)
        m["bcon"] = np.ascontiguousarray(_perm_d(f32(I["L0_ffn_b_con"]), 0, hh).reshape(8, 128).T)
        m["downbias"] = np.ascontiguousarray(_perm_d(f32(I["L0_ffn_down_bias"]), 0, hh).reshape(8, 128).T)
        m["eye"] = np.eye(128, dtype=np.float32)
        m["bfAu"] = bfAu
        m["bfAd"] = bfAd
        m["bfBu"] = np.ascontiguousarray(bfBu.transpose(3, 0, 1, 2))
        m["bfBd"] = np.ascontiguousarray(bfBd.transpose(3, 0, 1, 2))
        in_maps.append(m)
    return in_maps


_CACHE = {}


def kernel(**inputs):
    in_maps = make_in_maps(inputs)
    if "nc" not in _CACHE:
        _CACHE["nc"] = build_program()
    nc = _CACHE["nc"]
    res = run_bass_kernel_spmd(nc, in_maps, core_ids=list(range(8)))
    out = np.empty((4, FULL, 1024), np.float32)
    for c in range(8):
        b, hh = c // 2, c % 2
        r = np.asarray(res.results[c]["out_xT"], dtype=np.float32)
        if hh == 1:
            r = np.concatenate([r[:, 512:], r[:, :512]], axis=1)
        out[b, hh * TOK:(hh + 1) * TOK, :] = r
    return out
